# version 01
# baseline (speedup 1.0000x reference)
import contextlib
import numpy as np
import ml_dtypes
import concourse.bass as bass
import concourse.mybir as mybir
from concourse.bass_utils import run_bass_kernel_spmd

F32 = mybir.dt.float32
BF16 = mybir.dt.bfloat16
U8 = mybir.dt.uint8
AF = mybir.ActivationFunctionType
ALU = mybir.AluOpType
AX = mybir.AxisListType

D = 1024
T = 2048
DC = 8
NCORE = 8
BF = ml_dtypes.bfloat16

C_EPS, C_NEGHALF, C_ONE, C_NEGONE, C_EIGHTH, C_1_256, C_NEGBIG = range(7)
CONST_VALS = [1e-5, -0.5, 1.0, -1.0, 0.125, 1.0 / 256.0, -1e30]
NCONST = len(CONST_VALS)


class KB:
    def __init__(self, nc):
        self.nc = nc
        self.ctx = contextlib.ExitStack()
        self.s = {}
        self.cnt = {}
        self.sb = {}
        self.dram = {}
        self.uid = 0
        self._cms = {}

    def sem(self, name):
        if name not in self.s:
            self.s[name] = self.ctx.enter_context(self.nc.semaphore(name))
            self.cnt[name] = 0
        return self.s[name]

    def inc(self, name, n=1):
        self.sem(name)
        self.cnt[name] += n
        return self.cnt[name]

    def val(self, name):
        self.sem(name)
        return self.cnt[name]

    def sbuf(self, name, shape, dt):
        t = self.ctx.enter_context(self.nc.sbuf_tensor(name, shape, dt))
        self.sb[name] = t
        return t

    def init_arena(self, nbytes):
        self.arena = self.sbuf("arena", [128, nbytes], U8)
        self.arena_n = nbytes
        self.holes = [(0, nbytes)]
        self.allocs = {}

    def alloc(self, pfx, shape, dt):
        assert shape[0] <= 128
        esz = {F32: 4, BF16: 2, U8: 1, mybir.dt.int32: 4}[dt]
        inner = 1
        for d in shape[1:]:
            inner *= d
        need = inner * esz
        need_al = (need + 63) & ~63
        for idx, (off, size) in enumerate(self.holes):
            if size >= need_al:
                self.holes[idx] = (off + need_al, size - need_al)
                if self.holes[idx][1] == 0:
                    self.holes.pop(idx)
                ap = self.arena[:, off:off + need]
                if dt != U8:
                    ap = ap.bitcast(dt)
                if len(shape) == 2:
                    ap = ap.rearrange("p (a) -> p a")
                elif len(shape) == 3:
                    ap = ap.rearrange("p (a b) -> p a b", a=shape[1], b=shape[2])
                elif len(shape) == 4:
                    ap = ap.rearrange("p (a b c) -> p a b c", a=shape[1], b=shape[2], c=shape[3])
                if shape[0] < 128:
                    ap = ap[0:shape[0]]
                self.allocs[id(ap)] = (off, need_al)
                return ap
        raise MemoryError(f"arena OOM for {pfx} {shape} ({need_al}B); holes={self.holes}")

    def free(self, *ts):
        for t in ts:
            off, size = self.allocs.pop(id(t))
            self.holes.append((off, size))
        self.holes.sort()
        merged = []
        for off, size in self.holes:
            if merged and merged[-1][0] + merged[-1][1] == off:
                merged[-1] = (merged[-1][0], merged[-1][1] + size)
            else:
                merged.append((off, size))
        self.holes = [tuple(h) for h in merged]

    def din(self, name, shape, dt=F32):
        ap = self.nc.dram_tensor(name, list(shape), dt, kind="ExternalInput").ap()
        self.dram[name] = ap
        return ap

    def dscratch(self, name, shape, dt=F32):
        ap = self.nc.dram_tensor(name, list(shape), dt, kind="Internal").ap()
        self.dram[name] = ap
        return ap

    def uname(self, pfx):
        self.uid += 1
        return f"{pfx}{self.uid}"


def chain(inst, kb, name, n=1):
    inst.then_inc(kb.s[name], n)
    return kb.inc(name, n)


def dchain(inst, kb, name):
    return chain(inst, kb, name, 16)


def emit_ln(kb, mean_b=None, exact_mean=False):
    nc = kb.nc
    x, xn = kb.sb["x_res"], kb.sb["xn"]
    rows = kb.alloc("lnrw", [1, 4 * T], F32)
    cst = kb.sb["consts"]
    mean = rows[0:1, 0:T]
    work = rows[0:1, T:2 * T]
    rstd = rows[0:1, 2 * T:3 * T]
    nmr = rows[0:1, 3 * T:4 * T]
    xr0 = kb.val("xr")
    sc = kb.alloc("lnsc", [128, DC, T], BF16)
    nmb = kb.alloc("lnnm", [128, T], BF16)

    with contextlib.ExitStack() as pc:
        ps = pc.enter_context(nc.psum_tensor(kb.uname("lnps"), [128, 2 * T], F32))

        with nc.Block() as b:
            @b.vector
            def _(v):
                v.wait_ge(kb.s["xr"], xr0)
                for dc in range(DC):
                    i = v.tensor_copy(sc[:, dc, :], x[:, dc, :])
                chain(i, kb, "ve")

            @b.tensor
            def _(t):
                t.wait_ge(kb.s["ve"], kb.val("ve"))
                ob = kb.sb["ones128_bf"]
                for dc in range(DC):
                    for c0 in range(0, T, 512):
                        i = t.matmul(ps[0:1, c0:c0 + 512], ob[:, 0:1],
                                     sc[:, dc, c0:c0 + 512],
                                     start=(dc == 0),
                                     stop=(dc == DC - 1 and not exact_mean))
                chain(i, kb, "pe")

            if exact_mean:
                @b.vector
                def _(v):
                    v.wait_ge(kb.s["pe"], kb.val("pe"))
                    for dc in range(DC):
                        i = v.tensor_tensor(sc[:, dc, :], x[:, dc, :], sc[:, dc, :],
                                            ALU.subtract)
                    chain(i, kb, "ve")

                @b.tensor
                def _(t):
                    t.wait_ge(kb.s["ve"], kb.val("ve"))
                    ob = kb.sb["ones128_bf"]
                    for dc in range(DC):
                        for c0 in range(0, T, 512):
                            i = t.matmul(ps[0:1, c0:c0 + 512], ob[:, 0:1],
                                         sc[:, dc, c0:c0 + 512],
                                         start=False, stop=(dc == DC - 1))
                    chain(i, kb, "pe")

            @b.scalar
            def _(s):
                s.wait_ge(kb.s["pe"], kb.val("pe"))
                for dc in range(DC):
                    i = s.activation(sc[:, dc, :], x[:, dc, :], AF.Square)
                chain(i, kb, "ac")

            @b.tensor
            def _(t):
                t.wait_ge(kb.s["ac"], kb.val("ac"))
                ob = kb.sb["ones128_bf"]
                for dc in range(DC):
                    for c0 in range(0, T, 512):
                        i = t.matmul(ps[0:1, T + c0:T + c0 + 512], ob[:, 0:1],
                                     sc[:, dc, c0:c0 + 512],
                                     start=(dc == 0), stop=(dc == DC - 1))
                chain(i, kb, "pe")

        with nc.Block() as b:
            @b.vector
            def _(v):
                v.wait_ge(kb.s["pe"], kb.val("pe"))
                i = v.tensor_scalar(mean, ps[0:1, 0:T], 1.0 / D, None, ALU.mult)
                chain(i, kb, "ve")

            @b.scalar
            def _(s):
                s.wait_ge(kb.s["ve"], kb.val("ve"))
                i = s.activation(work, mean, AF.Square)
                chain(i, kb, "ac")

            @b.vector
            def _(v):
                v.wait_ge(kb.s["ac"], kb.val("ac"))
                i = v.scalar_tensor_tensor(work, ps[0:1, T:2 * T], 1.0 / D, work,
                                           ALU.mult, ALU.subtract)
                chain(i, kb, "ve")

            @b.scalar
            def _(s):
                s.wait_ge(kb.s["ve"], kb.val("ve"))
                i = s.activation(work, work, AF.Ln, bias=cst[0:1, C_EPS:C_EPS + 1])
                i.then_inc(kb.s["ac"], 1)
                kb.inc("ac")
                s.wait_ge(kb.s["ac"], kb.val("ac"))
                i = s.activation(rstd, work, AF.Exp,
                                 scale=cst[0:1, C_NEGHALF:C_NEGHALF + 1])
                chain(i, kb, "ac")

            @b.vector
            def _(v):
                v.wait_ge(kb.s["ac"], kb.val("ac"))
                i = v.tensor_tensor(nmr, mean, rstd, ALU.mult)
                chain(i, kb, "ve")

            @b.tensor
            def _(t):
                t.wait_ge(kb.s["ve"], kb.val("ve"))
                orow, nrow = kb.sb["onesrow_f32"], kb.sb["negrow_f32"]
                for c0 in range(0, T, 512):
                    t.matmul(ps[:, c0:c0 + 512], orow[0:1, :],
                             rstd[0:1, c0:c0 + 512], start=True, stop=True)
                for c0 in range(0, T, 512):
                    i = t.matmul(ps[:, T + c0:T + c0 + 512], nrow[0:1, :],
                                 nmr[0:1, c0:c0 + 512], start=True, stop=True)
                chain(i, kb, "pe")

        with nc.Block() as b:
            @b.vector
            def _(v):
                v.wait_ge(kb.s["pe"], kb.val("pe"))
                for dc in range(DC):
                    v.tensor_tensor(sc[:, dc, :], x[:, dc, :], ps[:, 0:T], ALU.mult)
                i = v.tensor_copy(nmb[:, :], ps[:, T:2 * T])
                chain(i, kb, "ve")

            @b.gpsimd
            def _(g):
                g.wait_ge(kb.s["ve"], kb.val("ve"))
                for dc in range(DC):
                    i = g.tensor_tensor(xn[:, dc, :], sc[:, dc, :], nmb[:, :], ALU.add)
                chain(i, kb, "ln")

        if mean_b is not None:
            with nc.Block() as b:
                @b.tensor
                def _(t):
                    t.wait_ge(kb.s["ve"], kb.val("ve"))
                    orow = kb.sb["onesrow_f32"]
                    for c0 in range(0, T, 512):
                        i = t.matmul(ps[:, c0:c0 + 512], orow[0:1, :],
                                     mean[0:1, c0:c0 + 512], start=True, stop=True)
                    chain(i, kb, "pe")

                @b.vector
                def _(v):
                    v.wait_ge(kb.s["pe"], kb.val("pe"))
                    i = v.tensor_copy(mean_b[:, :], ps[:, 0:T])
                    chain(i, kb, "ve")
    kb.free(sc, nmb, rows)


def emit_mm(kb, tiles, stat_fn, mov_fn, nk_fn, evac_fn, tag="mm", banks=4,
            tcols=512, waits=(), wait_ln=True, out_rows=128):
    nc = kb.nc
    with contextlib.ExitStack() as pc:
        pss = [pc.enter_context(nc.psum_tensor(kb.uname(tag), [128, tcols], F32))
               for _ in range(banks)]
        pe0 = kb.val("pe")
        ev0 = kb.val("evd")
        with nc.Block() as b:
            @b.tensor
            def _(t):
                if wait_ln:
                    t.wait_ge(kb.s["ln"], kb.val("ln"))
                for wn, wv in waits:
                    t.wait_ge(kb.s[wn], wv)
                for i, key in enumerate(tiles):
                    if i >= banks:
                        t.wait_ge(kb.s["evd"], ev0 + (i - banks) + 1)
                    ps = pss[i % banks]
                    nk = nk_fn(key)
                    for kc in range(nk):
                        ins = t.matmul(ps[0:out_rows, :], stat_fn(key, kc), mov_fn(key, kc),
                                       start=(kc == 0), stop=(kc == nk - 1))
                    ins.then_inc(kb.s["pe"], 1)
                    kb.inc("pe")

            for i, key in enumerate(tiles):
                evac_fn(b, key, pss[i % banks], pe0 + i + 1)
        assert kb.val("evd") == ev0 + len(tiles), "evac must inc evd once per tile"


def emit_l0ffn(kb):
    nc = kb.nc
    x, xn = kb.sb["x_res"], kb.sb["xn"]
    wsg = kb.alloc("l0fw", [128, DC, 1024], BF16)
    sc = kb.alloc("l0fv", [128, 16], F32)
    tbuf = kb.alloc("l0ft", [128, 4, 512], F32)

    with nc.Block() as b:
        @b.gpsimd
        def _(g):
            g.wait_ge(kb.s["xr"], kb.val("xr"))
            i = g.dma_start(out=wsg[:, :, :],
                            in_=kb.dram["l0f_wT"].rearrange("(c p) m -> p c m", p=128))
            dchain(i, kb, "lw")

        @b.sync
        def _(sy):
            i = sy.dma_start(out=sc[:, 0:8], in_=kb.dram["l0f_sc"])
            dchain(i, kb, "lw")
            i = sy.dma_start(out=sc[:, 8:16], in_=kb.dram["l0f_bi"])
            dchain(i, kb, "lw")

        @b.scalar
        def _(s):
            s.wait_ge(kb.s["lw"], kb.val("lw"))
            for dc in range(DC):
                i = s.activation(wsg[:, dc, :], wsg[:, dc, :], AF.Sign)
            chain(i, kb, "ac")

    tiles = [(mc, tc) for mc in range(DC) for tc in range(4)]
    po0 = kb.val("po")

    def evac(b, key, ps, pe_t):
        mc, tc = key
        i_t = tiles.index(key)
        tb = tbuf[:, i_t % 4, :]

        @b.vector
        def _(v):
            v.wait_ge(kb.s["pe"], pe_t)
            if i_t >= 4:
                v.wait_ge(kb.s["po"], po0 + (i_t - 4) + 1)
            i = v.tensor_scalar(tb, ps[:], sc[:, mc:mc + 1], sc[:, 8 + mc:9 + mc],
                                ALU.mult, ALU.add)
            chain(i, kb, "evd")

        @b.gpsimd
        def _(g):
            g.wait_ge(kb.s["evd"], kb.val("evd"))
            i = g.tensor_scalar_min(tb, tb, 100.0)
            chain(i, kb, "po")

        @b.vector
        def _(v):
            v.wait_ge(kb.s["po"], kb.val("po"))
            xs = x[:, mc, tc * 512:(tc + 1) * 512]
            i = v.scalar_tensor_tensor(xs, tb, -100.0, xs, ALU.max, ALU.add)
            chain(i, kb, "xr")

    emit_mm(kb, tiles,
            stat_fn=lambda key, kc: wsg[:, kc, key[0] * 128:(key[0] + 1) * 128],
            mov_fn=lambda key, kc: xn[:, kc, key[1] * 512:(key[1] + 1) * 512],
            nk_fn=lambda key: DC,
            evac_fn=evac, tag="l0f", waits=[("ac", kb.val("ac"))])
    kb.free(wsg, sc, tbuf)



def emit_proj_to_dram(kb, wname, dram_out, scale=None, bias_col=None,
                      n_m=DC, tag="pj", extra_waits=(), src=None, vtm_dram=None,
                      m_off=0):
    nc = kb.nc
    xn = kb.sb["xn"]
    if src is None:
        src = xn
    wsb = kb.alloc(tag + "w", [128, DC, n_m * 128], BF16)
    obuf = kb.alloc(tag + "o", [128, n_m, T], BF16)

    with nc.Block() as b:
        @b.gpsimd
        def _(g):
            g.wait_ge(kb.s["pw"], kb.val("pw"))
            i = g.dma_start(out=wsb[:, :, :],
                            in_=kb.dram[wname].rearrange("(c p) m -> p c m", p=128)[
                                :, :, m_off * 128:(m_off + n_m) * 128])
            dchain(i, kb, "lw")

    def evac(b, key, ps, pe_t):
        mc, tc = key

        @b.vector
        def _(v):
            v.wait_ge(kb.s["pe"], pe_t)
            o = obuf[:, mc, tc * 512:(tc + 1) * 512]
            if bias_col is not None and scale is not None:
                ins = v.tensor_scalar(o, ps[:], bias_col(mc), scale, ALU.add, ALU.mult)
            elif bias_col is not None:
                ins = v.tensor_scalar(o, ps[:], bias_col(mc), None, ALU.add)
            elif scale is not None:
                ins = v.tensor_scalar(o, ps[:], scale, None, ALU.mult)
            else:
                ins = v.tensor_copy(o, ps[:])
            chain(ins, kb, "evd")

    tiles = [(mc, tc) for mc in range(n_m) for tc in range(4)]
    emit_mm(kb, tiles,
            stat_fn=lambda key, kc: wsb[:, kc, key[0] * 128:(key[0] + 1) * 128],
            mov_fn=lambda key, kc: src[:, kc, key[1] * 512:(key[1] + 1) * 512],
            nk_fn=lambda key: DC, evac_fn=evac, tag=tag,
            waits=[("lw", kb.val("lw"))] + list(extra_waits))

    with nc.Block() as b:
        @b.sync
        def _(sy):
            sy.wait_ge(kb.s["evd"], kb.val("evd"))
            i = sy.dma_start(out=dram_out.rearrange("(c p) t -> p c t", p=128)[
                                 :, m_off:m_off + n_m, :],
                             in_=obuf[:, :, :])
            dchain(i, kb, "spl")

        @b.gpsimd
        def _(g):
            g.wait_ge(kb.s["spl"], kb.val("spl"))
            i = g.engine_nop()
            i.then_inc(kb.s["pw"], 1)
            kb.inc("pw")

    if vtm_dram is not None:
        emit_vtm_spill(kb, obuf, vtm_dram, n_m, hs_off=m_off * 2)
    kb.free(wsb, obuf)


def emit_vtm_spill(kb, vbuf, vtm_dram, n_m=DC, hs_off=0):
    nc = kb.nc
    idn = kb.sb["ident_bf"]
    with contextlib.ExitStack() as pc:
        pst = [pc.enter_context(nc.psum_tensor(kb.uname("vt"), [128, 128], BF16))
               for _ in range(4)]
        tb_ = kb.alloc("vtb", [128, 4, 128], BF16)
        pairs = [(dc, tb) for dc in range(n_m) for tb in range(16)]
        pe0 = kb.val("pe")
        ev0 = kb.val("evd")
        sp0 = kb.val("spl")
        with nc.Block() as b:
            @b.tensor
            def _(t):
                t.wait_ge(kb.s["evd"], kb.val("evd"))
                for i, (dc, tb) in enumerate(pairs):
                    if i >= 4:
                        t.wait_ge(kb.s["evd"], ev0 + (i - 4) + 1)
                    ins = t.transpose(pst[i % 4][:], vbuf[:, dc, tb * 128:(tb + 1) * 128], idn[:])
                    ins.then_inc(kb.s["pe"], 1)
                    kb.inc("pe")

            @b.vector
            def _(v):
                for i, (dc, tb) in enumerate(pairs):
                    v.wait_ge(kb.s["pe"], pe0 + i + 1)
                    if i >= 4:
                        v.wait_ge(kb.s["spl"], sp0 + 16 * 2 * (i - 3))
                    ins = v.tensor_copy(tb_[:, i % 4, :], pst[i % 4][:])
                    chain(ins, kb, "evd")

            @b.sync
            def _(sy):
                for i, (dc, tb) in enumerate(pairs):
                    sy.wait_ge(kb.s["evd"], ev0 + i + 1)
                    for half in range(2):
                        ins = sy.dma_start(
                            out=vtm_dram[hs_off + dc * 2 + half, tb * 128:(tb + 1) * 128, :],
                            in_=tb_[:, i % 4, half * 64:(half + 1) * 64])
                        dchain(ins, kb, "spl")

        with nc.Block() as b:
            @b.gpsimd
            def _(g):
                g.wait_ge(kb.s["spl"], kb.val("spl"))
                i = g.engine_nop()
                i.then_inc(kb.s["pw"], 1)
                kb.inc("pw")
        kb.free(tb_)


def emit_std_attn_core(kb, q_dram, k_dram, vtm_dram, nh, causal, sink,
                       l1aa_norm=False, tag="at"):
    nc = kb.nc
    cst = kb.sb["consts"]
    idn = kb.sb["ident_bf"]
    idnf = kb.sb["ident_f32"]
    onesf = kb.sb["onesrow_f32"]
    qb = kb.alloc(tag + "q", [128, 2, T], BF16)
    kb_t = kb.alloc(tag + "k", [128, 2, T], BF16)
    vt = kb.alloc(tag + "v", [128, 2, 32, 64], BF16)
    srow = kb.alloc(tag + "s", [128, T], F32)
    prow = kb.alloc(tag + "p", [128, T], BF16)
    ptb = kb.alloc(tag + "t", [128, 16, 128], BF16)
    cols = kb.alloc(tag + "c", [128, 8], F32)
    recb = kb.alloc(tag + "r", [64, 128], F32)
    outv = kb.alloc(tag + "o", [64, 128], F32)
    rrow = kb.alloc(tag + "rr", [1, 128], F32)

    NM, SUMS, REC, NMP, EV, LV = range(6)

    with contextlib.ExitStack() as pc:
        psS = [pc.enter_context(nc.psum_tensor(kb.uname(tag + "S"), [128, 512], F32))
               for _ in range(2)]
        psT = [pc.enter_context(nc.psum_tensor(kb.uname(tag + "T"), [128, 128], BF16))
               for _ in range(2)]
        psO = pc.enter_context(nc.psum_tensor(kb.uname(tag + "O"), [64, 128], F32))
        psR = pc.enter_context(nc.psum_tensor(kb.uname(tag + "R"), [64, 128], F32))

        for hc in range(nh // 2):
            par = hc % 2
            with nc.Block() as b:
                @b.sync
                def _(sy):
                    if hc >= 2:
                        sy.wait_ge(kb.s["hcd"], kb.val("hcd") - 1)
                    i = sy.dma_start(out=qb[:, par, :], in_=q_dram.rearrange(
                        "(c p) t -> p c t", p=128)[:, hc, :])
                    dchain(i, kb, "lq")
                    i = sy.dma_start(out=kb_t[:, par, :], in_=k_dram.rearrange(
                        "(c p) t -> p c t", p=128)[:, hc, :])
                    dchain(i, kb, "lq")
                    vtp = vt[:, par]
                    for half in range(2):
                        i = sy.dma_start(
                            out=vtp[:, half * 16:(half + 1) * 16, :],
                            in_=vtm_dram[hc * 2 + half].rearrange("(a p) b -> p a b", p=128))
                        dchain(i, kb, "lq")

            lq_t = kb.val("lq")
            for hh in range(2):
                h = hc * 2 + hh
                hp = hh * 64
                for qi in range(16):
                    w = (qi + 1) * 128 if causal else T
                    nkc = (w + 511) // 512
                    nkb = w // 128
                    with nc.Block() as b:
                        evA0 = kb.val("evd")

                        @b.tensor
                        def _(t):
                            t.wait_ge(kb.s["lq"], lq_t)
                            t.wait_ge(kb.s["acs"], kb.val("acs"))
                            for kc in range(nkc):
                                wc = min(512, w - kc * 512)
                                if kc >= 2:
                                    t.wait_ge(kb.s["evd"], evA0 + kc - 1)
                                ins = t.matmul(psS[kc % 2][:, 0:wc],
                                               qb[hp:hp + 64, par, qi * 128:(qi + 1) * 128],
                                               kb_t[hp:hp + 64, par, kc * 512:kc * 512 + wc],
                                               start=True, stop=True)
                                ins.then_inc(kb.s["pe"], 1)
                                kb.inc("pe")

                        pe_base = kb.val("pe") - nkc

                        @b.vector
                        def _(v):
                            for kc in range(nkc):
                                wc = min(512, w - kc * 512)
                                v.wait_ge(kb.s["pe"], pe_base + kc + 1)
                                ins = v.tensor_copy(srow[:, kc * 512:kc * 512 + wc],
                                                    psS[kc % 2][:, 0:wc])
                                chain(ins, kb, "evd")

                        if causal:
                            @b.gpsimd
                            def _(g):
                                g.wait_ge(kb.s["evd"], kb.val("evd"))
                                ins = g.tensor_tensor(srow[:, (w - 128):w],
                                                      srow[:, (w - 128):w],
                                                      kb.sb["cmask"][:], ALU.add)
                                chain(ins, kb, "po")

                        @b.vector
                        def _(v):
                            if causal:
                                v.wait_ge(kb.s["po"], kb.val("po"))
                            else:
                                v.wait_ge(kb.s["evd"], kb.val("evd"))
                            ins = v.tensor_reduce(cols[:, NM:NM + 1], srow[:, 0:w],
                                                  axis=AX.X, op=ALU.max, negate=True)
                            if l1aa_norm:
                                ins = v.tensor_reduce(cols[:, NMP:NMP + 1], srow[:, 0:w - 64],
                                                      axis=AX.X, op=ALU.max, negate=True)
                            chain(ins, kb, "ve")

                        @b.scalar
                        def _(s):
                            s.wait_ge(kb.s["ve"], kb.val("ve"))
                            if l1aa_norm:
                                s.activation(prow[:, 0:w - 64], srow[:, 0:w - 64], AF.Exp,
                                             bias=cols[:, NM:NM + 1])
                                ins = s.activation(prow[:, w - 64:w], srow[:, w - 64:w], AF.Exp,
                                                   bias=cols[:, NM:NM + 1],
                                                   accum_out=cols[:, SUMS:SUMS + 1])
                            else:
                                ins = s.activation(prow[:, 0:w], srow[:, 0:w], AF.Exp,
                                                   bias=cols[:, NM:NM + 1],
                                                   accum_out=cols[:, SUMS:SUMS + 1])
                            chain(ins, kb, "acs")

                        if l1aa_norm:
                            @b.vector
                            def _(v):
                                v.wait_ge(kb.s["acs"], kb.val("acs"))
                                ins = v.tensor_tensor(cols[:, EV:EV + 1], cols[:, NM:NM + 1],
                                                      cols[:, NMP:NMP + 1], ALU.subtract)
                                chain(ins, kb, "ve")

                            @b.scalar
                            def _(s):
                                s.wait_ge(kb.s["ve"], kb.val("ve"))
                                ins = s.activation(cols[:, EV:EV + 1], cols[:, EV:EV + 1], AF.Exp)
                                chain(ins, kb, "ac")

                            @b.vector
                            def _(v):
                                v.wait_ge(kb.s["ac"], kb.val("ac"))
                                ins = v.scalar_tensor_tensor(
                                    cols[:, LV:LV + 1], cols[:, EV:EV + 1],
                                    cols[:, SUMS:SUMS + 1], cols[:, SUMS:SUMS + 1],
                                    ALU.mult, ALU.add)
                                ins.then_inc(kb.s["ve"], 1)
                                kb.inc("ve")
                                v.wait_ge(kb.s["ve"], kb.val("ve"))
                                ins = v.reciprocal(cols[:, REC:REC + 1], cols[:, LV:LV + 1])
                                chain(ins, kb, "ve")
                        else:
                            @b.vector
                            def _(v):
                                v.wait_ge(kb.s["acs"], kb.val("acs"))
                                ins = v.reciprocal(cols[:, REC:REC + 1], cols[:, SUMS:SUMS + 1])
                                chain(ins, kb, "ve")

                        @b.tensor
                        def _(t):
                            t.wait_ge(kb.s["ve"], kb.val("ve"))
                            ins = t.matmul(psR[0:1, :], cols[:, REC:REC + 1], idnf[:, :],
                                           start=True, stop=True)
                            chain(ins, kb, "pe")

                        @b.vector
                        def _(v):
                            v.wait_ge(kb.s["pe"], kb.val("pe"))
                            ins = v.tensor_copy(rrow[0:1, :], psR[0:1, :])
                            chain(ins, kb, "ve")

                        @b.tensor
                        def _(t):
                            t.wait_ge(kb.s["ve"], kb.val("ve"))
                            ins = t.matmul(psR[0:64, :], onesf[0:1, 0:64], rrow[0:1, :],
                                           start=True, stop=True)
                            chain(ins, kb, "pe")

                        @b.vector
                        def _(v):
                            v.wait_ge(kb.s["pe"], kb.val("pe"))
                            ins = v.tensor_copy(recb[:, :], psR[0:64, :])
                            chain(ins, kb, "ve")

                        evG0 = kb.val("evd")

                        @b.tensor
                        def _(t):
                            t.wait_ge(kb.s["acs"], kb.val("acs"))
                            for kbi in range(nkb):
                                if kbi >= 2:
                                    t.wait_ge(kb.s["evd"], evG0 + kbi - 1)
                                ins = t.transpose(psT[kbi % 2][:],
                                                  prow[:, kbi * 128:(kbi + 1) * 128], idn[:])
                                ins.then_inc(kb.s["pe"], 1)
                                kb.inc("pe")

                        peT = kb.val("pe") - nkb

                        @b.vector
                        def _(v):
                            for kbi in range(nkb):
                                v.wait_ge(kb.s["pe"], peT + kbi + 1)
                                ins = v.tensor_copy(ptb[:, kbi, :], psT[kbi % 2][:])
                                chain(ins, kb, "evd")

                        @b.tensor
                        def _(t):
                            t.wait_ge(kb.s["evd"], kb.val("evd"))
                            for kbi in range(nkb):
                                ins = t.matmul(psO[:, :], vt[:, par][:, hh * 16 + kbi, :],
                                               ptb[:, kbi, :],
                                               start=(kbi == 0), stop=(kbi == nkb - 1))
                            ins.then_inc(kb.s["pe"], 1)
                            kb.inc("pe")

                        @b.vector
                        def _(v):
                            v.wait_ge(kb.s["pe"], kb.val("pe"))
                            ins = v.tensor_tensor(outv[:, :], psO[:, :], recb[:, :], ALU.mult)
                            chain(ins, kb, "evd")

                        sink(b, h, qi, outv)

            with nc.Block() as b:
                @b.gpsimd
                def _(g):
                    g.wait_ge(kb.s["spl"], kb.val("spl"))
                    g.wait_ge(kb.s["evd"], kb.val("evd"))
                    i = g.engine_nop()
                    i.then_inc(kb.s["hcd"], 1)
                    kb.inc("hcd")

    kb.free(qb, kb_t, vt, srow, prow, ptb, cols, recb, outv, rrow)


def emit_out_proj(kb, att_dram, wname, bias_name=None, n_k=DC, tag="op"):
    nc = kb.nc
    x = kb.sb["x_res"]
    wsb = kb.alloc(tag + "w", [128, n_k, 1024], BF16)
    ab = kb.alloc(tag + "a", [128, n_k, T], BF16)
    if bias_name is not None:
        bcol = kb.alloc(tag + "b", [128, 8], F32)

    with nc.Block() as b:
        @b.gpsimd
        def _(g):
            g.wait_ge(kb.s["pw"], kb.val("pw"))
            i = g.dma_start(out=wsb[:, :, :],
                            in_=kb.dram[wname].rearrange("(c p) m -> p c m", p=128))
            dchain(i, kb, "lw")

        @b.sync
        def _(sy):
            sy.wait_ge(kb.s["spl"], kb.val("spl"))
            i = sy.dma_start(out=ab[:, :, :],
                             in_=att_dram.rearrange("(c p) t -> p c t", p=128))
            dchain(i, kb, "lq")
            if bias_name is not None:
                i = sy.dma_start(out=bcol[:, :], in_=kb.dram[bias_name])
                dchain(i, kb, "lq")

    def evac(b, key, ps, pe_t):
        mc, tc = key

        @b.vector
        def _(v):
            v.wait_ge(kb.s["pe"], pe_t)
            xs = kb.sb["x_res"][:, mc, tc * 512:(tc + 1) * 512]
            if bias_name is not None:
                ins = v.scalar_tensor_tensor(xs, ps[:], bcol[:, mc:mc + 1], xs,
                                             ALU.add, ALU.add)
            else:
                ins = v.tensor_tensor(xs, xs, ps[:], ALU.add)
            chain(ins, kb, "evd")

    tiles = [(mc, tc) for mc in range(DC) for tc in range(4)]
    emit_mm(kb, tiles,
            stat_fn=lambda key, kc: wsb[:, kc, key[0] * 128:(key[0] + 1) * 128],
            mov_fn=lambda key, kc: ab[:, kc, key[1] * 512:(key[1] + 1) * 512],
            nk_fn=lambda key: n_k, evac_fn=evac, tag=tag,
            waits=[("lw", kb.val("lw")), ("lq", kb.val("lq"))])
    with kb.nc.Block() as b:
        @b.gpsimd
        def _(g):
            g.wait_ge(kb.s["evd"], kb.val("evd"))
            i = g.engine_nop()
            i.then_inc(kb.s["xr"], 1)
            kb.inc("xr")
            gi = g.engine_nop()
            gi.then_inc(kb.s["pw"], 1)
            kb.inc("pw")
    fl = [wsb, ab] + ([bcol] if bias_name is not None else [])
    kb.free(*fl)


def mk_spill_sink(kb, att_dram):
    def sink(b, h, qi, outv):
        @b.gpsimd
        def _(g):
            g.wait_ge(kb.s["evd"], kb.val("evd"))
            ins = g.dma_start(out=att_dram[h * 64:(h + 1) * 64, qi * 128:(qi + 1) * 128],
                              in_=outv[:, :])
            dchain(ins, kb, "spl")
    return sink


def emit_l2aa(kb):
    q_d = kb.dscratch(kb.uname("l2aq"), [D, T], BF16)
    k_d = kb.dscratch(kb.uname("l2ak"), [D, T], BF16)
    v_d = kb.dscratch(kb.uname("l2av"), [16, T, 64], BF16)
    a_d = kb.dscratch(kb.uname("l2ao"), [D, T], BF16)
    emit_proj_to_dram(kb, "l2a_wv", kb.dscratch(kb.uname("l2avf"), [D, T], BF16),
                      tag="l2av", vtm_dram=v_d)
    emit_proj_to_dram(kb, "l2a_wq", q_d, scale=0.125, tag="l2aq")
    emit_proj_to_dram(kb, "l2a_wk", k_d, tag="l2ak")
    kb.free(kb.sb["xn"])
    emit_std_attn_core(kb, q_d, k_d, v_d, 16, causal=False,
                       sink=mk_spill_sink(kb, a_d), tag="l2at")
    kb.sb["xn"] = kb.alloc("xn", [128, DC, T], BF16)
    emit_out_proj(kb, a_d, "l2a_wo", tag="l2ap")


def emit_l1aa(kb):
    q_d = kb.dscratch(kb.uname("l1aq"), [D, T], BF16)
    k_d = kb.dscratch(kb.uname("l1ak"), [D, T], BF16)
    v_d = kb.dscratch(kb.uname("l1av"), [16, T, 64], BF16)
    a_d = kb.dscratch(kb.uname("l1ao"), [D, T], BF16)
    emit_proj_to_dram(kb, "l1a_wv", kb.dscratch(kb.uname("l1avf"), [D, T], BF16),
                      tag="l1av", vtm_dram=v_d)
    emit_proj_to_dram(kb, "l1a_wq", q_d, scale=0.125, tag="l1aq")
    emit_proj_to_dram(kb, "l1a_wk", k_d, tag="l1ak")
    kb.free(kb.sb["xn"])
    emit_std_attn_core(kb, q_d, k_d, v_d, 16, causal=False,
                       sink=mk_spill_sink(kb, a_d), l1aa_norm=True, tag="l1at")
    kb.sb["xn"] = kb.alloc("xn", [128, DC, T], BF16)
    emit_out_proj(kb, a_d, "l1a_wo", bias_name="l1a_ob", tag="l1ap")



def emit_vtm_local(kb, vbuf, vtm_sb, n_m=DC):
    nc = kb.nc
    idn = kb.sb["ident_bf"]
    with contextlib.ExitStack() as pc:
        pst = [pc.enter_context(nc.psum_tensor(kb.uname("vl"), [128, 128], BF16))
               for _ in range(4)]
        pairs = [(dc, tb) for dc in range(n_m) for tb in range(16)]
        pe0 = kb.val("pe")
        ev0 = kb.val("evd")
        with nc.Block() as b:
            @b.tensor
            def _(t):
                t.wait_ge(kb.s["evd"], ev0)
                for i, (dc, tb) in enumerate(pairs):
                    if i >= 4:
                        t.wait_ge(kb.s["evd"], ev0 + (i - 4) + 1)
                    ins = t.transpose(pst[i % 4][:], vbuf[:, dc, tb * 128:(tb + 1) * 128], idn[:])
                    ins.then_inc(kb.s["pe"], 1)
                    kb.inc("pe")

            @b.vector
            def _(v):
                for i, (dc, tb) in enumerate(pairs):
                    v.wait_ge(kb.s["pe"], pe0 + i + 1)
                    ins = v.tensor_copy(vtm_sb[:, tb, dc * 128:(dc + 1) * 128], pst[i % 4][:])
                    chain(ins, kb, "evd")


def emit_l0aa(kb):
    nc = kb.nc
    x, xn = kb.sb["x_res"], kb.sb["xn"]
    cst = kb.sb["consts"]
    bv = kb.alloc("l0ab_", [128, 24], F32)
    sg = kb.alloc("l0as", [128, 24], F32)
    kt = kb.alloc("l0ak", [128, DC, T], BF16)
    vtm = kb.alloc("l0av", [128, 16, 1024], BF16)

    with nc.Block() as b:
        @b.sync
        def _(sy):
            i = sy.dma_start(out=bv[:, :], in_=kb.dram["l0a_b"])
            dchain(i, kb, "lw")

        @b.scalar
        def _(s):
            s.wait_ge(kb.s["lw"], kb.val("lw"))
            i = s.activation(sg[:, 0:24], bv[:, 0:24], AF.Sign)
            chain(i, kb, "ac")

        @b.vector
        def _(v):
            v.wait_ge(kb.s["ac"], kb.val("ac"))
            i = v.tensor_tensor(sg[:, 0:8], sg[:, 0:8], sg[:, 8:16], ALU.mult)
            chain(i, kb, "ve")

    mb = kb.sb["mean_b"]
    xm = kb.alloc("l0axm", [128, 2, T], F32)
    with nc.Block() as b:
        ac_s0 = kb.val("ac")

        @b.vector
        def _(v):
            v.wait_ge(kb.s["ve"], kb.val("ve"))
            v.wait_ge(kb.s["ln"], kb.val("ln"))
            for dc in range(DC):
                if dc >= 2:
                    v.wait_ge(kb.s["ac"], ac_s0 + dc - 1)
                i = v.tensor_tensor(xm[:, dc % 2, :], x[:, dc, :], mb[:, :], ALU.subtract)
                i.then_inc(kb.s["ve"], 1)
                kb.inc("ve")

        ve_s0 = kb.val("ve") - DC
        @b.scalar
        def _(s):
            for dc in range(DC):
                s.wait_ge(kb.s["ve"], ve_s0 + dc + 1)
                i = s.activation(kt[:, dc, :], xm[:, dc % 2, :], AF.Sign)
                i.then_inc(kb.s["ac"], 1)
                kb.inc("ac")
    kb.free(xm)
    kb.free(kb.sb.pop("mean_b"))

    vf = kb.alloc("l0avf", [128, DC, T], BF16)
    with nc.Block() as b:
        @b.vector
        def _(v):
            v.wait_ge(kb.s["ac"], kb.val("ac"))
            for dc in range(DC):
                i = v.tensor_scalar(vf[:, dc, :], xn[:, dc, :], sg[:, 16 + dc:17 + dc],
                                    None, ALU.mult)
            chain(i, kb, "evd")
    emit_vtm_local(kb, vf, vtm)
    kb.free(vf)
    kb.free(kb.sb["xn"])

    qs = kb.alloc("l0aq", [128, 2, DC, 512], BF16)
    ptc = kb.alloc("l0ap", [128, 16, 512], BF16)
    scrap = kb.alloc("l0ax", [128, 8], BF16)
    with contextlib.ExitStack() as pc:
        psS = [pc.enter_context(nc.psum_tensor(kb.uname("l0S"), [128, 512], F32))
               for _ in range(4)]
        psO = [pc.enter_context(nc.psum_tensor(kb.uname("l0O"), [128, 512], F32))
               for _ in range(2)]
        for qc in range(4):
            par = qc % 2
            with nc.Block() as b:
                @b.vector
                def _(v):
                    v.wait_ge(kb.s["evd"], kb.val("evd"))
                    if qc >= 2:
                        v.wait_ge(kb.s["pe"], kb.val("pe"))
                    qsp = qs[:, par]
                    for dc in range(DC):
                        ins = v.tensor_scalar(qsp[:, dc, :],
                                              kt[:, dc, qc * 512:(qc + 1) * 512],
                                              sg[:, dc:dc + 1], None, ALU.mult)
                    chain(ins, kb, "ve")

            nkb = qc * 4 + 4
            acs_q0 = kb.val("acs")
            for kbi in range(nkb):
                with nc.Block() as b:
                    @b.tensor
                    def _(t):
                        t.wait_ge(kb.s["ve"], kb.val("ve"))
                        if kbi >= 4:
                            t.wait_ge(kb.s["acs"], acs_q0 + kbi - 3)
                        else:
                            t.wait_ge(kb.s["acs"], acs_q0)
                        qsp = qs[:, par]
                        for dc in range(DC):
                            ins = t.matmul(psS[kbi % 4][:],
                                           kt[:, dc, kbi * 128:(kbi + 1) * 128],
                                           qsp[:, dc, :],
                                           start=(dc == 0), stop=(dc == DC - 1))
                        ins.then_inc(kb.s["pe"], 1)
                        kb.inc("pe")

                    @b.scalar
                    def _(s):
                        s.wait_ge(kb.s["pe"], kb.val("pe"))
                        ins = s.activation(ptc[:, kbi, :], psS[kbi % 4][:], AF.Sigmoid,
                                           scale=cst[:, C_EIGHTH:C_EIGHTH + 1])
                        chain(ins, kb, "acs")

                    if kbi >= qc * 4:
                        @b.gpsimd
                        def _(g):
                            g.wait_ge(kb.s["acs"], kb.val("acs"))
                            ins = g.affine_select(
                                ptc[:, kbi, :], ptc[:, kbi, :],
                                pattern=[[1, 512]], compare_op=ALU.is_ge, fill=0.0,
                                base=512 * qc - 128 * kbi, channel_multiplier=-1)
                            chain(ins, kb, "po")

            import os
            if os.environ.get("L0AA_DBG") and qc == 3:
                dbg_p = kb.dram.get("dbg_ptc")
                if dbg_p is None:
                    dbg_p = kb.nc.dram_tensor("dbg_ptc", [128, 16, 512], BF16,
                                              kind="ExternalOutput").ap()
                    kb.dram["dbg_ptc"] = dbg_p
                    dbg_q = kb.nc.dram_tensor("dbg_qs", [128, DC, 512], BF16,
                                              kind="ExternalOutput").ap()
                    kb.dram["dbg_qs"] = dbg_q
                    dbg_k = kb.nc.dram_tensor("dbg_kt", [128, DC, T], BF16,
                                              kind="ExternalOutput").ap()
                    kb.dram["dbg_kt"] = dbg_k
                with nc.Block() as bdg:
                    @bdg.gpsimd
                    def _(g):
                        g.wait_ge(kb.s["acs"], kb.val("acs"))
                        g.wait_ge(kb.s["po"], kb.val("po"))
                        i = g.dma_start(out=kb.dram["dbg_ptc"], in_=ptc[:, :, :])
                        dchain(i, kb, "spl")
                        i = g.dma_start(out=kb.dram["dbg_qs"], in_=qs[:, par])
                        dchain(i, kb, "spl")
                        i = g.dma_start(out=kb.dram["dbg_kt"], in_=kt[:, :, :])
                        dchain(i, kb, "spl")
                        g.wait_ge(kb.s["spl"], kb.val("spl"))
            with nc.Block() as bset:
                @bset.scalar
                def _(s2):
                    s2.wait_ge(kb.s["acs"], kb.val("acs"))
                    ins = s2.activation(scrap[:, 0:8], ptc[:, nkb - 1, 0:8], AF.Square)
                    chain(ins, kb, "acs")

            ev_c0 = kb.val("evd")
            with nc.Block() as b:
                @b.tensor
                def _(t):
                    t.wait_ge(kb.s["acs"], kb.val("acs"))
                    t.wait_ge(kb.s["po"], kb.val("po"))
                    t.wait_ge(kb.s["evd"], ev_c0)
                    for dvc in range(DC):
                        if dvc >= 2:
                            t.wait_ge(kb.s["evd"], ev_c0 + dvc - 1)
                        for kbi in range(nkb):
                            ins = t.matmul(psO[dvc % 2][:],
                                           vtm[:, kbi, dvc * 128:(dvc + 1) * 128],
                                           ptc[:, kbi, :],
                                           start=(kbi == 0), stop=(kbi == nkb - 1))
                        ins.then_inc(kb.s["pe"], 1)
                        kb.inc("pe")

                pe0 = kb.val("pe") - DC

                @b.vector
                def _(v):
                    for dvc in range(DC):
                        v.wait_ge(kb.s["pe"], pe0 + dvc + 1)
                        xs = x[:, dvc, qc * 512:(qc + 1) * 512]
                        ins = v.tensor_tensor(xs, xs, psO[dvc % 2][:], ALU.add)
                        chain(ins, kb, "evd")
    with nc.Block() as b:
        @b.gpsimd
        def _(g):
            g.wait_ge(kb.s["evd"], kb.val("evd"))
            i = g.engine_nop()
            i.then_inc(kb.s["xr"], 1)
            kb.inc("xr")
    kb.free(bv, sg, kt, vtm, qs, ptc, scrap)
    kb.sb["xn"] = kb.alloc("xn", [128, DC, T], BF16)


def emit_l2ffn(kb):
    nc = kb.nc
    x, xn = kb.sb["x_res"], kb.sb["xn"]
    wbuf = kb.alloc("l2fw", [128, DC, 1024], BF16)
    mu = kb.alloc("l2fm", [128, 16], F32)
    kact_d = kb.dscratch(kb.uname("l2fk"), [D, T], BF16)

    def load_w(wname):
        with nc.Block() as b:
            @b.gpsimd
            def _(g):
                g.wait_ge(kb.s["pw"], kb.val("pw"))
                i = g.dma_start(out=wbuf[:, :, :],
                                in_=kb.dram[wname].rearrange("(c p) m -> p c m", p=128))
                dchain(i, kb, "lw")

    def build_mix(mix, mu_off):
        with nc.Block() as b:
            @b.vector
            def _(v):
                v.wait_ge(kb.s["ln"], kb.val("ln"))
                v.wait_ge(kb.s["pw"], kb.val("pw"))
                for dc in range(DC):
                    v.tensor_tensor(mix[:, dc, 1:T], xn[:, dc, 1:T], xn[:, dc, 0:T - 1],
                                    ALU.subtract)
                for dc in range(DC):
                    v.scalar_tensor_tensor(mix[:, dc, 1:T], mix[:, dc, 1:T],
                                           mu[:, mu_off + dc:mu_off + dc + 1],
                                           xn[:, dc, 0:T - 1], ALU.mult, ALU.add)
                for dc in range(DC):
                    ins = v.tensor_scalar(mix[:, dc, 0:1], xn[:, dc, 0:1],
                                          mu[:, mu_off + dc:mu_off + dc + 1], None, ALU.mult)
                chain(ins, kb, "ve")

    load_w("l2f_wk")
    with nc.Block() as b:
        @b.sync
        def _(sy):
            i = sy.dma_start(out=mu[:, :], in_=kb.dram["l2f_mu"])
            dchain(i, kb, "lq")
    mix = kb.alloc("l2fx", [128, DC, T], BF16)
    kbuf = kb.alloc("l2fkb", [128, DC, T], BF16)
    build_mix(mix, 8)
    tb = kb.alloc("l2ft", [128, 4, 512], BF16)

    def evac_k(b, key, ps, pe_t):
        mc, tc = key
        i_t = key[0] * 4 + key[1]

        @b.vector
        def _(v):
            v.wait_ge(kb.s["pe"], pe_t)
            if i_t >= 4:
                v.wait_ge(kb.s["po"], kb.val("po") - 3)
            ins = v.tensor_scalar_max(tb[:, i_t % 4, :], ps[:], 0.0)
            chain(ins, kb, "evd")

        @b.gpsimd
        def _(g):
            g.wait_ge(kb.s["evd"], kb.val("evd"))
            ins = g.tensor_tensor(kbuf[:, mc, tc * 512:(tc + 1) * 512],
                                  tb[:, i_t % 4, :], tb[:, i_t % 4, :], ALU.mult)
            chain(ins, kb, "po")

    tiles = [(mc, tc) for mc in range(DC) for tc in range(4)]
    emit_mm(kb, tiles,
            stat_fn=lambda key, kc: wbuf[:, kc, key[0] * 128:(key[0] + 1) * 128],
            mov_fn=lambda key, kc: mix[:, kc, key[1] * 512:(key[1] + 1) * 512],
            nk_fn=lambda key: DC, evac_fn=evac_k, tag="l2fk",
            waits=[("ve", kb.val("ve")), ("lw", kb.val("lw"))])
    with nc.Block() as b:
        @b.sync
        def _(sy):
            sy.wait_ge(kb.s["po"], kb.val("po"))
            i = sy.dma_start(out=kact_d.rearrange("(c p) t -> p c t", p=128),
                             in_=kbuf[:, :, :])
            dchain(i, kb, "spl")

        @b.gpsimd
        def _(g):
            g.wait_ge(kb.s["spl"], kb.val("spl"))
            i = g.engine_nop()
            i.then_inc(kb.s["pw"], 1)
            kb.inc("pw")
    kb.free(kbuf, tb)

    load_w("l2f_wr")
    rr = kb.alloc("l2fr", [128, DC, T], BF16)
    build_mix(mix, 0)

    def evac_r(b, key, ps, pe_t):
        mc, tc = key

        @b.scalar
        def _(s):
            s.wait_ge(kb.s["pe"], pe_t)
            ins = s.activation(rr[:, mc, tc * 512:(tc + 1) * 512], ps[:], AF.Sigmoid)
            chain(ins, kb, "evd")

    emit_mm(kb, tiles,
            stat_fn=lambda key, kc: wbuf[:, kc, key[0] * 128:(key[0] + 1) * 128],
            mov_fn=lambda key, kc: mix[:, kc, key[1] * 512:(key[1] + 1) * 512],
            nk_fn=lambda key: DC, evac_fn=evac_r, tag="l2fr",
            waits=[("ve", kb.val("ve")), ("lw", kb.val("lw"))])
    kb.free(mix)
    with nc.Block() as b:
        @b.gpsimd
        def _(g):
            g.wait_ge(kb.s["evd"], kb.val("evd"))
            i = g.engine_nop()
            i.then_inc(kb.s["pw"], 1)
            kb.inc("pw")

    load_w("l2f_wv")
    kch = kb.alloc("l2fc", [128, 2, DC, 512], BF16)
    tv = kb.alloc("l2fv", [128, 4, 512], F32)
    with contextlib.ExitStack() as pc:
        pss = [pc.enter_context(nc.psum_tensor(kb.uname("l2fp"), [128, 512], F32))
               for _ in range(4)]
        pe_v0 = kb.val("pe")
        po_v0 = kb.val("po")
        ev_v0 = kb.val("evd")
        for tc in range(4):
            par = tc % 2
            with nc.Block() as b:
                @b.sync
                def _(sy):
                    if tc >= 2:
                        sy.wait_ge(kb.s["pe"], pe_v0 + (tc - 1) * DC)
                    i = sy.dma_start(
                        out=kch[:, par],
                        in_=kact_d.rearrange("(c p) t -> p c t", p=128)[
                            :, :, tc * 512:(tc + 1) * 512])
                    dchain(i, kb, "lq")

                lq_t = kb.val("lq")

                @b.tensor
                def _(t):
                    t.wait_ge(kb.s["lq"], lq_t)
                    t.wait_ge(kb.s["lw"], kb.val("lw"))
                    kchp = kch[:, par]
                    for mc in range(DC):
                        g = tc * DC + mc
                        if g >= 4:
                            t.wait_ge(kb.s["po"], po_v0 + (g - 4) + 1)
                        for kc in range(DC):
                            ins = t.matmul(pss[g % 4][:],
                                           wbuf[:, kc, mc * 128:(mc + 1) * 128],
                                           kchp[:, kc, :],
                                           start=(kc == 0), stop=(kc == DC - 1))
                        ins.then_inc(kb.s["pe"], 1)
                        kb.inc("pe")

                @b.vector
                def _(v):
                    for mc in range(DC):
                        g = tc * DC + mc
                        v.wait_ge(kb.s["pe"], pe_v0 + g + 1)
                        ins = v.tensor_tensor(tv[:, g % 4, :], pss[g % 4][:],
                                              rr[:, mc, tc * 512:(tc + 1) * 512], ALU.mult)
                        chain(ins, kb, "evd")

                @b.gpsimd
                def _(g_):
                    for mc in range(DC):
                        g = tc * DC + mc
                        g_.wait_ge(kb.s["evd"], ev_v0 + g + 1)
                        xs = x[:, mc, tc * 512:(tc + 1) * 512]
                        ins = g_.tensor_tensor(xs, xs, tv[:, g % 4, :], ALU.add)
                        chain(ins, kb, "po")
    with nc.Block() as b:
        @b.gpsimd
        def _(g):
            g.wait_ge(kb.s["po"], kb.val("po"))
            i = g.engine_nop()
            i.then_inc(kb.s["xr"], 1)
            kb.inc("xr")
            i2 = g.engine_nop()
            i2.then_inc(kb.s["pw"], 1)
            kb.inc("pw")
    kb.free(wbuf, mu, kch, tv, rr)


def emit_l2ab(kb):
    nc = kb.nc
    x = kb.sb["x_res"]
    vcols = kb.alloc("l2bv", [128, 56], F32)
    with nc.Block() as b:
        @b.sync
        def _(sy):
            i = sy.dma_start(out=vcols[:, :], in_=kb.dram["l2b_vec"])
            dchain(i, kb, "lq")

    v_d = kb.dscratch(kb.uname("l2bv"), [16, T, 64], BF16)
    vf_d = kb.dscratch(kb.uname("l2bvf"), [D, T], BF16)
    q_d = kb.dscratch(kb.uname("l2bq"), [D, T], BF16)
    k_d = kb.dscratch(kb.uname("l2bk"), [D, T], BF16)
    av_d = kb.dscratch(kb.uname("l2bav"), [D, T], BF16)
    avtm_d = kb.dscratch(kb.uname("l2bavt"), [16, T, 64], BF16)
    aav_d = kb.dscratch(kb.uname("l2baav"), [D, T], BF16)
    comb_d = kb.dscratch(kb.uname("l2bcb"), [D, T], BF16)

    emit_proj_to_dram(kb, "l2b_wv", vf_d, bias_col=lambda mc: vcols[:, 16 + mc:17 + mc],
                      tag="l2bv", vtm_dram=v_d)
    emit_proj_to_dram(kb, "l2b_wq", q_d, scale=0.125,
                      bias_col=lambda mc: vcols[:, 0 + mc:1 + mc], tag="l2bq")
    emit_proj_to_dram(kb, "l2b_wk", k_d, bias_col=lambda mc: vcols[:, 8 + mc:9 + mc],
                      tag="l2bk")
    kb.free(kb.sb["xn"])
    emit_std_attn_core(kb, q_d, k_d, v_d, 16, causal=False,
                       sink=mk_spill_sink(kb, av_d), tag="l2b1")
    avb = kb.alloc("l2bab", [128, DC, T], BF16)
    with nc.Block() as b:
        @b.sync
        def _(sy):
            sy.wait_ge(kb.s["spl"], kb.val("spl"))
            i = sy.dma_start(out=avb[:, :, :],
                             in_=av_d.rearrange("(c p) t -> p c t", p=128))
            dchain(i, kb, "lq")

        @b.gpsimd
        def _(g):
            g.wait_ge(kb.s["lq"], kb.val("lq"))
            i = g.engine_nop()
            i.then_inc(kb.s["evd"], 1)
            kb.inc("evd")
    emit_vtm_spill(kb, avb, avtm_d)
    kb.free(avb)
    emit_std_attn_core(kb, q_d, k_d, avtm_d, 16, causal=False,
                       sink=mk_spill_sink(kb, aav_d), tag="l2b2")

    st = kb.alloc("l2bs", [128, 3, DC, 512], BF16)
    cb = kb.alloc("l2bc", [128, DC, T], BF16)
    with nc.Block() as bq:
        @bq.vector
        def _(v):
            v.wait_ge(kb.s["lq"], kb.val("lq"))
            ins = v.tensor_tensor(vcols[:, 40:48], vcols[:, 40:48], vcols[:, 48:56],
                                  ALU.subtract)
            chain(ins, kb, "ve")

        @bq.gpsimd
        def _(g):
            g.wait_ge(kb.s["ve"], kb.val("ve"))
            ins = g.tensor_scalar(vcols[:, 48:56], vcols[:, 48:56], 2.0, None, ALU.mult)
            chain(ins, kb, "po")

    for tc in range(4):
        with nc.Block() as b:
            @b.sync
            def _(sy):
                sy.wait_ge(kb.s["spl"], kb.val("spl"))
                if tc > 0:
                    sy.wait_ge(kb.s["ve"], kb.val("ve"))
                for j, dsrc in enumerate((vf_d, av_d, aav_d)):
                    i = sy.dma_start(
                        out=st[:, j],
                        in_=dsrc.rearrange("(c p) t -> p c t", p=128)[
                            :, :, tc * 512:(tc + 1) * 512])
                    dchain(i, kb, "lq")

            lq_t = kb.val("lq")

            @b.vector
            def _(v):
                v.wait_ge(kb.s["lq"], lq_t)
                v.wait_ge(kb.s["po"], kb.val("po"))
                for dc in range(DC):
                    v.tensor_scalar(cb[:, dc, tc * 512:(tc + 1) * 512], st[:, 0, dc, :],
                                    vcols[:, 32 + dc:33 + dc], None, ALU.mult)
                for dc in range(DC):
                    v.scalar_tensor_tensor(cb[:, dc, tc * 512:(tc + 1) * 512],
                                           st[:, 1, dc, :], vcols[:, 40 + dc:41 + dc],
                                           cb[:, dc, tc * 512:(tc + 1) * 512],
                                           ALU.mult, ALU.add)
                for dc in range(DC):
                    ins = v.scalar_tensor_tensor(cb[:, dc, tc * 512:(tc + 1) * 512],
                                                 st[:, 2, dc, :], vcols[:, 48 + dc:49 + dc],
                                                 cb[:, dc, tc * 512:(tc + 1) * 512],
                                                 ALU.mult, ALU.add)
                chain(ins, kb, "ve")

    with nc.Block() as b:
        @b.sync
        def _(sy):
            sy.wait_ge(kb.s["ve"], kb.val("ve"))
            i = sy.dma_start(out=comb_d.rearrange("(c p) t -> p c t", p=128),
                             in_=cb[:, :, :])
            dchain(i, kb, "spl")

        @b.gpsimd
        def _(g):
            g.wait_ge(kb.s["spl"], kb.val("spl"))
            i = g.engine_nop()
            i.then_inc(kb.s["pw"], 1)
            kb.inc("pw")
    kb.free(st, cb, vcols)
    kb.sb["xn"] = kb.alloc("xn", [128, DC, T], BF16)
    emit_out_proj(kb, comb_d, "l2b_wo", bias_name="l2b_bo", tag="l2bo")





import os as _os

def _affine_probe(kb, label):
    if not _os.environ.get("AFP"):
        return
    sr = kb.alloc("afp", [128, 256], F32)
    with kb.nc.Block() as b:
        @b.gpsimd
        def _(g):
            try:
                g.affine_select(sr[:, 0:128], sr[:, 0:128], pattern=[[-1, 128]],
                                compare_op=ALU.is_ge, fill=-1e30, base=0,
                                channel_multiplier=1)
                print("AFP", label, "ok")
            except Exception as e:
                print("AFP", label, "FAIL", str(e)[:40])
    kb.free(sr)

def emit_l3ab_lambda(kb, lamt, neglamb):
    nc = kb.nc
    idnf = kb.sb["ident_f32"]
    onesf = kb.sb["onesrow_f32"]
    scol = kb.alloc("l3bsc", [8, 4], F32)
    rro = kb.alloc("l3brr0", [1, 8], F32)
    with contextlib.ExitStack() as pc:
        psR = pc.enter_context(nc.psum_tensor(kb.uname("l3lam"), [128, 8], F32))
        with nc.Block() as b:
            @b.vector
            def _(v):
                v.wait_ge(kb.s["lq"], kb.val("lq"))
                v.tensor_tensor(lamt[:, 0:128], lamt[:, 0:128], lamt[:, 128:256], ALU.mult)
                ins = v.tensor_tensor(lamt[:, 256:384], lamt[:, 256:384],
                                      lamt[:, 384:512], ALU.mult)
                ins.then_inc(kb.s["ve"], 1)
                kb.inc("ve")
                v.wait_ge(kb.s["ve"], kb.val("ve"))
                v.reduce_sum(scol[:, 0:1], lamt[:, 0:128], axis=AX.X)
                ins = v.reduce_sum(scol[:, 1:2], lamt[:, 256:384], axis=AX.X)
                chain(ins, kb, "ve")

            @b.scalar
            def _(s):
                s.wait_ge(kb.s["ve"], kb.val("ve"))
                s.activation(scol[:, 2:3], scol[:, 0:1], AF.Exp)
                ins = s.activation(scol[:, 3:4], scol[:, 1:2], AF.Exp)
                chain(ins, kb, "ac")

            @b.vector
            def _(v):
                v.wait_ge(kb.s["ac"], kb.val("ac"))
                ins = v.tensor_tensor(scol[:, 0:1], scol[:, 2:3], scol[:, 3:4],
                                      ALU.subtract)
                ins.then_inc(kb.s["ve"], 1)
                kb.inc("ve")
                v.wait_ge(kb.s["ve"], kb.val("ve"))
                ins = v.tensor_scalar(scol[:, 1:2], scol[:, 0:1], 0.8, -1.0,
                                      ALU.add, ALU.mult)
                chain(ins, kb, "ve")

            @b.tensor
            def _(t):
                t.wait_ge(kb.s["ve"], kb.val("ve"))
                ins = t.matmul(psR[0:1, :], scol[:, 1:2], idnf[0:8, 0:8],
                               start=True, stop=True)
                chain(ins, kb, "pe")

            @b.vector
            def _(v):
                v.wait_ge(kb.s["pe"], kb.val("pe"))
                ins = v.tensor_copy(rro[0:1, :], psR[0:1, :])
                chain(ins, kb, "ve")

            @b.tensor
            def _(t):
                t.wait_ge(kb.s["ve"], kb.val("ve"))
                ins = t.matmul(psR[:, :], onesf[0:1, :], rro[0:1, :],
                               start=True, stop=True)
                chain(ins, kb, "pe")

            @b.vector
            def _(v):
                v.wait_ge(kb.s["pe"], kb.val("pe"))
                ins = v.tensor_copy(neglamb[:, :], psR[:, :])
                chain(ins, kb, "ve")
    kb.free(scol, rro)


def emit_l3ab(kb):
    nc = kb.nc
    x = kb.sb["x_res"]
    cst = kb.sb["consts"]
    idn = kb.sb["ident_bf"]
    idnf = kb.sb["ident_f32"]
    onesf = kb.sb["onesrow_f32"]
    ob128 = kb.sb["ones128_bf"]
    QSC = 0.08838834764831845

    q_d = kb.dscratch(kb.uname("l3bq"), [2 * D, T], BF16)
    k_d = kb.dscratch(kb.uname("l3bk"), [2 * D, T], BF16)
    v_d = kb.dscratch(kb.uname("l3bv"), [32, T, 64], BF16)
    vf_d = kb.dscratch(kb.uname("l3bvf"), [2 * D, T], BF16)
    a_d = kb.dscratch(kb.uname("l3bo"), [2 * D, T], BF16)

    lamt = kb.alloc("l3bl", [8, 512], F32)
    neglamb = kb.alloc("l3bnl", [128, 8], F32)
    rmsc = kb.alloc("l3brs", [128, 2], F32)
    with nc.Block() as b:
        @b.sync
        def _(sy):
            i = sy.dma_start(out=lamt[:, :], in_=kb.dram["l3b_lam"])
            dchain(i, kb, "lq")
            i = sy.dma_start(out=rmsc[:, :], in_=kb.dram["l3b_rms"])
            dchain(i, kb, "lq")
    emit_l3ab_lambda(kb, lamt, neglamb)
    kb.free(lamt)

    for mo in (0, 8):
        emit_proj_to_dram(kb, "l3b_wv", vf_d, tag=f"l3bv{mo}", vtm_dram=v_d,
                          n_m=8, m_off=mo)
    for mo in (0, 8):
        emit_proj_to_dram(kb, "l3b_wq", q_d, scale=QSC, tag=f"l3bq{mo}",
                          n_m=8, m_off=mo)
    for mo in (0, 8):
        emit_proj_to_dram(kb, "l3b_wk", k_d, tag=f"l3bk{mo}", n_m=8, m_off=mo)
    kb.free(kb.sb["xn"])

    qb = kb.alloc("l3bqq", [128, 2, 2, T], BF16)
    kbt = kb.alloc("l3bkk", [128, 2, 2, T], BF16)
    vt = kb.alloc("l3bvt", [128, 2, 16, 256], BF16)
    srow = kb.alloc("l3bsr", [128, T], F32)
    prow = kb.alloc("l3bpr", [128, 2, T], BF16)
    ptb = kb.alloc("l3bpt", [128, 2, 16, 128], BF16)
    cols = kb.alloc("l3bc", [128, 8], F32)
    recb = kb.alloc("l3brb", [128, 2, 128], F32)
    rmsb = kb.alloc("l3brm", [128, 128], F32)
    sqb = kb.alloc("l3bsq", [128, 2, 128], BF16)
    oc = kb.alloc("l3boc", [128, 2, 128], F32)
    tv = kb.alloc("l3btv", [128, 128], F32)
    obf = kb.alloc("l3bob", [128, 2, 128], BF16)
    rrow = kb.alloc("l3brr", [1, 128], F32)

    with contextlib.ExitStack() as pc:
        psS = [pc.enter_context(nc.psum_tensor(kb.uname("l3S"), [128, 512], F32))
               for _ in range(2)]
        psT = [pc.enter_context(nc.psum_tensor(kb.uname("l3T"), [128, 128], BF16))
               for _ in range(2)]
        psO = [pc.enter_context(nc.psum_tensor(kb.uname("l3O"), [128, 128], F32))
               for _ in range(2)]
        psR = pc.enter_context(nc.psum_tensor(kb.uname("l3R"), [128, 128], F32))

        for h in range(8):
            par = h % 2
            with nc.Block() as b:
                @b.sync
                def _(sy):
                    if h >= 2:
                        sy.wait_ge(kb.s["hcd"], kb.val("hcd") - 1)
                    qbp, kbp, vtp = qb[:, par], kbt[:, par], vt[:, par]
                    for j in range(2):
                        i = sy.dma_start(out=qbp[:, j, :], in_=q_d.rearrange(
                            "(c p) t -> p c t", p=128)[:, 2 * h + j, :])
                        dchain(i, kb, "lq")
                        i = sy.dma_start(out=kbp[:, j, :], in_=k_d.rearrange(
                            "(c p) t -> p c t", p=128)[:, 2 * h + j, :])
                        dchain(i, kb, "lq")
                    for hs in range(4):
                        i = sy.dma_start(
                            out=vtp[:, :, hs * 64:(hs + 1) * 64],
                            in_=v_d[4 * h + hs].rearrange("(a p) b -> p a b", p=128))
                        dchain(i, kb, "lq")

            lq_t = kb.val("lq")
            for qi in range(16):
                w = (qi + 1) * 128
                nkc = (w + 511) // 512
                nkb = w // 128
                for j in range(2):
                    evA0 = kb.val("evd")
                    with nc.Block() as bj:
                        @bj.tensor
                        def _(t, j=j):
                            t.wait_ge(kb.s["lq"], lq_t)
                            t.wait_ge(kb.s["acs"], kb.val("acs"))
                            for kc in range(nkc):
                                wc = min(512, w - kc * 512)
                                if kc >= 2:
                                    t.wait_ge(kb.s["evd"], evA0 + kc - 1)
                                ins = t.matmul(psS[kc % 2][:, 0:wc],
                                               qb[:, par][:, j, qi * 128:(qi + 1) * 128],
                                               kbt[:, par][:, j, kc * 512:kc * 512 + wc],
                                               start=True, stop=True)
                                ins.then_inc(kb.s["pe"], 1)
                                kb.inc("pe")

                        pe_base = kb.val("pe") - nkc

                        @bj.vector
                        def _(v, j=j):
                            for kc in range(nkc):
                                wc = min(512, w - kc * 512)
                                v.wait_ge(kb.s["pe"], pe_base + kc + 1)
                                ins = v.tensor_copy(srow[:, kc * 512:kc * 512 + wc],
                                                    psS[kc % 2][:, 0:wc])
                                chain(ins, kb, "evd")

                    with nc.Block() as bm:
                        @bm.gpsimd
                        def _(g):
                            g.wait_ge(kb.s["evd"], kb.val("evd"))
                            ins = g.tensor_tensor(srow[:, (w - 128):w],
                                                  srow[:, (w - 128):w],
                                                  kb.sb["cmask"][:], ALU.add)
                            chain(ins, kb, "po")

                    with nc.Block() as bx:
                        @bx.vector
                        def _(v, j=j):
                            v.wait_ge(kb.s["po"], kb.val("po"))
                            ins = v.tensor_reduce(cols[:, j:j + 1], srow[:, 0:w],
                                                  axis=AX.X, op=ALU.max, negate=True)
                            chain(ins, kb, "ve")

                        @bx.scalar
                        def _(s, j=j):
                            s.wait_ge(kb.s["ve"], kb.val("ve"))
                            ins = s.activation(prow[:, j, 0:w], srow[:, 0:w], AF.Exp,
                                               bias=cols[:, j:j + 1],
                                               accum_out=cols[:, 2 + j:3 + j])
                            chain(ins, kb, "acs")

                with nc.Block() as b:
                    @b.vector
                    def _(v):
                        v.wait_ge(kb.s["acs"], kb.val("acs"))
                        v.reciprocal(cols[:, 4:5], cols[:, 2:3])
                        ins = v.reciprocal(cols[:, 5:6], cols[:, 3:4])
                        chain(ins, kb, "ve")

                    for j in range(2):
                        @b.tensor
                        def _(t, j=j):
                            t.wait_ge(kb.s["ve"], kb.val("ve"))
                            ins = t.matmul(psR[0:1, :], cols[:, 4 + j:5 + j], idnf[:, :],
                                           start=True, stop=True)
                            chain(ins, kb, "pe")

                        @b.vector
                        def _(v, j=j):
                            v.wait_ge(kb.s["pe"], kb.val("pe"))
                            ins = v.tensor_copy(rrow[0:1, :], psR[0:1, :])
                            chain(ins, kb, "ve")

                        @b.tensor
                        def _(t, j=j):
                            t.wait_ge(kb.s["ve"], kb.val("ve"))
                            ins = t.matmul(psR[:, :], onesf[0:1, :], rrow[0:1, :],
                                           start=True, stop=True)
                            chain(ins, kb, "pe")

                        @b.vector
                        def _(v, j=j):
                            v.wait_ge(kb.s["pe"], kb.val("pe"))
                            ins = v.tensor_copy(recb[:, j, :], psR[:, :])
                            chain(ins, kb, "ve")

                    for j in range(2):
                        evG0 = kb.val("evd")

                        @b.tensor
                        def _(t, j=j):
                            t.wait_ge(kb.s["acs"], kb.val("acs"))
                            t.wait_ge(kb.s["evd"], evG0)
                            for kbi in range(nkb):
                                if kbi >= 2:
                                    t.wait_ge(kb.s["evd"], evG0 + kbi - 1)
                                ins = t.transpose(psT[kbi % 2][:],
                                                  prow[:, j, kbi * 128:(kbi + 1) * 128],
                                                  idn[:])
                                ins.then_inc(kb.s["pe"], 1)
                                kb.inc("pe")

                        peT = kb.val("pe") - nkb

                        @b.vector
                        def _(v, j=j):
                            for kbi in range(nkb):
                                v.wait_ge(kb.s["pe"], peT + kbi + 1)
                                ins = v.tensor_copy(ptb[:, j][:, kbi, :], psT[kbi % 2][:])
                                chain(ins, kb, "evd")

                    for c in range(2):
                        @b.tensor
                        def _(t, c=c):
                            t.wait_ge(kb.s["evd"], kb.val("evd"))
                            t.wait_ge(kb.s["ve"], kb.val("ve"))
                            for j in range(2):
                                for kbi in range(nkb):
                                    ins = t.matmul(psO[j][:, :],
                                                   vt[:, par][:, kbi, c * 128:(c + 1) * 128],
                                                   ptb[:, j][:, kbi, :],
                                                   start=(kbi == 0), stop=(kbi == nkb - 1))
                                ins.then_inc(kb.s["pe"], 1)
                                kb.inc("pe")

                        @b.vector
                        def _(v, c=c):
                            v.wait_ge(kb.s["pe"], kb.val("pe"))
                            v.tensor_tensor(tv[:, :], psO[0][:], recb[:, 0, :], ALU.mult)
                            ins = v.tensor_tensor(oc[:, c, :], psO[1][:], recb[:, 1, :],
                                                  ALU.mult)
                            ins.then_inc(kb.s["ve"], 1)
                            kb.inc("ve")
                            v.wait_ge(kb.s["ve"], kb.val("ve"))
                            ins = v.scalar_tensor_tensor(oc[:, c, :], oc[:, c, :],
                                                         neglamb[:, h:h + 1], tv[:, :],
                                                         ALU.mult, ALU.add)
                            chain(ins, kb, "ve")

                        @b.scalar
                        def _(s, c=c):
                            s.wait_ge(kb.s["ve"], kb.val("ve"))
                            ins = s.activation(sqb[:, c, :], oc[:, c, :], AF.Square)
                            chain(ins, kb, "acs")

                    @b.tensor
                    def _(t):
                        t.wait_ge(kb.s["acs"], kb.val("acs"))
                        for c in range(2):
                            ins = t.matmul(psR[0:1, :], ob128[:, 0:1], sqb[:, c, :],
                                           start=(c == 0), stop=(c == 1))
                        chain(ins, kb, "pe")

                    @b.scalar
                    def _(s):
                        s.wait_ge(kb.s["pe"], kb.val("pe"))
                        ins = s.activation(rrow[0:1, :], psR[0:1, :], AF.Ln,
                                           scale=cst[0:1, C_1_256:C_1_256 + 1],
                                           bias=cst[0:1, C_EPS:C_EPS + 1])
                        ins.then_inc(kb.s["ac"], 1)
                        kb.inc("ac")
                        s.wait_ge(kb.s["ac"], kb.val("ac"))
                        ins = s.activation(rrow[0:1, :], rrow[0:1, :], AF.Exp,
                                           scale=cst[0:1, C_NEGHALF:C_NEGHALF + 1])
                        chain(ins, kb, "ac")

                    @b.tensor
                    def _(t):
                        t.wait_ge(kb.s["ac"], kb.val("ac"))
                        ins = t.matmul(psR[:, :], onesf[0:1, :], rrow[0:1, :],
                                       start=True, stop=True)
                        chain(ins, kb, "pe")

                    @b.vector
                    def _(v):
                        v.wait_ge(kb.s["pe"], kb.val("pe"))
                        ins = v.tensor_copy(rmsb[:, :], psR[:, :])
                        ins.then_inc(kb.s["ve"], 1)
                        kb.inc("ve")
                        v.wait_ge(kb.s["ve"], kb.val("ve"))
                        for c in range(2):
                            v.tensor_tensor(tv[:, :], oc[:, c, :], rmsb[:, :], ALU.mult)
                            ins = v.tensor_scalar(obf[:, c, :], tv[:, :],
                                                  rmsc[:, c:c + 1], 0.2, ALU.mult,
                                                  ALU.mult)
                            ins.then_inc(kb.s["ve"], 1)
                            kb.inc("ve")
                            v.wait_ge(kb.s["ve"], kb.val("ve"))

                    @b.sync
                    def _(sy):
                        sy.wait_ge(kb.s["ve"], kb.val("ve"))
                        for c in range(2):
                            ins = sy.dma_start(
                                out=a_d[h * 256 + c * 128:h * 256 + (c + 1) * 128,
                                        qi * 128:(qi + 1) * 128],
                                in_=obf[:, c, :])
                            dchain(ins, kb, "spl")

            with nc.Block() as b:
                @b.gpsimd
                def _(g):
                    g.wait_ge(kb.s["spl"], kb.val("spl"))
                    g.wait_ge(kb.s["evd"], kb.val("evd"))
                    i = g.engine_nop()
                    i.then_inc(kb.s["hcd"], 1)
                    kb.inc("hcd")

    kb.free(qb, kbt, vt, srow, prow, ptb, cols, recb, rmsb, sqb, oc, tv, obf,
            rrow, neglamb, rmsc)
    emit_out_proj(kb, a_d, "l3b_wo", n_k=16, tag="l3bp")
    kb.sb["xn"] = kb.alloc("xn", [128, DC, T], BF16)



def emit_l1ffn(kb):
    nc = kb.nc
    x, xn = kb.sb["x_res"], kb.sb["xn"]
    cst = kb.sb["consts"]
    idnf = kb.sb["ident_f32"]
    onesf = kb.sb["onesrow_f32"]

    gate = kb.alloc("l1fg", [128, DC, 8], BF16)
    lg = kb.alloc("l1fl", [128, 16, 8], F32)
    rt = kb.alloc("l1fr", [128, 16, 8], F32)
    cmb = kb.alloc("l1fc", [128, 16, 8], F32)
    rcol = kb.alloc("l1fm", [128, 16, 6], F32)
    combT = kb.alloc("l1fct", [8, T], BF16)
    oh8 = kb.alloc("l1foh", [8, 8 * 128], BF16)
    with nc.Block() as b:
        @b.gpsimd
        def _(g):
            g.wait_ge(kb.s["pw"], kb.val("pw"))
            i = g.dma_start(out=gate[:, :, :],
                            in_=kb.dram["l1f_gate"].rearrange("(c p) m -> p c m", p=128))
            dchain(i, kb, "lw")

        @b.vector
        def _(v):
            v.memset(oh8[:, :], 1.0)
            i = v.engine_nop()
            chain(i, kb, "ve")

        @b.gpsimd
        def _(g):
            g.wait_ge(kb.s["ve"], kb.val("ve"))
            g.affine_select(oh8[:, :], oh8[:, :], pattern=[[1, 8 * 128]],
                            compare_op=ALU.is_ge, fill=0.0, base=0,
                            channel_multiplier=-128)
            i = g.affine_select(oh8[:, :], oh8[:, :], pattern=[[-1, 8 * 128]],
                               compare_op=ALU.is_ge, fill=0.0, base=127,
                               channel_multiplier=128)
            chain(i, kb, "po")

    with contextlib.ExitStack() as pc:
        psL = [pc.enter_context(nc.psum_tensor(kb.uname("l1fL"), [128, 8], F32))
               for _ in range(4)]
        with nc.Block() as b:
            pe_l0 = kb.val("pe")

            ev_lg0 = kb.val("evd")

            @b.tensor
            def _(t):
                t.wait_ge(kb.s["lw"], kb.val("lw"))
                t.wait_ge(kb.s["ln"], kb.val("ln"))
                for tt in range(16):
                    if tt >= 4:
                        t.wait_ge(kb.s["evd"], ev_lg0 + tt - 3)
                    for kc in range(DC):
                        ins = t.matmul(psL[tt % 4][:],
                                       xn[:, kc, tt * 128:(tt + 1) * 128],
                                       gate[:, kc, :],
                                       start=(kc == 0), stop=(kc == DC - 1))
                    ins.then_inc(kb.s["pe"], 1)
                    kb.inc("pe")

            ev_l0 = kb.val("evd")

            @b.vector
            def _(v):
                for tt in range(16):
                    v.wait_ge(kb.s["pe"], pe_l0 + tt + 1)
                    ins = v.tensor_copy(lg[:, tt, :], psL[tt % 4][:])
                    chain(ins, kb, "evd")

    with nc.Block() as b:
        @b.vector
        def _(v):
            v.wait_ge(kb.s["evd"], kb.val("evd"))
            for tt in range(16):
                v.tensor_reduce(rcol[:, tt, 0:1], lg[:, tt, :], axis=AX.X,
                                op=ALU.max, negate=True)
            for tt in range(16):
                v.tensor_scalar(rt[:, tt, :], lg[:, tt, :], rcol[:, tt, 0:1],
                                None, ALU.add)
            for tt in range(16):
                v.tensor_scalar(rt[:, tt, :], rt[:, tt, :], -1e30, 0.0,
                                ALU.is_ge, ALU.bypass) if False else None
            i = v.engine_nop()
            chain(i, kb, "ve")

        @b.gpsimd
        def _(g):
            g.wait_ge(kb.s["ve"], kb.val("ve"))
            for tt in range(16):
                g.tensor_scalar(cmb[:, tt, :], rt[:, tt, :], 0.0, None, ALU.is_ge)
            i = g.engine_nop()
            chain(i, kb, "po")

        @b.vector
        def _(v):
            v.wait_ge(kb.s["po"], kb.val("po"))
            for tt in range(16):
                v.scalar_tensor_tensor(rt[:, tt, :], cmb[:, tt, :], -1e30,
                                       lg[:, tt, :], ALU.mult, ALU.add)
            i = v.engine_nop()
            chain(i, kb, "ve")
            v.wait_ge(kb.s["ve"], kb.val("ve"))
            for tt in range(16):
                v.tensor_reduce(rcol[:, tt, 1:2], rt[:, tt, :], axis=AX.X,
                                op=ALU.max, negate=True)
            i = v.engine_nop()
            chain(i, kb, "ve")

        @b.scalar
        def _(s):
            s.wait_ge(kb.s["ve"], kb.val("ve"))
            for tt in range(16):
                s.activation(lg[:, tt, :], lg[:, tt, :], AF.Exp,
                             bias=rcol[:, tt, 0:1], accum_out=rcol[:, tt, 2:3])
            for tt in range(16):
                s.activation(rcol[:, tt, 3:4], rcol[:, tt, 1:2], AF.Exp,
                             scale=cst[:, C_NEGONE:C_NEGONE + 1],
                             bias=rcol[:, tt, 0:1])
            i = s.activation(rcol[:, 0, 0:1], rcol[:, 0, 0:1], AF.Identity) \
                if False else s.copy(cmb[:, 0, 0:1], cmb[:, 0, 0:1])
            chain(i, kb, "ac")

        @b.vector
        def _(v):
            v.wait_ge(kb.s["ac"], kb.val("ac"))
            for tt in range(16):
                v.tensor_scalar(rcol[:, tt, 4:5], rcol[:, tt, 2:3], 1e-20, 1.0,
                                ALU.mult, ALU.add)
            for tt in range(16):
                v.tensor_tensor(rcol[:, tt, 4:5], rcol[:, tt, 4:5],
                                rcol[:, tt, 3:4], ALU.add)
            for tt in range(16):
                v.reciprocal(rcol[:, tt, 5:6], rcol[:, tt, 4:5])
            i = v.engine_nop()
            chain(i, kb, "ve")

        @b.gpsimd
        def _(g):
            g.wait_ge(kb.s["ve"], kb.val("ve"))
            for tt in range(16):
                g.tensor_scalar(rt[:, tt, :], rt[:, tt, :], rcol[:, tt, 1:2],
                                None, ALU.add)
            i = g.engine_nop()
            chain(i, kb, "po")

        @b.vector
        def _(v):
            v.wait_ge(kb.s["po"], kb.val("po"))
            for tt in range(16):
                v.tensor_scalar(rt[:, tt, :], rt[:, tt, :], 0.0, None, ALU.is_ge)
            i = v.engine_nop()
            chain(i, kb, "ve")
            v.wait_ge(kb.s["ve"], kb.val("ve"))
            for tt in range(16):
                v.tensor_tensor(rcol[:, tt, 3:4], rcol[:, tt, 3:4],
                                rcol[:, tt, 5:6], ALU.mult)
            i = v.engine_nop()
            chain(i, kb, "ve")
            v.wait_ge(kb.s["ve"], kb.val("ve"))
            for tt in range(16):
                v.tensor_scalar(cmb[:, tt, :], cmb[:, tt, :], rcol[:, tt, 5:6],
                                None, ALU.mult)
            i = v.engine_nop()
            chain(i, kb, "ve")
            v.wait_ge(kb.s["ve"], kb.val("ve"))
            for tt in range(16):
                v.scalar_tensor_tensor(cmb[:, tt, :], rt[:, tt, :],
                                       rcol[:, tt, 3:4], cmb[:, tt, :],
                                       ALU.mult, ALU.add)
            i = v.engine_nop()
            chain(i, kb, "ve")

    cmbb = kb.alloc("l1fcbb", [128, 16, 8], BF16)
    with nc.Block() as b:
        @b.vector
        def _(v):
            v.wait_ge(kb.s["ve"], kb.val("ve"))
            for tt in range(16):
                ins = v.tensor_copy(cmbb[:, tt, :], cmb[:, tt, :])
            chain(ins, kb, "ve")
    with contextlib.ExitStack() as pc:
        psC = [pc.enter_context(nc.psum_tensor(kb.uname("l1fT"), [8, 128], BF16))
               for _ in range(2)]
        with nc.Block() as b:
            pe_c0 = kb.val("pe")
            ev_c0 = kb.val("evd")

            @b.tensor
            def _(t):
                t.wait_ge(kb.s["ve"], kb.val("ve"))
                for tt in range(16):
                    if tt >= 2:
                        t.wait_ge(kb.s["evd"], ev_c0 + tt - 1)
                    ins = t.transpose(psC[tt % 2][:], cmbb[:, tt, :], kb.sb["ident_bf"][:, :])
                    ins.then_inc(kb.s["pe"], 1)
                    kb.inc("pe")

            @b.vector
            def _(v):
                for tt in range(16):
                    v.wait_ge(kb.s["pe"], pe_c0 + tt + 1)
                    ins = v.tensor_copy(combT[:, tt * 128:(tt + 1) * 128],
                                        psC[tt % 2][:])
                    chain(ins, kb, "evd")

    if _os.environ.get("L1F_DBG"):
        dbg_l = kb.nc.dram_tensor("dbg_lg", [128, 16, 8], F32, kind="ExternalOutput").ap()
        dbg_c = kb.nc.dram_tensor("dbg_cmb", [128, 16, 8], F32, kind="ExternalOutput").ap()
        with nc.Block() as b:
            @b.gpsimd
            def _(g):
                g.wait_ge(kb.s["ve"], kb.val("ve"))
                i = g.dma_start(out=dbg_l, in_=lg[:, :, :])
                dchain(i, kb, "spl")
                i = g.dma_start(out=dbg_c, in_=cmb[:, :, :])
                dchain(i, kb, "spl")
                g.wait_ge(kb.s["spl"], kb.val("spl"))
    kb.free(gate, lg, rt, rcol)

    xn_d = kb.dscratch(kb.uname("l1fx"), [D, T], BF16)
    with nc.Block() as b:
        @b.sync
        def _(sy):
            sy.wait_ge(kb.s["ve"], kb.val("ve"))
            i = sy.dma_start(out=xn_d.rearrange("(c p) t -> p c t", p=128),
                             in_=xn[:, :, :])
            dchain(i, kb, "spl")

        @b.gpsimd
        def _(g):
            g.wait_ge(kb.s["spl"], kb.val("spl"))
            i = g.engine_nop()
            i.then_inc(kb.s["pw"], 1)
            kb.inc("pw")
    kb.free(kb.sb["xn"])

    wbuf = kb.alloc("l1fw", [128, 16, 1024], BF16)
    hbuf = kb.alloc("l1fh", [128, 16, T], BF16)
    bcol = kb.alloc("l1fb", [128, 16], F32)
    cbc = kb.alloc("l1fcb", [128, T], BF16)
    tvb = kb.alloc("l1ftv", [128, 4, 512], F32)
    xnch = kb.alloc("l1fxc", [128, 2, DC, 512], BF16)

    def expert(e):
        w1n, w2n = ("l1f_w1", "l1f_w2") if e < 8 else ("l1f_ws1", "l1f_ws2")
        with nc.Block() as b:
            @b.gpsimd
            def _(g):
                g.wait_ge(kb.s["pw"], kb.val("pw"))
                src = kb.dram[w1n]
                if e < 8:
                    src = src[e]
                wv = wbuf[:, :, :].rearrange("p a m -> p (a m)").rearrange(
                    "p (c m) -> p c m", c=DC, m=2048)
                i = g.dma_start(out=wv, in_=src.rearrange("(c p) m -> p c m", p=128))
                dchain(i, kb, "lw")

            @b.sync
            def _(sy):
                bsrc = kb.dram["l1f_b1"][e] if e < 8 else kb.dram["l1f_bs1"]
                i = sy.dma_start(out=bcol[:, :], in_=bsrc)
                dchain(i, kb, "lq")

        wv = wbuf[:, :, :].rearrange("p a m -> p (a m)").rearrange(
            "p (c m) -> p c m", c=DC, m=2048)

        def evac_h(b, key, ps, pe_t):
            fc, tc = key

            @b.scalar
            def _(s):
                s.wait_ge(kb.s["pe"], pe_t)
                ins = s.activation(hbuf[:, fc, tc * 512:(tc + 1) * 512], ps[:],
                                   AF.Relu, bias=bcol[:, fc:fc + 1])
                chain(ins, kb, "evd")

        for tc in range(4):
            with nc.Block() as b:
                @b.sync
                def _(sy):
                    if tc >= 2 or e > 0:
                        sy.wait_ge(kb.s["pe"], kb.val("pe"))
                    i = sy.dma_start(
                        out=xnch[:, tc % 2],
                        in_=xn_d.rearrange("(c p) t -> p c t", p=128)[
                            :, :, tc * 512:(tc + 1) * 512])
                    dchain(i, kb, "lq")
            tiles_h = [(fc, tc) for fc in range(16)]
            emit_mm(kb, tiles_h,
                    stat_fn=lambda key, kc: wv[:, kc, key[0] * 128:(key[0] + 1) * 128],
                    mov_fn=lambda key, kc, tc=tc: xnch[:, tc % 2][:, kc, :],
                    nk_fn=lambda key: DC, evac_fn=evac_h, tag=f"l1fh{e}_{tc}",
                    waits=[("lw", kb.val("lw")), ("lq", kb.val("lq"))],
                    wait_ln=False)

        with nc.Block() as b:
            @b.gpsimd
            def _(g):
                g.wait_ge(kb.s["evd"], kb.val("evd"))
                src = kb.dram[w2n]
                if e < 8:
                    src = src[e]
                i = g.dma_start(out=wbuf[:, :, 0:1024],
                                in_=src.rearrange("(c p) m -> p c m", p=128))
                dchain(i, kb, "lw")

            @b.sync
            def _(sy):
                sy.wait_ge(kb.s["evd"], kb.val("evd"))
                bsrc = kb.dram["l1f_b2"][e] if e < 8 else kb.dram["l1f_bs2"]
                i = sy.dma_start(out=bcol[:, 0:8], in_=bsrc)
                dchain(i, kb, "lq")

        if e < 8:
            with contextlib.ExitStack() as pc:
                psB = pc.enter_context(nc.psum_tensor(kb.uname(f"l1fB{e}"), [128, 512], F32))
                with nc.Block() as b:
                    pe_b0 = kb.val("pe")
                    ev_b0 = kb.val("evd")

                    @b.tensor
                    def _(t):
                        t.wait_ge(kb.s["ve"], kb.val("ve"))
                        for tc in range(4):
                            if tc >= 1:
                                t.wait_ge(kb.s["evd"], ev_b0 + tc)
                            ins = t.matmul(psB[:], oh8[:, e * 128:(e + 1) * 128],
                                           combT[:, tc * 512:(tc + 1) * 512],
                                           start=True, stop=True)
                            ins.then_inc(kb.s["pe"], 1)
                            kb.inc("pe")

                    @b.vector
                    def _(v):
                        for tc in range(4):
                            v.wait_ge(kb.s["pe"], pe_b0 + tc + 1)
                            ins = v.tensor_copy(cbc[:, tc * 512:(tc + 1) * 512], psB[:])
                            chain(ins, kb, "evd")

        w2v = wbuf[:, :, 0:1024]

        def evac_eo(b, key, ps, pe_t):
            mc, tc = key
            slot = (mc * 4 + tc) % 4

            @b.vector
            def _(v):
                v.wait_ge(kb.s["pe"], pe_t)
                if e < 8:
                    ins = v.scalar_tensor_tensor(tvb[:, slot, :], ps[:],
                                                 bcol[:, mc:mc + 1],
                                                 cbc[:, tc * 512:(tc + 1) * 512],
                                                 ALU.add, ALU.mult)
                else:
                    ins = v.tensor_scalar(tvb[:, slot, :], ps[:], bcol[:, mc:mc + 1],
                                          None, ALU.add)
                chain(ins, kb, "evd")

            @b.gpsimd
            def _(g2):
                g2.wait_ge(kb.s["evd"], kb.val("evd"))
                xs = x[:, mc, tc * 512:(tc + 1) * 512]
                ins = g2.tensor_tensor(xs, xs, tvb[:, slot, :], ALU.add)
                chain(ins, kb, "po")

        tiles_eo = [(mc, tc) for mc in range(DC) for tc in range(4)]
        emit_mm(kb, tiles_eo,
                stat_fn=lambda key, kc: w2v[:, kc, key[0] * 128:(key[0] + 1) * 128],
                mov_fn=lambda key, kc: hbuf[:, kc, key[1] * 512:(key[1] + 1) * 512],
                nk_fn=lambda key: 16, evac_fn=evac_eo, tag=f"l1fe{e}",
                waits=[("lw", kb.val("lw")), ("lq", kb.val("lq"))], wait_ln=False)
        with nc.Block() as b:
            @b.gpsimd
            def _(g):
                g.wait_ge(kb.s["po"], kb.val("po"))
                i = g.engine_nop()
                i.then_inc(kb.s["pw"], 1)
                kb.inc("pw")

    for e in range(9):
        expert(e)
    with nc.Block() as b:
        @b.gpsimd
        def _(g):
            g.wait_ge(kb.s["po"], kb.val("po"))
            i = g.engine_nop()
            i.then_inc(kb.s["xr"], 1)
            kb.inc("xr")
    kb.free(wbuf, hbuf, bcol, cbc, tvb, cmb, cmbb, combT, oh8, xnch)
    kb.sb["xn"] = kb.alloc("xn", [128, DC, T], BF16)



def emit_l0ab(kb):
    nc = kb.nc
    x, xn = kb.sb["x_res"], kb.sb["xn"]
    cst = kb.sb["consts"]
    idn = kb.sb["ident_bf"]
    idnf = kb.sb["ident_f32"]
    onesf = kb.sb["onesrow_f32"]
    a_d = kb.dscratch(kb.uname("l0bo"), [D, T], BF16)

    xn_d = kb.dscratch(kb.uname("l0bx"), [D, T], BF16)
    with nc.Block() as b:
        @b.sync
        def _(sy):
            sy.wait_ge(kb.s["ln"], kb.val("ln"))
            i = sy.dma_start(out=xn_d.rearrange("(c p) t -> p c t", p=128),
                             in_=xn[:, :, :])
            dchain(i, kb, "spl")

        @b.gpsimd
        def _(g):
            g.wait_ge(kb.s["spl"], kb.val("spl"))
            i = g.engine_nop()
            i.then_inc(kb.s["pw"], 1)
            kb.inc("pw")
    kb.free(kb.sb["xn"])
    xnch = kb.alloc("l0bxc", [128, 2, DC, 512], BF16)
    wq = kb.alloc("l0bw", [128, DC, 384], BF16)
    qkv = kb.alloc("l0bqk", [64, 6, T], BF16)
    vctm = kb.alloc("l0bvt", [128, 16, 64], BF16)
    laT = kb.alloc("l0bla", [128, 16, T], BF16)
    t1b = kb.alloc("l0bt1", [128, 2, 128], BF16)
    srow = kb.alloc("l0bsr", [128, T], F32)
    prow = kb.alloc("l0bpr", [128, T], BF16)
    ptb = kb.alloc("l0bpt", [128, 16, 128], BF16)
    cols = kb.alloc("l0bc", [128, 4], F32)
    recb = kb.alloc("l0brc", [64, 128], F32)
    outv = kb.alloc("l0bov", [64, 128], F32)
    rrow = kb.alloc("l0brr", [1, 128], F32)

    for h in range(16):
        with nc.Block() as b:
            @b.gpsimd
            def _(g):
                g.wait_ge(kb.s["pw"], kb.val("pw"))
                i = g.dma_start(out=wq[:, :, :],
                                in_=kb.dram["l0b_wqkv"][h].rearrange(
                                    "(c p) m -> p c m", p=128))
                dchain(i, kb, "lw")

        def evac_q(b, key, ps, pe_t):
            sidx, tc = key

            @b.vector
            def _(v):
                v.wait_ge(kb.s["pe"], pe_t)
                o = qkv[:, sidx, tc * 512:(tc + 1) * 512]
                if sidx in (0, 3):
                    ins = v.tensor_scalar(o, ps[0:64, :], 0.125, None, ALU.mult)
                else:
                    ins = v.tensor_copy(o, ps[0:64, :])
                chain(ins, kb, "evd")

        for tc in range(4):
            with nc.Block() as b:
                @b.sync
                def _(sy):
                    if tc >= 2 or h > 0:
                        sy.wait_ge(kb.s["pe"], kb.val("pe"))
                    i = sy.dma_start(
                        out=xnch[:, tc % 2],
                        in_=xn_d.rearrange("(c p) t -> p c t", p=128)[
                            :, :, tc * 512:(tc + 1) * 512])
                    dchain(i, kb, "lq")
            tiles = [(sidx, tc) for sidx in range(6)]
            emit_mm(kb, tiles,
                    stat_fn=lambda key, kc: wq[:, kc, key[0] * 64:(key[0] + 1) * 64],
                    mov_fn=lambda key, kc, tc=tc: xnch[:, tc % 2][:, kc, :],
                    nk_fn=lambda key: DC, evac_fn=evac_q, tag=f"l0bq{h}_{tc}",
                    waits=[("lw", kb.val("lw")), ("lq", kb.val("lq"))], wait_ln=False,
                    out_rows=64)

        qu = qkv[:, 0]
        ku = qkv[:, 1]
        vu = qkv[:, 2]
        qc = qkv[:, 3]
        kcn = qkv[:, 4]
        vc = qkv[:, 5]

        with contextlib.ExitStack() as pc1:
            psT = pc1.enter_context(nc.psum_tensor(kb.uname("l0bT"), [128, 128], BF16))
            psL = [pc1.enter_context(nc.psum_tensor(kb.uname("l0bL"), [128, 512], F32))
                   for _ in range(2)]

            with nc.Block() as b:
                pe_v0 = kb.val("pe")
                ev_v0 = kb.val("evd")

                @b.tensor
                def _(t):
                    t.wait_ge(kb.s["evd"], ev_v0)
                    for tb in range(16):
                        if tb >= 1:
                            t.wait_ge(kb.s["evd"], ev_v0 + tb)
                        ins = t.transpose(psT[0:128, 0:64],
                                          vc[:, tb * 128:(tb + 1) * 128],
                                          idn[0:64, 0:64])
                        ins.then_inc(kb.s["pe"], 1)
                        kb.inc("pe")

                @b.vector
                def _(v):
                    for tb in range(16):
                        v.wait_ge(kb.s["pe"], pe_v0 + tb + 1)
                        ins = v.tensor_copy(vctm[:, tb, :], psT[:, 0:64])
                        chain(ins, kb, "evd")

            jobs = []
            for J in range(16):
                wk = (J + 1) * 128
                for kc in range((wk + 511) // 512):
                    jobs.append((J, kc, min(512, wk - kc * 512)))
            with nc.Block() as b:
                acs_l0 = kb.val("acs")
                ev_l0 = kb.val("evd")

                @b.tensor
                def _(t):
                    t.wait_ge(kb.s["evd"], ev_l0)
                    for n_mm, (J, kc, wc) in enumerate(jobs):
                        if n_mm >= 2:
                            t.wait_ge(kb.s["acs"], acs_l0 + n_mm - 1)
                        ins = t.matmul(psL[n_mm % 2][:, 0:wc],
                                       ku[:, J * 128:(J + 1) * 128],
                                       qu[:, kc * 512:kc * 512 + wc],
                                       start=True, stop=True)
                        ins.then_inc(kb.s["pe"], 1)
                        kb.inc("pe")

                pe_l0 = kb.val("pe") - len(jobs)

                @b.scalar
                def _(sc_):
                    for n_mm, (J, kc, wc) in enumerate(jobs):
                        sc_.wait_ge(kb.s["pe"], pe_l0 + n_mm + 1)
                        ins = sc_.activation(laT[:, J, kc * 512:kc * 512 + wc],
                                             psL[n_mm % 2][:, 0:wc], AF.Sigmoid)
                        chain(ins, kb, "acs")

            with nc.Block() as b:
                @b.gpsimd
                def _(g):
                    g.wait_ge(kb.s["acs"], kb.val("acs"))
                    for J in range(16):
                        ins = g.tensor_tensor(laT[:, J, J * 128:(J + 1) * 128],
                                              laT[:, J, J * 128:(J + 1) * 128],
                                              kb.sb["smask"][:, :], ALU.mult)
                    chain(ins, kb, "po")

        with contextlib.ExitStack() as pc2:
            psU = [pc2.enter_context(nc.psum_tensor(kb.uname("l0bU"), [128, 512], F32))
                   for _ in range(4)]
            psT2 = [pc2.enter_context(nc.psum_tensor(kb.uname("l0bT2"), [128, 128], BF16))
                    for _ in range(2)]
            psM = [pc2.enter_context(nc.psum_tensor(kb.uname("l0bM"), [128, 128], F32))
                   for _ in range(2)]
            psO = psM[0]
            psR = psM[1]

            for qi in range(16):
                w = (qi + 1) * 128
                nkc = (w + 511) // 512
                nkb = qi + 1
                with nc.Block() as b:
                    pe_s0 = kb.val("pe")
                    ev_s0 = kb.val("evd")
                    pe_t1 = {}

                    @b.tensor
                    def _(t):
                        t.wait_ge(kb.s["po"], kb.val("po"))
                        t.wait_ge(kb.s["acs"], kb.val("acs"))
                        t.wait_ge(kb.s["evd"], ev_s0)
                        for J in range(qi + 1):
                            ins = t.matmul(psM[J % 2][:],
                                           vu[:, J * 128:(J + 1) * 128],
                                           qc[:, qi * 128:(qi + 1) * 128],
                                           start=True, stop=True)
                            ins.then_inc(kb.s["pe"], 1)
                            pe_t1[J] = kb.inc("pe")
                            t.wait_ge(kb.s["evd"], ev_s0 + J + 1)
                            for kc in range(min(J // 4, qi // 4) + 1):
                                wc = min(512, w - kc * 512)
                                wcj = min(wc, (J + 1) * 128 - kc * 512)
                                ins = t.matmul(psU[kc][:, 0:wcj], t1b[:, J % 2, :],
                                               laT[:, J, kc * 512:kc * 512 + wcj],
                                               start=(J == 4 * kc), stop=(J == qi))
                                ins.then_inc(kb.s["pe"], 1)
                                kb.inc("pe")

                    @b.vector
                    def _(v):
                        for J in range(qi + 1):
                            v.wait_ge(kb.s["pe"], pe_t1[J])
                            if J == qi:
                                ins = v.tensor_tensor(t1b[:, J % 2, :], psM[J % 2][:],
                                                      kb.sb["lemask"][:, :], ALU.mult)
                            else:
                                ins = v.tensor_copy(t1b[:, J % 2, :], psM[J % 2][:])
                            chain(ins, kb, "evd")

                    pe_su = kb.val("pe")

                    @b.scalar
                    def _(sc_):
                        sc_.wait_ge(kb.s["pe"], pe_su)
                        for kc in range(nkc):
                            wc = min(512, w - kc * 512)
                            ins = sc_.activation(srow[:, kc * 512:kc * 512 + wc],
                                                 psU[kc][:, 0:wc], AF.Silu)
                        chain(ins, kb, "acs")

                with nc.Block() as b:
                    pe_c0 = kb.val("pe")

                    @b.tensor
                    def _(t):
                        t.wait_ge(kb.s["acs"], kb.val("acs"))
                        for kc in range(nkc):
                            wc = min(512, w - kc * 512)
                            ins = t.matmul(psU[kc][:, 0:wc],
                                           qc[:, qi * 128:(qi + 1) * 128],
                                           kcn[:, kc * 512:kc * 512 + wc],
                                           start=True, stop=True)
                            ins.then_inc(kb.s["pe"], 1)
                            kb.inc("pe")

                    @b.vector
                    def _(v):
                        for kc in range(nkc):
                            wc = min(512, w - kc * 512)
                            v.wait_ge(kb.s["pe"], pe_c0 + kc + 1)
                            sr = srow[:, kc * 512:kc * 512 + wc]
                            ins = v.tensor_tensor(sr, psU[kc][:, 0:wc], sr, ALU.subtract)
                            chain(ins, kb, "evd")

                if _os.environ.get("L0B_DBG") and h == 0 and qi == 1:
                    dbg_s = kb.nc.dram_tensor("dbg_srow", [128, 512], F32,
                                              kind="ExternalOutput").ap()
                    dbg_la = kb.nc.dram_tensor("dbg_la", [128, 2, T], BF16,
                                               kind="ExternalOutput").ap()
                    dbg_t1 = kb.nc.dram_tensor("dbg_t1", [128, 2, 128], BF16,
                                               kind="ExternalOutput").ap()
                    dbg_qk = kb.nc.dram_tensor("dbg_qkv", [64, 6, T], BF16,
                                               kind="ExternalOutput").ap()
                    with nc.Block() as bd:
                        @bd.gpsimd
                        def _(g):
                            g.wait_ge(kb.s["evd"], kb.val("evd"))
                            i = g.dma_start(out=dbg_s, in_=srow[:, 0:512])
                            dchain(i, kb, "spl")
                            i = g.dma_start(out=dbg_la, in_=laT[:, 0:2, :])
                            dchain(i, kb, "spl")
                            i = g.dma_start(out=dbg_t1, in_=t1b[:, :, :])
                            dchain(i, kb, "spl")
                            i = g.dma_start(out=dbg_qk, in_=qkv[:, :, :])
                            dchain(i, kb, "spl")
                            g.wait_ge(kb.s["spl"], kb.val("spl"))

                with nc.Block() as b:
                    @b.gpsimd
                    def _(g):
                        g.wait_ge(kb.s["evd"], kb.val("evd"))
                        ins = g.tensor_tensor(srow[:, (w - 128):w], srow[:, (w - 128):w],
                                              kb.sb["cmask"][:], ALU.add)
                        chain(ins, kb, "po")

                with nc.Block() as b:
                    @b.vector
                    def _(v):
                        v.wait_ge(kb.s["po"], kb.val("po"))
                        ins = v.tensor_reduce(cols[:, 0:1], srow[:, 0:w], axis=AX.X,
                                              op=ALU.max, negate=True)
                        chain(ins, kb, "ve")

                    @b.scalar
                    def _(sc_):
                        sc_.wait_ge(kb.s["ve"], kb.val("ve"))
                        ins = sc_.activation(prow[:, 0:w], srow[:, 0:w], AF.Exp,
                                             bias=cols[:, 0:1], accum_out=cols[:, 1:2])
                        chain(ins, kb, "acs")

                    @b.vector
                    def _(v):
                        v.wait_ge(kb.s["acs"], kb.val("acs"))
                        ins = v.reciprocal(cols[:, 2:3], cols[:, 1:2])
                        chain(ins, kb, "ve")

                    @b.tensor
                    def _(t):
                        t.wait_ge(kb.s["ve"], kb.val("ve"))
                        ins = t.matmul(psR[0:1, :], cols[:, 2:3], idnf[:, :],
                                       start=True, stop=True)
                        chain(ins, kb, "pe")

                    @b.vector
                    def _(v):
                        v.wait_ge(kb.s["pe"], kb.val("pe"))
                        ins = v.tensor_copy(rrow[0:1, :], psR[0:1, :])
                        chain(ins, kb, "ve")

                    @b.tensor
                    def _(t):
                        t.wait_ge(kb.s["ve"], kb.val("ve"))
                        ins = t.matmul(psR[0:64, :], onesf[0:1, 0:64], rrow[0:1, :],
                                       start=True, stop=True)
                        chain(ins, kb, "pe")

                    @b.vector
                    def _(v):
                        v.wait_ge(kb.s["pe"], kb.val("pe"))
                        ins = v.tensor_copy(recb[:, :], psR[0:64, :])
                        chain(ins, kb, "ve")

                    evG0 = kb.val("evd")

                    @b.tensor
                    def _(t):
                        t.wait_ge(kb.s["acs"], kb.val("acs"))
                        for kbi in range(nkb):
                            if kbi >= 2:
                                t.wait_ge(kb.s["evd"], evG0 + kbi - 1)
                            ins = t.transpose(psT2[kbi % 2][:],
                                              prow[:, kbi * 128:(kbi + 1) * 128], idn[:])
                            ins.then_inc(kb.s["pe"], 1)
                            kb.inc("pe")

                    peT = kb.val("pe") - nkb

                    @b.vector
                    def _(v):
                        for kbi in range(nkb):
                            v.wait_ge(kb.s["pe"], peT + kbi + 1)
                            ins = v.tensor_copy(ptb[:, kbi, :], psT2[kbi % 2][:])
                            chain(ins, kb, "evd")

                    @b.tensor
                    def _(t):
                        t.wait_ge(kb.s["evd"], kb.val("evd"))
                        for kbi in range(nkb):
                            ins = t.matmul(psO[0:64, :], vctm[:, kbi, :], ptb[:, kbi, :],
                                           start=(kbi == 0), stop=(kbi == nkb - 1))
                        ins.then_inc(kb.s["pe"], 1)
                        kb.inc("pe")

                    @b.vector
                    def _(v):
                        v.wait_ge(kb.s["pe"], kb.val("pe"))
                        ins = v.tensor_tensor(outv[:, :], psO[0:64, :], recb[:, :], ALU.mult)
                        chain(ins, kb, "evd")

                    @b.gpsimd
                    def _(g):
                        g.wait_ge(kb.s["evd"], kb.val("evd"))
                        ins = g.dma_start(
                            out=a_d[h * 64:(h + 1) * 64, qi * 128:(qi + 1) * 128],
                            in_=outv[:, :])
                        dchain(ins, kb, "spl")

        with nc.Block() as b:
            @b.gpsimd
            def _(g):
                g.wait_ge(kb.s["spl"], kb.val("spl"))
                g.wait_ge(kb.s["evd"], kb.val("evd"))
                i = g.engine_nop()
                i.then_inc(kb.s["pw"], 1)
                kb.inc("pw")

    kb.free(wq, qkv, vctm, laT, t1b, srow, prow, ptb, cols, recb, outv, rrow, xnch)
    emit_out_proj(kb, a_d, "l0b_wout", tag="l0bp")
    kb.sb["xn"] = kb.alloc("xn", [128, DC, T], BF16)



def emit_l3ffn(kb):
    nc = kb.nc
    x, xn = kb.sb["x_res"], kb.sb["xn"]
    cst = kb.sb["consts"]
    idnf = kb.sb["ident_f32"]
    onesf = kb.sb["onesrow_f32"]
    ob128 = kb.sb["ones128_bf"]
    E = 16

    kf = kb.alloc("l3fk", [128, DC, E], F32)
    rwf = kb.alloc("l3fr", [128, DC, E], F32)
    krw = kb.alloc("l3fkr", [128, DC, E], BF16)
    ksq = kb.alloc("l3fks", [128, DC, E], BF16)
    srow = kb.alloc("l3fsr", [1, 64], F32)
    crowb = kb.alloc("l3fcw", [128, E], F32)
    lg = kb.alloc("l3flg", [128, 16, E], F32)
    rt = kb.alloc("l3frt", [128, 16, E], F32)
    cmb = kb.alloc("l3fcm", [128, 16, E], F32)
    rcol = kb.alloc("l3fmc", [128, 16, 6], F32)
    cmbb = kb.alloc("l3fcb", [128, 16, E], BF16)
    combT = kb.alloc("l3fct", [E, T], BF16)
    oh16 = kb.alloc("l3foh", [E, E * 128], BF16)

    with nc.Block() as b:
        @b.sync
        def _(sy):
            i = sy.dma_start(out=kf[:, :, :],
                             in_=kb.dram["l3f_keys"].rearrange("(c p) m -> p c m", p=128))
            dchain(i, kb, "lq")
            i = sy.dma_start(out=rwf[:, :, :],
                             in_=kb.dram["l3f_rw"].rearrange("(c p) m -> p c m", p=128))
            dchain(i, kb, "lq")
            i = sy.dma_start(out=srow[0:1, 0:16], in_=kb.dram["l3f_rb"])
            dchain(i, kb, "lq")

        @b.vector
        def _(v):
            v.memset(oh16[:, :], 1.0)
            v.wait_ge(kb.s["lq"], kb.val("lq"))
            for dc in range(DC):
                v.scalar_tensor_tensor(krw[:, dc, :], kf[:, dc, :], 2.0,
                                       rwf[:, dc, :], ALU.mult, ALU.add)
            i = v.engine_nop()
            chain(i, kb, "ve")

        @b.scalar
        def _(s):
            s.wait_ge(kb.s["lq"], kb.val("lq"))
            for dc in range(DC):
                i = s.activation(ksq[:, dc, :], kf[:, dc, :], AF.Square)
            chain(i, kb, "ac")

        @b.gpsimd
        def _(g):
            g.wait_ge(kb.s["ve"], kb.val("ve"))
            g.affine_select(oh16[:, :], oh16[:, :], pattern=[[1, E * 128]],
                            compare_op=ALU.is_ge, fill=0.0, base=0,
                            channel_multiplier=-128)
            i = g.affine_select(oh16[:, :], oh16[:, :], pattern=[[-1, E * 128]],
                               compare_op=ALU.is_ge, fill=0.0, base=127,
                               channel_multiplier=128)
            chain(i, kb, "po")

    with contextlib.ExitStack() as pc:
        psK = pc.enter_context(nc.psum_tensor(kb.uname("l3fK"), [128, E], F32))
        with nc.Block() as b:
            @b.tensor
            def _(t):
                t.wait_ge(kb.s["ac"], kb.val("ac"))
                for dc in range(DC):
                    i = t.matmul(psK[0:1, :], ob128[:, 0:1], ksq[:, dc, :],
                                 start=(dc == 0), stop=(dc == DC - 1))
                chain(i, kb, "pe")

            @b.vector
            def _(v):
                v.wait_ge(kb.s["pe"], kb.val("pe"))
                ins = v.tensor_copy(srow[0:1, 16:32], psK[0:1, :])
                ins.then_inc(kb.s["ve"], 1)
                kb.inc("ve")
                v.wait_ge(kb.s["ve"], kb.val("ve"))
                ins = v.tensor_tensor(srow[0:1, 32:48], srow[0:1, 0:16],
                                      srow[0:1, 16:32], ALU.subtract)
                chain(ins, kb, "ve")

            @b.tensor
            def _(t):
                t.wait_ge(kb.s["ve"], kb.val("ve"))
                ins = t.matmul(psK[:, :], onesf[0:1, :], srow[0:1, 32:48],
                               start=True, stop=True)
                chain(ins, kb, "pe")

            @b.vector
            def _(v):
                v.wait_ge(kb.s["pe"], kb.val("pe"))
                ins = v.tensor_copy(crowb[:, :], psK[:, :])
                chain(ins, kb, "ve")

    with contextlib.ExitStack() as pc:
        psL = [pc.enter_context(nc.psum_tensor(kb.uname("l3fL"), [128, E], F32))
               for _ in range(4)]
        with nc.Block() as b:
            pe_l0 = kb.val("pe")
            ev_l0 = kb.val("evd")

            @b.tensor
            def _(t):
                t.wait_ge(kb.s["ve"], kb.val("ve"))
                t.wait_ge(kb.s["ln"], kb.val("ln"))
                for tt in range(16):
                    if tt >= 4:
                        t.wait_ge(kb.s["evd"], ev_l0 + tt - 3)
                    for kc in range(DC):
                        ins = t.matmul(psL[tt % 4][:],
                                       xn[:, kc, tt * 128:(tt + 1) * 128],
                                       krw[:, kc, :],
                                       start=(kc == 0), stop=(kc == DC - 1))
                    ins.then_inc(kb.s["pe"], 1)
                    kb.inc("pe")

            @b.vector
            def _(v):
                for tt in range(16):
                    v.wait_ge(kb.s["pe"], pe_l0 + tt + 1)
                    ins = v.tensor_tensor(lg[:, tt, :], psL[tt % 4][:], crowb[:, :],
                                          ALU.add)
                    chain(ins, kb, "evd")

    with nc.Block() as b:
        @b.vector
        def _(v):
            v.wait_ge(kb.s["evd"], kb.val("evd"))
            for tt in range(16):
                v.tensor_copy(rt[:, tt, :], lg[:, tt, :])
            i = v.engine_nop()
            chain(i, kb, "ve")

    for rnd in range(4):
        with nc.Block() as b:
            @b.vector
            def _(v):
                v.wait_ge(kb.s["ve"], kb.val("ve"))
                if rnd > 0:
                    v.wait_ge(kb.s["po"], kb.val("po"))
                for tt in range(16):
                    v.tensor_reduce(rcol[:, tt, 1:2], rt[:, tt, :], axis=AX.X,
                                    op=ALU.max, negate=True)
                i = v.engine_nop()
                chain(i, kb, "ve")
                if rnd == 0:
                    v.wait_ge(kb.s["ve"], kb.val("ve"))
                    for tt in range(16):
                        v.tensor_copy(rcol[:, tt, 0:1], rcol[:, tt, 1:2])
                    i = v.engine_nop()
                    chain(i, kb, "ve")

            if rnd < 3:
                @b.gpsimd
                def _(g):
                    g.wait_ge(kb.s["ve"], kb.val("ve"))
                    for tt in range(16):
                        g.tensor_scalar(cmb[:, tt, :], rt[:, tt, :],
                                        rcol[:, tt, 1:2], None, ALU.add)
                    i = g.engine_nop()
                    chain(i, kb, "po")

                @b.vector
                def _(v):
                    v.wait_ge(kb.s["po"], kb.val("po"))
                    for tt in range(16):
                        v.tensor_scalar(cmb[:, tt, :], cmb[:, tt, :], 0.0, None,
                                        ALU.is_ge)
                    i = v.engine_nop()
                    chain(i, kb, "ve")
                    v.wait_ge(kb.s["ve"], kb.val("ve"))
                    for tt in range(16):
                        v.scalar_tensor_tensor(rt[:, tt, :], cmb[:, tt, :], -1e30,
                                               rt[:, tt, :], ALU.mult, ALU.add)
                    i = v.engine_nop()
                    chain(i, kb, "ve")

    with nc.Block() as b:
        @b.gpsimd
        def _(g):
            g.wait_ge(kb.s["ve"], kb.val("ve"))
            for tt in range(16):
                g.tensor_scalar(cmb[:, tt, :], lg[:, tt, :], rcol[:, tt, 1:2],
                                None, ALU.add)
            i = g.engine_nop()
            chain(i, kb, "po")

        @b.vector
        def _(v):
            v.wait_ge(kb.s["po"], kb.val("po"))
            for tt in range(16):
                v.tensor_scalar(rt[:, tt, :], cmb[:, tt, :], 0.0, None, ALU.is_ge)
            i = v.engine_nop()
            chain(i, kb, "ve")


    with nc.Block() as b:
        @b.scalar
        def _(s):
            s.wait_ge(kb.s["ve"], kb.val("ve"))
            for tt in range(16):
                s.activation(lg[:, tt, :], lg[:, tt, :], AF.Exp,
                             bias=rcol[:, tt, 0:1])
            i = s.copy(rcol[:, 0, 4:5], rcol[:, 0, 4:5])
            chain(i, kb, "ac")

        @b.vector
        def _(v):
            v.wait_ge(kb.s["ac"], kb.val("ac"))
            for tt in range(16):
                v.tensor_tensor(cmb[:, tt, :], lg[:, tt, :], rt[:, tt, :], ALU.mult)
            i = v.engine_nop()
            chain(i, kb, "ve")
            v.wait_ge(kb.s["ve"], kb.val("ve"))
            for tt in range(16):
                v.tensor_reduce(rcol[:, tt, 2:3], cmb[:, tt, :], axis=AX.X, op=ALU.add)
            i = v.engine_nop()
            chain(i, kb, "ve")
            v.wait_ge(kb.s["ve"], kb.val("ve"))
            for tt in range(16):
                v.reciprocal(rcol[:, tt, 3:4], rcol[:, tt, 2:3])
            i = v.engine_nop()
            chain(i, kb, "ve")
            v.wait_ge(kb.s["ve"], kb.val("ve"))
            for tt in range(16):
                v.tensor_scalar(cmbb[:, tt, :], cmb[:, tt, :], rcol[:, tt, 3:4],
                                None, ALU.mult)
            i = v.engine_nop()
            chain(i, kb, "ve")

    with contextlib.ExitStack() as pc:
        psC = [pc.enter_context(nc.psum_tensor(kb.uname("l3fTc"), [E, 128], BF16))
               for _ in range(2)]
        with nc.Block() as b:
            pe_c0 = kb.val("pe")
            ev_c0 = kb.val("evd")

            @b.tensor
            def _(t):
                t.wait_ge(kb.s["ve"], kb.val("ve"))
                for tt in range(16):
                    if tt >= 2:
                        t.wait_ge(kb.s["evd"], ev_c0 + tt - 1)
                    ins = t.transpose(psC[tt % 2][:], cmbb[:, tt, :],
                                      kb.sb["ident_bf"][:, :])
                    ins.then_inc(kb.s["pe"], 1)
                    kb.inc("pe")

            @b.vector
            def _(v):
                for tt in range(16):
                    v.wait_ge(kb.s["pe"], pe_c0 + tt + 1)
                    ins = v.tensor_copy(combT[:, tt * 128:(tt + 1) * 128],
                                        psC[tt % 2][:])
                    chain(ins, kb, "evd")

    kb.free(kf, rwf, krw, ksq, srow, lg, rt, cmb, rcol, cmbb)

    xn_d = kb.dscratch(kb.uname("l3fx"), [D, T], BF16)
    with nc.Block() as b:
        @b.sync
        def _(sy):
            sy.wait_ge(kb.s["ve"], kb.val("ve"))
            i = sy.dma_start(out=xn_d.rearrange("(c p) t -> p c t", p=128),
                             in_=xn[:, :, :])
            dchain(i, kb, "spl")

        @b.gpsimd
        def _(g):
            g.wait_ge(kb.s["spl"], kb.val("spl"))
            i = g.engine_nop()
            i.then_inc(kb.s["pw"], 1)
            kb.inc("pw")
    kb.free(kb.sb["xn"])

    w1b = kb.alloc("l3fw1", [128, DC, 2048], BF16)
    w2b = kb.alloc("l3fw2", [128, 16, 1024], BF16)
    hb = kb.alloc("l3fh", [128, 16, 512], BF16)
    bc1 = kb.alloc("l3fb1", [128, 16], F32)
    bc2 = kb.alloc("l3fb2", [128, 8], F32)
    cbc = kb.alloc("l3fcbc", [128, T], BF16)
    tvb = kb.alloc("l3ftv", [128, 4, 512], F32)
    xnch = kb.alloc("l3fxc", [128, 2, DC, 512], BF16)

    for e in range(E):
        with contextlib.ExitStack() as pc:
            psB = pc.enter_context(nc.psum_tensor(kb.uname(f"l3fB{e}"), [128, 512], F32))
            with nc.Block() as b:
                pe_b0 = kb.val("pe")
                ev_b0 = kb.val("evd")

                @b.tensor
                def _(t):
                    t.wait_ge(kb.s["ve"], kb.val("ve"))
                    t.wait_ge(kb.s["po"], kb.val("po"))
                    for tc in range(4):
                        if tc >= 1:
                            t.wait_ge(kb.s["evd"], ev_b0 + tc)
                        ins = t.matmul(psB[:], oh16[:, e * 128:(e + 1) * 128],
                                       combT[:, tc * 512:(tc + 1) * 512],
                                       start=True, stop=True)
                        ins.then_inc(kb.s["pe"], 1)
                        kb.inc("pe")

                @b.vector
                def _(v):
                    for tc in range(4):
                        v.wait_ge(kb.s["pe"], pe_b0 + tc + 1)
                        ins = v.tensor_copy(cbc[:, tc * 512:(tc + 1) * 512], psB[:])
                        chain(ins, kb, "evd")

        for half in range(2):
            with nc.Block() as b:
                @b.gpsimd
                def _(g):
                    g.wait_ge(kb.s["pw"], kb.val("pw"))
                    i = g.dma_start(
                        out=w1b[:, :, :],
                        in_=kb.dram["l3f_w1"][e].rearrange("(c p) m -> p c m", p=128)[
                            :, :, half * 2048:(half + 1) * 2048])
                    dchain(i, kb, "lw")
                    i = g.dma_start(
                        out=w2b[:, :, :],
                        in_=kb.dram["l3f_w2"][e].rearrange("(c p) m -> p c m", p=128)[
                            :, half * 16:(half + 1) * 16, :])
                    dchain(i, kb, "lw")

                @b.sync
                def _(sy):
                    i = sy.dma_start(out=bc1[:, :],
                                     in_=kb.dram["l3f_b1"][e][:, half * 16:(half + 1) * 16])
                    dchain(i, kb, "lq")
                    if half == 1:
                        i = sy.dma_start(out=bc2[:, :], in_=kb.dram["l3f_b2"][e])
                        dchain(i, kb, "lq")

            with contextlib.ExitStack() as pc:
                psH = [pc.enter_context(nc.psum_tensor(kb.uname(f"l3fH{e}_{half}"),
                                                       [128, 512], F32))
                       for _ in range(4)]
                psE = [pc.enter_context(nc.psum_tensor(kb.uname(f"l3fE{e}_{half}"),
                                                       [128, 512], F32))
                       for _ in range(4)]
                for tc in range(4):
                    with nc.Block() as b:
                        @b.sync
                        def _(sy):
                            sy.wait_ge(kb.s["pe"], kb.val("pe"))
                            i = sy.dma_start(
                                out=xnch[:, tc % 2],
                                in_=xn_d.rearrange("(c p) t -> p c t", p=128)[
                                    :, :, tc * 512:(tc + 1) * 512])
                            dchain(i, kb, "lq")

                    with nc.Block() as b:
                        lq_t = kb.val("lq")
                        pe_h0 = kb.val("pe")
                        acs_h0 = kb.val("acs")

                        @b.tensor
                        def _(t):
                            t.wait_ge(kb.s["lq"], lq_t)
                            t.wait_ge(kb.s["lw"], kb.val("lw"))
                            t.wait_ge(kb.s["pe2"], kb.val("pe2"))
                            for fc in range(16):
                                if fc >= 4:
                                    t.wait_ge(kb.s["acs"], acs_h0 + fc - 3)
                                for kc in range(DC):
                                    ins = t.matmul(psH[fc % 4][:],
                                                   w1b[:, kc, fc * 128:(fc + 1) * 128],
                                                   xnch[:, tc % 2][:, kc, :],
                                                   start=(kc == 0), stop=(kc == DC - 1))
                                ins.then_inc(kb.s["pe"], 1)
                                kb.inc("pe")

                        @b.scalar
                        def _(s):
                            for fc in range(16):
                                s.wait_ge(kb.s["pe"], pe_h0 + fc + 1)
                                ins = s.activation(hb[:, fc, :], psH[fc % 4][:],
                                                   AF.Gelu, bias=bc1[:, half * 0 + fc:
                                                                     half * 0 + fc + 1]
                                                   if False else bc1[:, fc:fc + 1])
                                chain(ins, kb, "acs")

                    with nc.Block() as b:
                        pe_e0 = kb.val("pe2")
                        ev_e0 = kb.val("evd")

                        @b.tensor
                        def _(t):
                            t.wait_ge(kb.s["acs"], kb.val("acs"))
                            for gc in range(DC):
                                if gc >= 4:
                                    t.wait_ge(kb.s["evd"], ev_e0 + gc - 3)
                                for fc in range(16):
                                    ins = t.matmul(psE[gc % 4][:],
                                                   w2b[:, fc, gc * 128:(gc + 1) * 128],
                                                   hb[:, fc, :],
                                                   start=(fc == 0), stop=(fc == 15))
                                ins.then_inc(kb.s["pe2"], 1)
                                kb.inc("pe2")

                        @b.vector
                        def _(v):
                            for gc in range(DC):
                                v.wait_ge(kb.s["pe2"], pe_e0 + gc + 1)
                                if half == 0:
                                    ins = v.tensor_tensor(tvb[:, gc % 4, :], psE[gc % 4][:],
                                                          cbc[:, tc * 512:(tc + 1) * 512],
                                                          ALU.mult)
                                else:
                                    ins = v.scalar_tensor_tensor(
                                        tvb[:, gc % 4, :], psE[gc % 4][:],
                                        bc2[:, gc:gc + 1],
                                        cbc[:, tc * 512:(tc + 1) * 512],
                                        ALU.add, ALU.mult)
                                chain(ins, kb, "evd")

                        ev_p0 = kb.val("evd") - DC

                        @b.gpsimd
                        def _(g):
                            for gc in range(DC):
                                g.wait_ge(kb.s["evd"], ev_p0 + gc + 1)
                                xs = x[:, gc, tc * 512:(tc + 1) * 512]
                                ins = g.tensor_tensor(xs, xs, tvb[:, gc % 4, :], ALU.add)
                                chain(ins, kb, "po")

            with nc.Block() as b:
                @b.gpsimd
                def _(g):
                    g.wait_ge(kb.s["po"], kb.val("po"))
                    i = g.engine_nop()
                    i.then_inc(kb.s["pw"], 1)
                    kb.inc("pw")

    with nc.Block() as b:
        @b.gpsimd
        def _(g):
            g.wait_ge(kb.s["po"], kb.val("po"))
            i = g.engine_nop()
            i.then_inc(kb.s["xr"], 1)
            kb.inc("xr")
    kb.free(w1b, w2b, hb, bc1, bc2, cbc, tvb, xnch, combT, oh16, crowb)
    kb.sb["xn"] = kb.alloc("xn", [128, DC, T], BF16)



def emit_l3aa(kb):
    nc = kb.nc
    x, xn = kb.sb["x_res"], kb.sb["xn"]
    cst = kb.sb["consts"]
    onesf = kb.sb["onesrow_f32"]

    vcs = kb.alloc("l3acv", [128, 64], F32)
    alc = kb.alloc("l3aal", [128, 128], F32)
    with nc.Block() as b:
        @b.sync
        def _(sy):
            i = sy.dma_start(out=vcs[:, 0:32], in_=kb.dram["l3a_cw"])
            dchain(i, kb, "lq")
            i = sy.dma_start(out=vcs[:, 32:40], in_=kb.dram["l3a_cb"])
            dchain(i, kb, "lq")
            i = sy.dma_start(out=vcs[:, 40:48], in_=kb.dram["l3a_dtw"])
            dchain(i, kb, "lq")
            i = sy.dma_start(out=vcs[:, 48:56], in_=kb.dram["l3a_dtb"])
            dchain(i, kb, "lq")
            i = sy.dma_start(out=vcs[:, 56:64], in_=kb.dram["l3a_dp"])
            dchain(i, kb, "lq")
            i = sy.dma_start(out=alc[:, :], in_=kb.dram["l3a_al"])
            dchain(i, kb, "lq")

        @b.scalar
        def _(s):
            s.wait_ge(kb.s["lq"], kb.val("lq"))
            i = s.activation(alc[:, :], alc[:, :], AF.Exp)
            chain(i, kb, "ac")

        @b.vector
        def _(v):
            v.wait_ge(kb.s["ac"], kb.val("ac"))
            i = v.tensor_scalar(alc[:, :], alc[:, :], -1.0, None, ALU.mult)
            chain(i, kb, "ve")

    acc = kb.alloc("l3aac", [128, DC, T], F32)
    with nc.Block() as b:
        @b.vector
        def _(v):
            v.wait_ge(kb.s["ln"], kb.val("ln"))
            v.wait_ge(kb.s["ve"], kb.val("ve"))
            for dc in range(DC):
                v.tensor_scalar(acc[:, dc, :], xn[:, dc, :],
                                vcs[:, dc * 4 + 3:dc * 4 + 4], None, ALU.mult)
            for k in (2, 1, 0):
                for dc in range(DC):
                    sh = 3 - k
                    v.scalar_tensor_tensor(acc[:, dc, sh:T], xn[:, dc, 0:T - sh],
                                           vcs[:, dc * 4 + k:dc * 4 + k + 1],
                                           acc[:, dc, sh:T], ALU.mult, ALU.add)
            i = v.engine_nop()
            chain(i, kb, "ve")

    xc = kb.alloc("l3axc", [128, DC, T], BF16)
    with nc.Block() as b:
        @b.scalar
        def _(s):
            s.wait_ge(kb.s["ve"], kb.val("ve"))
            for dc in range(DC):
                i = s.activation(xc[:, dc, :], acc[:, dc, :], AF.Silu,
                                 bias=vcs[:, 32 + dc:33 + dc])
            chain(i, kb, "ac")
    kb.free(acc)

    xpw = kb.alloc("l3axp", [128, DC, 33], BF16)
    bcselb = kb.alloc("l3abb", [32, T], BF16)
    oh32 = kb.alloc("l3aoh", [32, 32 * 128], BF16)
    prow = kb.alloc("l3apw", [1, T], F32)
    with nc.Block() as b:
        @b.gpsimd
        def _(g):
            i = g.dma_start(out=xpw[:, :, :],
                            in_=kb.dram["l3a_xp"].rearrange("(c p) m -> p c m", p=128))
            dchain(i, kb, "lw")

        @b.vector
        def _(v):
            i = v.memset(oh32[:, :], 1.0)
            chain(i, kb, "ve")

        @b.gpsimd
        def _(g):
            g.wait_ge(kb.s["ve"], kb.val("ve"))
            g.affine_select(oh32[:, :], oh32[:, :], pattern=[[1, 32 * 128]],
                            compare_op=ALU.is_ge, fill=0.0, base=0,
                            channel_multiplier=-128)
            i = g.affine_select(oh32[:, :], oh32[:, :], pattern=[[-1, 32 * 128]],
                               compare_op=ALU.is_ge, fill=0.0, base=127,
                               channel_multiplier=128)
            chain(i, kb, "po")

    with contextlib.ExitStack() as pc:
        psP = pc.enter_context(nc.psum_tensor(kb.uname("l3aP"), [32, 2048], F32))
        psQ = pc.enter_context(nc.psum_tensor(kb.uname("l3aQ"), [1, 2048], F32))
        with nc.Block() as b:
            @b.tensor
            def _(t):
                t.wait_ge(kb.s["ac"], kb.val("ac"))
                t.wait_ge(kb.s["lw"], kb.val("lw"))
                for c0 in range(0, T, 512):
                    for kc in range(DC):
                        t.matmul(psP[:, c0:c0 + 512], xpw[:, kc, 0:32],
                                 xc[:, kc, c0:c0 + 512],
                                 start=(kc == 0), stop=(kc == DC - 1))
                    for kc in range(DC):
                        i = t.matmul(psQ[:, c0:c0 + 512], xpw[:, kc, 32:33],
                                     xc[:, kc, c0:c0 + 512],
                                     start=(kc == 0), stop=(kc == DC - 1))
                chain(i, kb, "pe")

            @b.vector
            def _(v):
                v.wait_ge(kb.s["pe"], kb.val("pe"))
                v.tensor_copy(bcselb[:, :], psP[:, :])
                i = v.tensor_copy(prow[0:1, :], psQ[0:1, :])
                chain(i, kb, "ve")

    dtb_ = kb.alloc("l3adt", [128, T], F32)
    vxb = kb.alloc("l3avx", [128, T], F32)
    ab_ = kb.alloc("l3aab", [128, T], F32)
    ub_ = kb.alloc("l3aub", [128, T], F32)
    hb_ = kb.alloc("l3ahb", [128, T], F32)
    yb_ = kb.alloc("l3ayb", [128, T], F32)
    with contextlib.ExitStack() as pc:
        psB = pc.enter_context(nc.psum_tensor(kb.uname("l3aB"), [128, 2048], F32))
        psC = pc.enter_context(nc.psum_tensor(kb.uname("l3aC"), [128, 2048], F32))
        for dc in range(DC):
            with nc.Block() as b:
                @b.tensor
                def _(t):
                    t.wait_ge(kb.s["ve"], kb.val("ve"))
                    for c0 in range(0, T, 512):
                        i = t.matmul(psB[:, c0:c0 + 512], onesf[0:1, :],
                                     prow[0:1, c0:c0 + 512], start=True, stop=True)
                    chain(i, kb, "pe")

                @b.vector
                def _(v):
                    v.wait_ge(kb.s["pe"], kb.val("pe"))
                    ins = v.tensor_scalar(dtb_[:, :], psB[:, :],
                                          vcs[:, 40 + dc:41 + dc],
                                          vcs[:, 48 + dc:49 + dc], ALU.mult, ALU.add)
                    chain(ins, kb, "ve")

                @b.scalar
                def _(s):
                    s.wait_ge(kb.s["ve"], kb.val("ve"))
                    ins = s.activation(ab_[:, :], dtb_[:, :], AF.Abs)
                    ins.then_inc(kb.s["ac"], 1)
                    kb.inc("ac")
                    s.wait_ge(kb.s["ac"], kb.val("ac"))
                    ins = s.activation(ab_[:, :], ab_[:, :], AF.Exp,
                                       scale=cst[:, C_NEGONE:C_NEGONE + 1])
                    ins.then_inc(kb.s["ac"], 1)
                    kb.inc("ac")
                    s.wait_ge(kb.s["ac"], kb.val("ac"))
                    ins = s.activation(ab_[:, :], ab_[:, :], AF.Ln,
                                       bias=cst[:, C_ONE:C_ONE + 1])
                    chain(ins, kb, "ac")

                @b.gpsimd
                def _(g):
                    g.wait_ge(kb.s["ve"], kb.val("ve"))
                    ins = g.tensor_scalar_max(ub_[:, :], dtb_[:, :], 0.0)
                    chain(ins, kb, "po")

                @b.vector
                def _(v):
                    v.wait_ge(kb.s["ac"], kb.val("ac"))
                    v.wait_ge(kb.s["po"], kb.val("po"))
                    ins = v.tensor_tensor(dtb_[:, :], ab_[:, :], ub_[:, :], ALU.add)
                    ins.then_inc(kb.s["ve"], 1)
                    kb.inc("ve")
                    v.wait_ge(kb.s["ve"], kb.val("ve"))
                    v.tensor_tensor(vxb[:, :], dtb_[:, :], xc[:, dc, :], ALU.mult)
                    ins = v.memset(yb_[:, :], 0.0)
                    chain(ins, kb, "ve")

            for n in range(16):
                with nc.Block() as b:
                    @b.tensor
                    def _(t):
                        t.wait_ge(kb.s["ve"], kb.val("ve"))
                        t.wait_ge(kb.s["po"], kb.val("po"))
                        for c0 in range(0, T, 512):
                            t.matmul(psB[:, c0:c0 + 512],
                                     oh32[:, n * 128:(n + 1) * 128],
                                     bcselb[:, c0:c0 + 512], start=True, stop=True)
                        for c0 in range(0, T, 512):
                            i = t.matmul(psC[:, c0:c0 + 512],
                                         oh32[:, (16 + n) * 128:(17 + n) * 128],
                                         bcselb[:, c0:c0 + 512], start=True, stop=True)
                        chain(i, kb, "pe")

                    @b.scalar
                    def _(s):
                        s.wait_ge(kb.s["ve"], kb.val("ve"))
                        ins = s.activation(ab_[:, :], dtb_[:, :], AF.Exp,
                                           scale=alc[:, dc * 16 + n:dc * 16 + n + 1])
                        chain(ins, kb, "acs")

                    @b.vector
                    def _(v):
                        v.wait_ge(kb.s["pe"], kb.val("pe"))
                        ins = v.tensor_tensor(ub_[:, :], vxb[:, :], psB[:, :], ALU.mult)
                        ins.then_inc(kb.s["ve"], 1)
                        kb.inc("ve")
                        v.wait_ge(kb.s["ve"], kb.val("ve"))
                        v.wait_ge(kb.s["acs"], kb.val("acs"))
                        ins = v.tensor_tensor_scan(hb_[:, :], ab_[:, :], ub_[:, :],
                                                   0.0, ALU.mult, ALU.add)
                        ins.then_inc(kb.s["ve"], 1)
                        kb.inc("ve")
                        v.wait_ge(kb.s["ve"], kb.val("ve"))
                        ins = v.tensor_tensor(hb_[:, :], hb_[:, :], psC[:, :], ALU.mult)
                        ins.then_inc(kb.s["ve"], 1)
                        kb.inc("ve")

                    @b.gpsimd
                    def _(g):
                        g.wait_ge(kb.s["ve"], kb.val("ve"))
                        ins = g.tensor_tensor(yb_[:, :], yb_[:, :], hb_[:, :], ALU.add)
                        chain(ins, kb, "po")

            with nc.Block() as b:
                @b.vector
                def _(v):
                    v.wait_ge(kb.s["po"], kb.val("po"))
                    ins = v.tensor_tensor(x[:, dc, :], x[:, dc, :], yb_[:, :], ALU.add)
                    ins.then_inc(kb.s["ve"], 1)
                    kb.inc("ve")
                    v.wait_ge(kb.s["ve"], kb.val("ve"))
                    ins = v.scalar_tensor_tensor(x[:, dc, :], xn[:, dc, :],
                                                 vcs[:, 56 + dc:57 + dc],
                                                 x[:, dc, :], ALU.mult, ALU.add)
                    chain(ins, kb, "ve")

    with nc.Block() as b:
        @b.gpsimd
        def _(g):
            g.wait_ge(kb.s["ve"], kb.val("ve"))
            i = g.engine_nop()
            i.then_inc(kb.s["xr"], 1)
            kb.inc("xr")
    kb.free(vcs, alc, xc, xpw, bcselb, oh32, prow, dtb_, vxb, ab_, ub_, hb_, yb_)



def emit_l1ab(kb):
    nc = kb.nc
    x, xn = kb.sb["x_res"], kb.sb["xn"]
    cst = kb.sb["consts"]
    idn = kb.sb["ident_bf"]
    onesf = kb.sb["onesrow_f32"]

    brow = kb.alloc("l1bbr", [1, 2 * D], F32)
    rels = kb.alloc("l1brl", [1, 16], F32)
    bcols = kb.alloc("l1bbc", [128, 16], F32)
    erow = kb.alloc("l1ber", [1, 128], F32)
    bqb = kb.alloc("l1bqb", [128, D], BF16)
    bkb = kb.alloc("l1bkb", [128, D], BF16)
    rbb = kb.alloc("l1brb", [128, 8, 128], BF16)
    wbuf = kb.alloc("l1bw", [128, DC, 1024], BF16)

    with nc.Block() as b:
        @b.sync
        def _(sy):
            i = sy.dma_start(out=brow[0:1, 0:D], in_=kb.dram["l1b_bq"])
            dchain(i, kb, "lq")
            i = sy.dma_start(out=brow[0:1, D:2 * D], in_=kb.dram["l1b_bk"])
            dchain(i, kb, "lq")
            i = sy.dma_start(out=rels[0:1, 0:15],
                             in_=kb.dram["l1b_rel"][0:1, :])
            dchain(i, kb, "lq")
            i = sy.dma_start(out=bcols[:, :], in_=kb.dram["l1b_bvo"])
            dchain(i, kb, "lq")

        @b.vector
        def _(v):
            i = v.memset(erow[0:1, :], 0.125)
            chain(i, kb, "ve")

    with contextlib.ExitStack() as pc:
        psB = pc.enter_context(nc.psum_tensor(kb.uname("l1bB"), [128, 512], F32))
        with nc.Block() as b:
            pe0 = kb.val("pe")
            ev0 = kb.val("evd")
            jobs = [(bqb, 0, erow), (bkb, D, onesf)]

            @b.tensor
            def _(t):
                t.wait_ge(kb.s["lq"], kb.val("lq"))
                t.wait_ge(kb.s["ve"], kb.val("ve"))
                nj = 0
                for dst, off, lrow in jobs:
                    for c0 in range(0, D, 512):
                        if nj >= 1:
                            t.wait_ge(kb.s["evd"], ev0 + nj)
                        ins = t.matmul(psB[:], lrow[0:1, :],
                                       brow[0:1, off + c0:off + c0 + 512],
                                       start=True, stop=True)
                        ins.then_inc(kb.s["pe"], 1)
                        kb.inc("pe")
                        nj += 1

            @b.vector
            def _(v):
                nj = 0
                for dst, off, lrow in jobs:
                    for c0 in range(0, D, 512):
                        v.wait_ge(kb.s["pe"], pe0 + nj + 1)
                        ins = v.tensor_copy(dst[:, c0:c0 + 512], psB[:])
                        chain(ins, kb, "evd")
                        nj += 1

        with nc.Block() as b:
            @b.vector
            def _(v):
                v.wait_ge(kb.s["evd"], kb.val("evd"))
                for bp in range(8):
                    src = rels[0:1, 7 - bp:15 - bp].unsqueeze(1) \
                        .broadcast_to([1, 16, 8])
                    ins = v.tensor_copy(brow[0:1, bp * 128:(bp + 1) * 128]
                                        .rearrange("p (g k) -> p g k", g=16, k=8), src)
                chain(ins, kb, "ve")

        with nc.Block() as b:
            pe0 = kb.val("pe")
            evr0 = kb.val("evd")

            @b.tensor
            def _(t):
                t.wait_ge(kb.s["ve"], kb.val("ve"))
                for bp in range(8):
                    if bp >= 1:
                        t.wait_ge(kb.s["evd"], evr0 + bp)
                    ins = t.matmul(psB[:, 0:128], erow[0:1, :],
                                   brow[0:1, bp * 128:(bp + 1) * 128],
                                   start=True, stop=True)
                    ins.then_inc(kb.s["pe"], 1)
                    kb.inc("pe")

            @b.vector
            def _(v):
                for bp in range(8):
                    v.wait_ge(kb.s["pe"], pe0 + bp + 1)
                    ins = v.tensor_copy(rbb[:, bp, :], psB[:, 0:128])
                    chain(ins, kb, "evd")

    def tm_proj(wname, dst, bias_b, scale):
        with nc.Block() as b:
            @b.gpsimd
            def _(g):
                g.wait_ge(kb.s["pw"], kb.val("pw"))
                i = g.dma_start(out=wbuf[:, :, :],
                                in_=kb.dram[wname].rearrange("(c p) m -> p c m", p=128))
                dchain(i, kb, "lw")

        def evac(b, key, ps, pe_t):
            tt, mh = key

            @b.vector
            def _(v):
                v.wait_ge(kb.s["pe"], pe_t)
                o = dst[:, tt, mh * 512:(mh + 1) * 512]
                ins = v.scalar_tensor_tensor(o, ps[:], scale,
                                             bias_b[:, mh * 512:(mh + 1) * 512],
                                             ALU.mult, ALU.add)
                chain(ins, kb, "evd")

        tiles = [(tt, mh) for tt in range(16) for mh in range(2)]
        emit_mm(kb, tiles,
                stat_fn=lambda key, kc: xn[:, kc, key[0] * 128:(key[0] + 1) * 128],
                mov_fn=lambda key, kc: wbuf[:, kc, key[1] * 512:(key[1] + 1) * 512],
                nk_fn=lambda key: DC, evac_fn=evac, tag="l1btm",
                waits=[("lw", kb.val("lw"))])
        with nc.Block() as b:
            @b.gpsimd
            def _(g):
                g.wait_ge(kb.s["evd"], kb.val("evd"))
                i = g.engine_nop()
                i.then_inc(kb.s["pw"], 1)
                kb.inc("pw")

    v_d = kb.dscratch(kb.uname("l1bvd"), [D, T], BF16)
    vf = kb.alloc("l1bvf", [128, DC, T], BF16)
    with nc.Block() as b:
        @b.gpsimd
        def _(g):
            g.wait_ge(kb.s["pw"], kb.val("pw"))
            i = g.dma_start(out=wbuf[:, :, :],
                            in_=kb.dram["l1b_wv"].rearrange("(c p) m -> p c m", p=128))
            dchain(i, kb, "lw")

    def evac_v(b, key, ps, pe_t):
        mc, tc = key

        @b.vector
        def _(v):
            v.wait_ge(kb.s["pe"], pe_t)
            ins = v.tensor_scalar(vf[:, mc, tc * 512:(tc + 1) * 512], ps[:],
                                  bcols[:, mc:mc + 1], None, ALU.add)
            chain(ins, kb, "evd")

    tiles = [(mc, tc) for mc in range(DC) for tc in range(4)]
    emit_mm(kb, tiles,
            stat_fn=lambda key, kc: wbuf[:, kc, key[0] * 128:(key[0] + 1) * 128],
            mov_fn=lambda key, kc: xn[:, kc, key[1] * 512:(key[1] + 1) * 512],
            nk_fn=lambda key: DC, evac_fn=evac_v, tag="l1bv",
            waits=[("lw", kb.val("lw"))])
    with nc.Block() as b:
        @b.sync
        def _(sy):
            sy.wait_ge(kb.s["evd"], kb.val("evd"))
            i = sy.dma_start(out=v_d.rearrange("(c p) t -> p c t", p=128),
                             in_=vf[:, :, :])
            dchain(i, kb, "spl")

        @b.gpsimd
        def _(g):
            g.wait_ge(kb.s["spl"], kb.val("spl"))
            i = g.engine_nop()
            i.then_inc(kb.s["pw"], 1)
            kb.inc("pw")
    kb.free(vf)

    qtm = kb.alloc("l1bqt", [128, 16, 1024], BF16)
    ktm = kb.alloc("l1bkt", [128, 16, 1024], BF16)
    tm_proj("l1b_wq", qtm, bqb, 0.125)
    tm_proj("l1b_wk", ktm, bkb, 1.0)
    kb.free(kb.sb["xn"], brow, rels, erow, bqb, bkb)

    sc = kb.alloc("l1bsc", [128, 2, 8, 128], F32)
    mmb = kb.alloc("l1bmm", [128, 1024], BF16)
    red = kb.alloc("l1brd", [128, 2, 8, 16], F32)
    with nc.Block() as b:
        @b.vector
        def _(v):
            v.wait_ge(kb.s["evd"], kb.val("evd"))
            for u in range(2):
                for bp in range(8):
                    for bb in range(8):
                        v.tensor_tensor(mmb[:, :], qtm[:, 2 * bp + u, :],
                                        ktm[:, 2 * bb + u, :], ALU.mult)
                        v.tensor_reduce(
                            sc[:, u, bp, bb:bb + 1 + 15 * 8:8].rearrange(
                                "p (g o) -> p g o", g=16, o=1) if False else
                            sc[:, u][:, bp][:, bb::8].unsqueeze(2),
                            mmb[:, :].rearrange("p (g k) -> p g k", g=16, k=64),
                            axis=AX.X, op=ALU.add)
            i = v.engine_nop()
            chain(i, kb, "ve")

        @b.gpsimd
        def _(g):
            g.wait_ge(kb.s["ve"], kb.val("ve"))
            for u in range(2):
                for bp in range(8):
                    ins = g.tensor_tensor(sc[:, u][:, bp], sc[:, u][:, bp],
                                          rbb[:, bp, :], ALU.add)
            chain(ins, kb, "po")

    pb = kb.alloc("l1bpb", [128, 2, 8, 128], BF16)
    with nc.Block() as b:
        @b.vector
        def _(v):
            v.wait_ge(kb.s["po"], kb.val("po"))
            for u in range(2):
                for bp in range(8):
                    v.tensor_reduce(red[:, u, bp, :].unsqueeze(2),
                                    sc[:, u][:, bp].rearrange("p (g k) -> p g k",
                                                              g=16, k=8),
                                    axis=AX.X, op=ALU.max)
            i = v.engine_nop()
            chain(i, kb, "ve")
            v.wait_ge(kb.s["ve"], kb.val("ve"))
            for u in range(2):
                for bp in range(8):
                    mxb = red[:, u, bp, :].unsqueeze(2) \
                        .broadcast_to([128, 16, 8])
                    v.tensor_tensor(sc[:, u][:, bp].rearrange("p (g k) -> p g k",
                                                              g=16, k=8),
                                    sc[:, u][:, bp].rearrange("p (g k) -> p g k",
                                                              g=16, k=8),
                                    mxb, ALU.subtract)
            i = v.engine_nop()
            chain(i, kb, "ve")

        @b.scalar
        def _(s):
            s.wait_ge(kb.s["ve"], kb.val("ve"))
            for u in range(2):
                for bp in range(8):
                    i = s.activation(sc[:, u][:, bp], sc[:, u][:, bp], AF.Exp)
            chain(i, kb, "ac")

        @b.vector
        def _(v):
            v.wait_ge(kb.s["ac"], kb.val("ac"))
            for u in range(2):
                for bp in range(8):
                    v.tensor_reduce(red[:, u, bp, :].unsqueeze(2),
                                    sc[:, u][:, bp].rearrange("p (g k) -> p g k",
                                                              g=16, k=8),
                                    axis=AX.X, op=ALU.add)
            i = v.engine_nop()
            chain(i, kb, "ve")
            v.wait_ge(kb.s["ve"], kb.val("ve"))
            for u in range(2):
                for bp in range(8):
                    v.reciprocal(red[:, u, bp, :], red[:, u, bp, :])
            i = v.engine_nop()
            chain(i, kb, "ve")
            v.wait_ge(kb.s["ve"], kb.val("ve"))
            for u in range(2):
                for bp in range(8):
                    rzb = red[:, u, bp, :].unsqueeze(2) \
                        .broadcast_to([128, 16, 8])
                    v.tensor_tensor(pb[:, u][:, bp].rearrange("p (g k) -> p g k",
                                                              g=16, k=8),
                                    sc[:, u][:, bp].rearrange("p (g k) -> p g k",
                                                              g=16, k=8),
                                    rzb, ALU.mult)
            i = v.engine_nop()
            chain(i, kb, "ve")
    kb.free(sc, mmb, red, qtm, ktm)

    ptb = kb.alloc("l1bpt", [128, 2, 8, 128], BF16)
    sel2 = kb.alloc("l1bs2", [128, 64, 128], BF16)
    outa = kb.alloc("l1boa", [128, DC, T], F32)
    w64 = kb.alloc("l1bw64", [128, 2, 256], BF16)
    idnbf = kb.sb["ident_bf"]
    with nc.Block() as b:
        @b.vector
        def _(v):
            for dc in range(DC):
                for bb in range(8):
                    j = dc * 8 + bb
                    v.tensor_copy(sel2[:, j, 0:64],
                                  idnbf[:, 16 * dc + bb:16 * dc + bb + 1]
                                  .broadcast_to([128, 64]))
                    v.tensor_copy(sel2[:, j, 64:128],
                                  idnbf[:, 16 * dc + 8 + bb:16 * dc + 8 + bb + 1]
                                  .broadcast_to([128, 64]))
            i = v.engine_nop()
            chain(i, kb, "po")

        @b.vector
        def _(v):
            v.wait_ge(kb.s["evd"], kb.val("evd"))
            for dc in range(DC):
                i = v.memset(outa[:, dc, :], 0.0)
            chain(i, kb, "ve")

    with contextlib.ExitStack() as pc:
        psT = [pc.enter_context(nc.psum_tensor(kb.uname("l1bT"), [128, 128], BF16))
               for _ in range(2)]
        psW = [pc.enter_context(nc.psum_tensor(kb.uname("l1bW"), [128, 256], F32))
               for _ in range(2)]
        with nc.Block() as b:
            pe0 = kb.val("pe")
            ev0 = kb.val("evd")
            pairs = [(u, bp) for u in range(2) for bp in range(8)]

            @b.tensor
            def _(t):
                t.wait_ge(kb.s["ve"], kb.val("ve"))
                for i_, (u, bp) in enumerate(pairs):
                    if i_ >= 2:
                        t.wait_ge(kb.s["evd"], ev0 + i_ - 1)
                    ins = t.transpose(psT[i_ % 2][:], pb[:, u][:, bp], idn[:])
                    ins.then_inc(kb.s["pe"], 1)
                    kb.inc("pe")

            @b.vector
            def _(v):
                for i_, (u, bp) in enumerate(pairs):
                    v.wait_ge(kb.s["pe"], pe0 + i_ + 1)
                    ins = v.tensor_copy(ptb[:, u][:, bp], psT[i_ % 2][:])
                    chain(ins, kb, "evd")

        vch = kb.alloc("l1bvc", [128, DC, 256], BF16)
        for bb in range(8):
            with nc.Block() as b:
                @b.sync
                def _(sy):
                    sy.wait_ge(kb.s["po"], kb.val("po"))
                    i = sy.dma_start(
                        out=vch[:, :, :],
                        in_=v_d.rearrange("(c p) t -> p c t", p=128)[
                            :, :, bb * 256:(bb + 1) * 256])
                    dchain(i, kb, "lq")
            lq_t = kb.val("lq")
            for bp in range(8):
                for dc in range(DC):
                    with nc.Block() as b:
                        pe0 = kb.val("pe")

                        @b.tensor
                        def _(t):
                            t.wait_ge(kb.s["lq"], lq_t)
                            t.wait_ge(kb.s["evd"], kb.val("evd"))
                            t.wait_ge(kb.s["po"], kb.val("po"))
                            for u in range(2):
                                ins = t.matmul(psW[dc % 2][:, u * 128:(u + 1) * 128],
                                               sel2[:, dc * 8 + bb, :],
                                               ptb[:, u][:, bp],
                                               start=True, stop=True)
                                ins.then_inc(kb.s["pe"], 1)
                                kb.inc("pe")

                        @b.vector
                        def _(v):
                            v.wait_ge(kb.s["pe"], pe0 + 2)
                            ins = v.tensor_copy(w64[:, dc % 2, :], psW[dc % 2][:])
                            chain(ins, kb, "evd")

                        @b.gpsimd
                        def _(g):
                            g.wait_ge(kb.s["evd"], kb.val("evd"))
                            o = outa[:, dc, bp * 256:(bp + 1) * 256]
                            g.tensor_tensor(w64[:, dc % 2, :], w64[:, dc % 2, :],
                                            vch[:, dc, :], ALU.mult)
                            i = g.engine_nop()
                            i.then_inc(kb.s["po"], 1)
                            kb.inc("po")
                            g.wait_ge(kb.s["po"], kb.val("po"))
                            ins = g.tensor_tensor(o, o, w64[:, dc % 2, :], ALU.add)
                            chain(ins, kb, "po")
        kb.free(vch)

    kb.free(pb, ptb, sel2, w64, rbb)

    outb = kb.alloc("l1bob", [128, DC, T], BF16)
    with nc.Block() as b:
        @b.vector
        def _(v):
            v.wait_ge(kb.s["po"], kb.val("po"))
            for dc in range(DC):
                i = v.tensor_copy(outb[:, dc, :], outa[:, dc, :])
            chain(i, kb, "ve")

        @b.gpsimd
        def _(g):
            g.wait_ge(kb.s["pw"], kb.val("pw"))
            i = g.dma_start(out=wbuf[:, :, :],
                            in_=kb.dram["l1b_wo"].rearrange("(c p) m -> p c m", p=128))
            dchain(i, kb, "lw")

    def evac_o(b, key, ps, pe_t):
        mc, tc = key

        @b.vector
        def _(v):
            v.wait_ge(kb.s["pe"], pe_t)
            xs = x[:, mc, tc * 512:(tc + 1) * 512]
            ins = v.scalar_tensor_tensor(xs, ps[:], bcols[:, 8 + mc:9 + mc], xs,
                                         ALU.add, ALU.add)
            chain(ins, kb, "evd")

    tiles = [(mc, tc) for mc in range(DC) for tc in range(4)]
    emit_mm(kb, tiles,
            stat_fn=lambda key, kc: wbuf[:, kc, key[0] * 128:(key[0] + 1) * 128],
            mov_fn=lambda key, kc: outb[:, kc, key[1] * 512:(key[1] + 1) * 512],
            nk_fn=lambda key: DC, evac_fn=evac_o, tag="l1bo",
            waits=[("lw", kb.val("lw")), ("ve", kb.val("ve"))], wait_ln=False)
    with nc.Block() as b:
        @b.gpsimd
        def _(g):
            g.wait_ge(kb.s["evd"], kb.val("evd"))
            i = g.engine_nop()
            i.then_inc(kb.s["xr"], 1)
            kb.inc("xr")
    kb.free(outa, outb, wbuf, bcols)
    kb.sb["xn"] = kb.alloc("xn", [128, DC, T], BF16)


def build_program(subs):
    nc = bass.Bass("TRN2", target_bir_lowering=False)
    kb = KB(nc)
    x_in = kb.din("x_in", [D, T])
    x_out = nc.dram_tensor("x_out", [D, T], F32, kind="ExternalOutput").ap()
    wdecl = {}
    for s in subs:
        for name, shape in SUB_WEIGHTS[s]:
            wdecl[name] = shape
    for name, shape in wdecl.items():
        kb.din(name, shape)

    with kb.ctx:
        kb.sbuf("x_res", [128, DC, T], F32)
        kb.sbuf("consts", [128, NCONST], F32)
        nc.const_aps.register(F32, -1e30, kb.sb["consts"][:, C_NEGBIG:C_NEGBIG + 1])
        kb.sbuf("ones128_bf", [128, 1], BF16)
        kb.sbuf("onesrow_f32", [1, 128], F32)
        kb.sbuf("negrow_f32", [1, 128], F32)
        kb.sbuf("onesrow_bf", [1, 128], BF16)
        kb.sbuf("ident_bf", [128, 128], BF16)
        kb.sbuf("ident_f32", [128, 128], F32)
        kb.sbuf("cmask", [128, 128], F32)
        kb.sbuf("smask", [128, 128], BF16)
        kb.sbuf("lemask", [128, 128], BF16)
        kb.init_arena(140 * 1024)
        kb.sb["xn"] = kb.alloc("xn", [128, DC, T], BF16)
        for s in ("xr", "ln", "ve", "ac", "pe", "po", "lw", "evd", "st", "lx",
                  "pw", "spl", "lq", "acs", "hcd", "snk", "pe2"):
            kb.sem(s)

        with nc.Block() as b:
            @b.vector
            def _(v):
                for j, val in enumerate(CONST_VALS):
                    v.memset(kb.sb["consts"][:, j:j + 1], val)
                v.memset(kb.sb["ones128_bf"][:], 1.0)
                v.memset(kb.sb["onesrow_f32"][:], 1.0)
                v.memset(kb.sb["onesrow_bf"][:], 1.0)
                v.memset(kb.sb["ident_bf"][:], 1.0)
                v.memset(kb.sb["ident_f32"][:], 1.0)
                v.memset(kb.sb["cmask"][:], -1e30)
                v.memset(kb.sb["smask"][:], 1.0)
                v.memset(kb.sb["lemask"][:], 1.0)
                i = v.memset(kb.sb["negrow_f32"][:], -1.0)
                chain(i, kb, "ve")

            @b.gpsimd
            def _(g):
                g.wait_ge(kb.s["ve"], kb.val("ve"))
                g.affine_select(kb.sb["ident_bf"][:], kb.sb["ident_bf"][:],
                                pattern=[[-1, 128]], compare_op=ALU.is_equal,
                                fill=0.0, base=0, channel_multiplier=1)
                g.affine_select(kb.sb["ident_f32"][:], kb.sb["ident_f32"][:],
                                pattern=[[-1, 128]], compare_op=ALU.is_equal,
                                fill=0.0, base=0, channel_multiplier=1)
                g.affine_select(kb.sb["cmask"][:], kb.sb["cmask"][:],
                                pattern=[[1, 128]], compare_op=ALU.is_ge,
                                fill=0.0, base=-1, channel_multiplier=-1)
                g.affine_select(kb.sb["smask"][:], kb.sb["smask"][:],
                                pattern=[[-1, 128]], compare_op=ALU.is_gt,
                                fill=0.0, base=0, channel_multiplier=1)
                i = g.affine_select(kb.sb["lemask"][:], kb.sb["lemask"][:],
                                    pattern=[[1, 128]], compare_op=ALU.is_ge,
                                    fill=0.0, base=0, channel_multiplier=-1)
                chain(i, kb, "po")

            @b.sync
            def _(sy):
                i = sy.dma_start(out=kb.sb["x_res"][:, :, :],
                                 in_=x_in.rearrange("(c p) t -> p c t", p=128))
                dchain(i, kb, "lx")

            @b.gpsimd
            def _(g):
                g.wait_ge(kb.s["lx"], kb.val("lx"))
                g.wait_ge(kb.s["po"], kb.val("po"))
                i = g.engine_nop()
                i.then_inc(kb.s["xr"], 1)
                kb.inc("xr")

        for s in subs:
            if s == "l0aa":
                kb.sb["mean_b"] = kb.alloc("meanb", [128, T], F32)
                emit_ln(kb, mean_b=kb.sb["mean_b"], exact_mean=True)
            else:
                emit_ln(kb)
            SUB_EMITTERS[s](kb)

        with nc.Block() as b:
            @b.sync
            def _(sy):
                sy.wait_ge(kb.s["xr"], kb.val("xr"))
                i = sy.dma_start(out=x_out.rearrange("(c p) t -> p c t", p=128),
                                 in_=kb.sb["x_res"][:, :, :])
                dchain(i, kb, "st")

            @b.gpsimd
            def _(g):
                g.wait_ge(kb.s["st"], kb.val("st"))

    return nc


SUB_EMITTERS = {
    "l0ffn": emit_l0ffn,
    "l2aa": emit_l2aa,
    "l1aa": emit_l1aa,
    "l0aa": emit_l0aa,
    "l2ffn": emit_l2ffn,
    "l2ab": emit_l2ab,
    "l3ab": emit_l3ab,
    "l1ffn": emit_l1ffn,
    "l0ab": emit_l0ab,
    "l3ffn": emit_l3ffn,
    "l3aa": emit_l3aa,
    "l1ab": emit_l1ab,
}

SUB_WEIGHTS = {
    "l0ffn": [("l0f_wT", (D, D)), ("l0f_sc", (128, 8)), ("l0f_bi", (128, 8))],
    "l2aa": [("l2a_wq", (D, D)), ("l2a_wk", (D, D)), ("l2a_wv", (D, D)),
             ("l2a_wo", (D, D))],
    "l1aa": [("l1a_wq", (D, D)), ("l1a_wk", (D, D)), ("l1a_wv", (D, D)),
             ("l1a_wo", (D, D)), ("l1a_ob", (128, 8))],
    "l0aa": [("l0a_b", (128, 24))],
    "l2ffn": [("l2f_wr", (D, D)), ("l2f_wk", (D, D)), ("l2f_wv", (D, D)),
              ("l2f_mu", (128, 16))],
    "l2ab": [("l2b_wq", (D, D)), ("l2b_wk", (D, D)), ("l2b_wv", (D, D)),
             ("l2b_wo", (D, D)), ("l2b_vec", (128, 56)), ("l2b_bo", (128, 8))],
    "l3ab": [("l3b_wq", (D, 2 * D)), ("l3b_wk", (D, 2 * D)), ("l3b_wv", (D, 2 * D)),
             ("l3b_wo", (2 * D, D)), ("l3b_lam", (8, 512)), ("l3b_rms", (128, 2))],
    "l1ffn": [("l1f_gate", (D, 8)), ("l1f_w1", (8, D, 2048)), ("l1f_b1", (8, 128, 16)),
              ("l1f_w2", (8, 2048, D)), ("l1f_b2", (8, 128, 8)),
              ("l1f_ws1", (D, 2048)), ("l1f_bs1", (128, 16)),
              ("l1f_ws2", (2048, D)), ("l1f_bs2", (128, 8))],
    "l0ab": [("l0b_wqkv", (16, D, 384)), ("l0b_wout", (D, D))],
    "l3ffn": [("l3f_keys", (D, 16)), ("l3f_rw", (D, 16)), ("l3f_rb", (1, 16)),
              ("l3f_w1", (16, D, 4096)), ("l3f_b1", (16, 128, 32)),
              ("l3f_w2", (16, 4096, D)), ("l3f_b2", (16, 128, 8))],
    "l3aa": [("l3a_cw", (128, 32)), ("l3a_cb", (128, 8)), ("l3a_dtw", (128, 8)),
             ("l3a_dtb", (128, 8)), ("l3a_dp", (128, 8)), ("l3a_al", (128, 128)),
             ("l3a_xp", (D, 33))],
    "l1ab": [("l1b_wq", (D, D)), ("l1b_wk", (D, D)), ("l1b_wv", (D, D)),
             ("l1b_wo", (D, D)), ("l1b_bq", (1, D)), ("l1b_bk", (1, D)),
             ("l1b_bvo", (128, 16)), ("l1b_rel", (1, 15))],
}


def col128(v):
    return np.ascontiguousarray(np.asarray(v, np.float32).reshape(DC, 128).T)


def col128x(v, n):
    return np.ascontiguousarray(np.asarray(v, np.float32).reshape(n, 128).T)


def prep_weights(inp):
    def A(x):
        return np.ascontiguousarray(np.asarray(x, np.float32))
    w = {}
    w["l0f_wT"] = A(np.asarray(inp["L0_ffn_weights"], np.float32).T)
    w["l0f_sc"] = col128(inp["L0_ffn_scale"])
    w["l0f_bi"] = col128(inp["L0_ffn_bias"])
    w["l2a_wq"] = A(np.asarray(inp["L2_attn_a_W_q"], np.float32).transpose(1, 0, 2).reshape(D, D))
    w["l2a_wk"] = A(np.asarray(inp["L2_attn_a_W_k"], np.float32).transpose(1, 0, 2).reshape(D, D))
    w["l2a_wv"] = A(np.asarray(inp["L2_attn_a_W_v"], np.float32).transpose(1, 0, 2).reshape(D, D))
    w["l2a_wo"] = A(inp["L2_attn_a_W_o"])
    w["l1a_wq"] = A(np.asarray(inp["L1_attn_a_q_w"], np.float32).T)
    w["l1a_wk"] = A(np.asarray(inp["L1_attn_a_k_w"], np.float32).T)
    w["l1a_wv"] = A(np.asarray(inp["L1_attn_a_v_w"], np.float32).T)
    w["l1a_wo"] = A(np.asarray(inp["L1_attn_a_out_w"], np.float32).T)
    w["l1a_ob"] = col128(inp["L1_attn_a_out_b"])
    w["l0a_b"] = np.concatenate([col128(inp["L0_attn_a_bv_q"]),
                                 col128(inp["L0_attn_a_bv_k"]),
                                 col128(inp["L0_attn_a_bv_v"])], axis=1)
    w["l2f_wr"] = A(np.asarray(inp["L2_ffn_W_r"], np.float32).T)
    w["l2f_wk"] = A(np.asarray(inp["L2_ffn_W_k"], np.float32).T)
    w["l2f_wv"] = A(np.asarray(inp["L2_ffn_W_v"], np.float32).T)
    w["l2f_mu"] = np.concatenate([col128(inp["L2_ffn_mu_r"]),
                                  col128(inp["L2_ffn_mu_k"])], axis=1)
    w["l2b_wq"] = A(inp["L2_attn_b_Wq"])
    w["l2b_wk"] = A(inp["L2_attn_b_Wk"])
    w["l2b_wv"] = A(inp["L2_attn_b_Wv"])
    w["l2b_wo"] = A(inp["L2_attn_b_Wo"])

    def rep64(vv):
        return col128(np.repeat(np.asarray(vv, np.float32), 64))
    w["l2b_vec"] = np.concatenate(
        [col128(inp["L2_attn_b_bq"]), col128(inp["L2_attn_b_bk"]),
         col128(inp["L2_attn_b_bv"]), col128(inp["L2_attn_b_bo"]),
         rep64(inp["L2_attn_b_w0"]), rep64(inp["L2_attn_b_w1"]),
         rep64(inp["L2_attn_b_wK"])], axis=1)
    w["l2b_bo"] = col128(inp["L2_attn_b_bo"])
    w["l3b_wq"] = A(inp["L3_attn_b_W_q"])
    w["l3b_wk"] = A(inp["L3_attn_b_W_k"])
    w["l3b_wv"] = A(inp["L3_attn_b_W_v"])
    w["l3b_wo"] = A(inp["L3_attn_b_W_o"])
    w["l3b_lam"] = np.concatenate(
        [A(inp["L3_attn_b_lambda_q1"]), A(inp["L3_attn_b_lambda_k1"]),
         A(inp["L3_attn_b_lambda_q2"]), A(inp["L3_attn_b_lambda_k2"])], axis=1)
    w["l3b_rms"] = np.ascontiguousarray(
        np.asarray(inp["L3_attn_b_rms_scale"], np.float32).reshape(2, 128).T)
    w["l1f_gate"] = A(np.asarray(inp["L1_ffn_gate_w"], np.float32).T)
    w["l1f_w1"] = A(np.asarray(inp["L1_ffn_w1"], np.float32).transpose(0, 2, 1))
    w["l1f_b1"] = A(np.asarray(inp["L1_ffn_b1"], np.float32).reshape(8, 16, 128)
                    .transpose(0, 2, 1))
    w["l1f_w2"] = A(np.asarray(inp["L1_ffn_w2"], np.float32).transpose(0, 2, 1))
    w["l1f_b2"] = A(np.asarray(inp["L1_ffn_b2"], np.float32).reshape(8, 8, 128)
                    .transpose(0, 2, 1))
    w["l1f_ws1"] = A(np.asarray(inp["L1_ffn_ws1"], np.float32).T)
    w["l1f_bs1"] = col128x(inp["L1_ffn_bs1"], 16)
    w["l1f_ws2"] = A(np.asarray(inp["L1_ffn_ws2"], np.float32).T)
    w["l1f_bs2"] = col128(inp["L1_ffn_bs2"])
    wqkv = np.asarray(inp["L0_attn_b_W_qkv"], np.float32).reshape(D, 6, 16, 64)
    w["l0b_wqkv"] = A(wqkv.transpose(2, 0, 1, 3).reshape(16, D, 384))
    w["l0b_wout"] = A(inp["L0_attn_b_W_out"])
    w["l3f_keys"] = A(np.asarray(inp["L3_ffn_keys"], np.float32).T)
    w["l3f_rw"] = A(np.asarray(inp["L3_ffn_rw"], np.float32).T)
    w["l3f_rb"] = A(np.asarray(inp["L3_ffn_rb"], np.float32).reshape(1, 16))
    w["l3f_w1"] = A(np.asarray(inp["L3_ffn_w1"], np.float32).transpose(0, 2, 1))
    w["l3f_b1"] = A(np.asarray(inp["L3_ffn_b1"], np.float32).reshape(16, 32, 128)
                    .transpose(0, 2, 1))
    w["l3f_w2"] = A(np.asarray(inp["L3_ffn_w2"], np.float32).transpose(0, 2, 1))
    w["l3f_b2"] = A(np.asarray(inp["L3_ffn_b2"], np.float32).reshape(16, 8, 128)
                    .transpose(0, 2, 1))
    w["l3a_cw"] = A(np.asarray(inp["L3_attn_a_conv_w"], np.float32).reshape(8, 128, 4)
                    .transpose(1, 0, 2).reshape(128, 32))
    w["l3a_cb"] = col128(inp["L3_attn_a_conv_b"])
    w["l3a_dtw"] = col128(np.asarray(inp["L3_attn_a_dt_w"], np.float32).reshape(1024))
    w["l3a_dtb"] = col128(inp["L3_attn_a_dt_b"])
    w["l3a_dp"] = col128(inp["L3_attn_a_Dp"])
    w["l3a_al"] = A(np.asarray(inp["L3_attn_a_a_log"], np.float32).reshape(8, 128, 16)
                    .transpose(1, 0, 2).reshape(128, 128))
    w["l3a_xp"] = A(np.asarray(inp["L3_attn_a_xproj_w"], np.float32).T)
    w["l1b_wq"] = A(inp["L1_attn_b_Wq"])
    w["l1b_wk"] = A(inp["L1_attn_b_Wk"])
    w["l1b_wv"] = A(inp["L1_attn_b_Wv"])
    w["l1b_wo"] = A(inp["L1_attn_b_Wo"])
    w["l1b_bq"] = A(np.asarray(inp["L1_attn_b_bq"], np.float32).reshape(1, D))
    w["l1b_bk"] = A(np.asarray(inp["L1_attn_b_bk"], np.float32).reshape(1, D))
    w["l1b_bvo"] = np.concatenate([col128(inp["L1_attn_b_bv"]),
                                   col128(inp["L1_attn_b_bo"])], axis=1)
    w["l1b_rel"] = A(np.asarray(inp["L1_attn_b_rel"], np.float32)[2040:2055]
                     .reshape(1, 15))
    return w


def shard_x(x, mode):
    if mode == "batch":
        return [np.ascontiguousarray(x[c].T) for c in range(NCORE)]
    return [np.ascontiguousarray(
        x[:, 256 * c:256 * (c + 1), :].reshape(8 * 256, 1024).T)
        for c in range(NCORE)]


def unshard_x(outs, mode):
    if mode == "batch":
        return np.ascontiguousarray(np.stack([o.T for o in outs]))
    x = np.empty((8, 2048, 1024), np.float32)
    for c in range(NCORE):
        x[:, 256 * c:256 * (c + 1), :] = outs[c].T.reshape(8, 256, 1024)
    return x


_PROG_CACHE = {}


def run_chain(x, subs, weights, shard="batch"):
    key = tuple(subs)
    if key not in _PROG_CACHE:
        _PROG_CACHE[key] = build_program(subs)
    nc = _PROG_CACHE[key]
    need = set(n for s in subs for (n, _) in SUB_WEIGHTS[s])
    shards = shard_x(x, shard)
    in_maps = []
    for c in range(NCORE):
        m = {"x_in": shards[c]}
        for n in need:
            m[n] = weights[n]
        in_maps.append(m)
    res = run_bass_kernel_spmd(nc, in_maps, core_ids=list(range(NCORE)))
    return unshard_x([np.asarray(r["x_out"]) for r in res.results], shard)


GROUPS = [
    (["l0aa", "l0ffn", "l0ab", "l0ffn", "l1aa", "l1ffn"], "batch"),
    (["l1ab"], "time"),
    (["l1ffn", "l2aa", "l2ffn", "l2ab", "l2ffn", "l3aa", "l3ffn", "l3ab",
      "l3ffn"], "batch"),
]


def kernel(**inputs):
    x = np.asarray(inputs["L0_attn_a_x"], np.float32)
    w = prep_weights(inputs)
    for subs, mode in GROUPS:
        x = run_chain(x, subs, w, shard=mode)
    return x


# version 02
# speedup vs baseline: 1.1650x; 1.1650x over previous
import contextlib
import numpy as np
import ml_dtypes
import concourse.bass as bass
import concourse.mybir as mybir
from concourse.bass_utils import run_bass_kernel_spmd

F32 = mybir.dt.float32
BF16 = mybir.dt.bfloat16
U8 = mybir.dt.uint8
AF = mybir.ActivationFunctionType
ALU = mybir.AluOpType
AX = mybir.AxisListType

D = 1024
T = 2048
DC = 8
NCORE = 8
BF = ml_dtypes.bfloat16

C_EPS, C_NEGHALF, C_ONE, C_NEGONE, C_EIGHTH, C_1_256, C_NEGBIG = range(7)
CONST_VALS = [1e-5, -0.5, 1.0, -1.0, 0.125, 1.0 / 256.0, -1e30]
NCONST = len(CONST_VALS)


class KB:
    def __init__(self, nc):
        self.nc = nc
        self.ctx = contextlib.ExitStack()
        self.s = {}
        self.cnt = {}
        self.sb = {}
        self.dram = {}
        self.uid = 0
        self._cms = {}

    def sem(self, name):
        if name not in self.s:
            self.s[name] = self.ctx.enter_context(self.nc.semaphore(name))
            self.cnt[name] = 0
        return self.s[name]

    def inc(self, name, n=1):
        self.sem(name)
        self.cnt[name] += n
        return self.cnt[name]

    def val(self, name):
        self.sem(name)
        return self.cnt[name]

    def sbuf(self, name, shape, dt):
        t = self.ctx.enter_context(self.nc.sbuf_tensor(name, shape, dt))
        self.sb[name] = t
        return t

    def init_arena(self, nbytes):
        self.arena = self.sbuf("arena", [128, nbytes], U8)
        self.arena_n = nbytes
        self.holes = [(0, nbytes)]
        self.allocs = {}

    def alloc(self, pfx, shape, dt):
        assert shape[0] <= 128
        esz = {F32: 4, BF16: 2, U8: 1, mybir.dt.int32: 4}[dt]
        inner = 1
        for d in shape[1:]:
            inner *= d
        need = inner * esz
        need_al = (need + 63) & ~63
        for idx, (off, size) in enumerate(self.holes):
            if size >= need_al:
                self.holes[idx] = (off + need_al, size - need_al)
                if self.holes[idx][1] == 0:
                    self.holes.pop(idx)
                ap = self.arena[:, off:off + need]
                if dt != U8:
                    ap = ap.bitcast(dt)
                if len(shape) == 2:
                    ap = ap.rearrange("p (a) -> p a")
                elif len(shape) == 3:
                    ap = ap.rearrange("p (a b) -> p a b", a=shape[1], b=shape[2])
                elif len(shape) == 4:
                    ap = ap.rearrange("p (a b c) -> p a b c", a=shape[1], b=shape[2], c=shape[3])
                if shape[0] < 128:
                    ap = ap[0:shape[0]]
                self.allocs[id(ap)] = (off, need_al)
                return ap
        raise MemoryError(f"arena OOM for {pfx} {shape} ({need_al}B); holes={self.holes}")

    def free(self, *ts):
        for t in ts:
            off, size = self.allocs.pop(id(t))
            self.holes.append((off, size))
        self.holes.sort()
        merged = []
        for off, size in self.holes:
            if merged and merged[-1][0] + merged[-1][1] == off:
                merged[-1] = (merged[-1][0], merged[-1][1] + size)
            else:
                merged.append((off, size))
        self.holes = [tuple(h) for h in merged]

    def din(self, name, shape, dt=F32):
        ap = self.nc.dram_tensor(name, list(shape), dt, kind="ExternalInput").ap()
        self.dram[name] = ap
        return ap

    def dscratch(self, name, shape, dt=F32):
        ap = self.nc.dram_tensor(name, list(shape), dt, kind="Internal").ap()
        self.dram[name] = ap
        return ap

    def uname(self, pfx):
        self.uid += 1
        return f"{pfx}{self.uid}"


def chain(inst, kb, name, n=1):
    inst.then_inc(kb.s[name], n)
    return kb.inc(name, n)


def dchain(inst, kb, name):
    return chain(inst, kb, name, 16)


def emit_ln(kb, mean_b=None, exact_mean=False):
    nc = kb.nc
    x, xn = kb.sb["x_res"], kb.sb["xn"]
    rows = kb.alloc("lnrw", [1, 4 * T], F32)
    cst = kb.sb["consts"]
    mean = rows[0:1, 0:T]
    work = rows[0:1, T:2 * T]
    rstd = rows[0:1, 2 * T:3 * T]
    nmr = rows[0:1, 3 * T:4 * T]
    xr0 = kb.val("xr")
    sc = kb.alloc("lnsc", [128, DC, T], BF16)
    nmb = kb.alloc("lnnm", [128, T], BF16)

    with contextlib.ExitStack() as pc:
        ps = pc.enter_context(nc.psum_tensor(kb.uname("lnps"), [128, 2 * T], F32))

        with nc.Block() as b:
            @b.vector
            def _(v):
                v.wait_ge(kb.s["xr"], xr0)
                for dc in range(DC):
                    i = v.tensor_copy(sc[:, dc, :], x[:, dc, :])
                chain(i, kb, "ve")

            @b.tensor
            def _(t):
                t.wait_ge(kb.s["ve"], kb.val("ve"))
                ob = kb.sb["ones128_bf"]
                for dc in range(DC):
                    for c0 in range(0, T, 512):
                        i = t.matmul(ps[0:1, c0:c0 + 512], ob[:, 0:1],
                                     sc[:, dc, c0:c0 + 512],
                                     start=(dc == 0), stop=(dc == DC - 1))
                chain(i, kb, "pe")


            @b.scalar
            def _(s):
                s.wait_ge(kb.s["pe"], kb.val("pe"))
                for dc in range(DC):
                    i = s.activation(sc[:, dc, :], x[:, dc, :], AF.Square)
                chain(i, kb, "ac")

            @b.tensor
            def _(t):
                t.wait_ge(kb.s["ac"], kb.val("ac"))
                ob = kb.sb["ones128_bf"]
                for dc in range(DC):
                    for c0 in range(0, T, 512):
                        i = t.matmul(ps[0:1, T + c0:T + c0 + 512], ob[:, 0:1],
                                     sc[:, dc, c0:c0 + 512],
                                     start=(dc == 0), stop=(dc == DC - 1))
                chain(i, kb, "pe")

        if exact_mean:
            MAGIC = 12582912.0
            scf = sc.rearrange("p a b -> p (a b)").bitcast(F32)
            qt = [scf[:, 0:T], scf[:, 2 * T:3 * T], scf[:, T:2 * T]]
            rb_ = scf[:, T:4 * T].rearrange("p (a b) -> p a b", a=3, b=T)
            of_ = kb.sb["onesrow_f32"]
            S1, S2, S3 = mean, work, rstd
            SCL = [4096.0, 2.0 ** 25, 2.0 ** 38]
            with nc.Block() as bq:
                @bq.vector
                def _(v):
                    v.wait_ge(kb.s["pe"], kb.val("pe"))
                    v.memset(S1, 0.0)
                    v.memset(S2, 0.0)
                    i = v.memset(S3, 0.0)
                    chain(i, kb, "ve")

            for dc in range(DC):
                with nc.Block() as bq:
                    @bq.vector
                    def _(v):
                        v.wait_ge(kb.s["ve"], kb.val("ve"))
                        v.wait_ge(kb.s["evd"], kb.val("evd"))
                        v.wait_ge(kb.s["pe"], kb.val("pe"))

                        def edge(ins):
                            ins.then_inc(kb.s["ve"], 1)
                            kb.inc("ve")
                            v.wait_ge(kb.s["ve"], kb.val("ve"))

                        srcs = [x[:, dc, :], rb_[:, 0, :], rb_[:, 2, :]]
                        for b_ in range(3):
                            edge(v.tensor_scalar(qt[b_], srcs[b_], SCL[b_],
                                                 MAGIC, ALU.mult, ALU.add))
                            edge(v.tensor_scalar(qt[b_], qt[b_], -MAGIC,
                                                 1.0 / SCL[b_], ALU.add, ALU.mult))
                            if b_ < 2:
                                edge(v.tensor_tensor(rb_[:, 2 * b_, :], srcs[b_],
                                                     qt[b_], ALU.subtract))
                        i = v.engine_nop()
                        chain(i, kb, "ve")

                    ev_bq = kb.val("evd")

                    @bq.tensor
                    def _(t):
                        t.wait_ge(kb.s["ve"], kb.val("ve"))
                        for b_ in range(3):
                            if b_ >= 1:
                                t.wait_ge(kb.s["evd"], ev_bq + b_)
                            for c0 in range(0, T, 512):
                                i = t.matmul(ps[0:1, c0:c0 + 512],
                                             kb.sb["onescol_f32"][:, 0:1],
                                             qt[b_][:, c0:c0 + 512],
                                             start=True, stop=True)
                            i.then_inc(kb.s["pe"], 1)
                            kb.inc("pe")

                    pe_bb = kb.val("pe") - 3

                    @bq.vector
                    def _(v):
                        for b_, Sb in enumerate((S1, S2, S3)):
                            v.wait_ge(kb.s["pe"], pe_bb + b_ + 1)
                            ins = v.scalar_tensor_tensor(Sb, ps[0:1, 0:T], 1.0, Sb,
                                                         ALU.mult, ALU.add)
                            chain(ins, kb, "evd")

            with nc.Block() as bq:
                @bq.vector
                def _(v):
                    v.wait_ge(kb.s["ve"], kb.val("ve"))
                    ins = v.tensor_tensor(S1, S1, S2, ALU.add)
                    ins.then_inc(kb.s["ve"], 1)
                    kb.inc("ve")
                    v.wait_ge(kb.s["ve"], kb.val("ve"))
                    ins = v.tensor_tensor(S1, S1, S3, ALU.add)
                    ins.then_inc(kb.s["ve"], 1)
                    kb.inc("ve")
                    v.wait_ge(kb.s["ve"], kb.val("ve"))
                    ins = v.tensor_scalar(rows[0:1, 3 * T:4 * T], S1, 1.0 / 1024.0,
                                          None, ALU.mult)
                    chain(ins, kb, "ve")

        with nc.Block() as b:
            @b.vector
            def _(v):
                v.wait_ge(kb.s["pe"], kb.val("pe"))
                if exact_mean:
                    i = v.tensor_scalar(mean, rows[0:1, 3 * T:4 * T], 1.0, None,
                                        ALU.mult)
                else:
                    i = v.tensor_scalar(mean, ps[0:1, 0:T], 1.0 / D, None, ALU.mult)
                chain(i, kb, "ve")

            @b.scalar
            def _(s):
                s.wait_ge(kb.s["ve"], kb.val("ve"))
                i = s.activation(work, mean, AF.Square)
                chain(i, kb, "ac")

            @b.vector
            def _(v):
                v.wait_ge(kb.s["ac"], kb.val("ac"))
                i = v.scalar_tensor_tensor(work, ps[0:1, T:2 * T], 1.0 / D, work,
                                           ALU.mult, ALU.subtract)
                chain(i, kb, "ve")

            @b.scalar
            def _(s):
                s.wait_ge(kb.s["ve"], kb.val("ve"))
                i = s.activation(work, work, AF.Ln, bias=cst[0:1, C_EPS:C_EPS + 1])
                i.then_inc(kb.s["ac"], 1)
                kb.inc("ac")
                s.wait_ge(kb.s["ac"], kb.val("ac"))
                i = s.activation(rstd, work, AF.Exp,
                                 scale=cst[0:1, C_NEGHALF:C_NEGHALF + 1])
                chain(i, kb, "ac")

            @b.vector
            def _(v):
                v.wait_ge(kb.s["ac"], kb.val("ac"))
                i = v.tensor_tensor(nmr, mean, rstd, ALU.mult)
                chain(i, kb, "ve")

            @b.tensor
            def _(t):
                t.wait_ge(kb.s["ve"], kb.val("ve"))
                orow, nrow = kb.sb["onesrow_f32"], kb.sb["negrow_f32"]
                for c0 in range(0, T, 512):
                    t.matmul(ps[:, c0:c0 + 512], orow[0:1, :],
                             rstd[0:1, c0:c0 + 512], start=True, stop=True)
                for c0 in range(0, T, 512):
                    i = t.matmul(ps[:, T + c0:T + c0 + 512], nrow[0:1, :],
                                 nmr[0:1, c0:c0 + 512], start=True, stop=True)
                chain(i, kb, "pe")

        with nc.Block() as b:
            @b.vector
            def _(v):
                v.wait_ge(kb.s["pe"], kb.val("pe"))
                for dc in range(DC):
                    v.tensor_tensor(sc[:, dc, :], x[:, dc, :], ps[:, 0:T], ALU.mult)
                i = v.tensor_copy(nmb[:, :], ps[:, T:2 * T])
                chain(i, kb, "ve")

            @b.gpsimd
            def _(g):
                g.wait_ge(kb.s["ve"], kb.val("ve"))
                for dc in range(DC):
                    i = g.tensor_tensor(xn[:, dc, :], sc[:, dc, :], nmb[:, :], ALU.add)
                chain(i, kb, "ln")

        if mean_b is not None:
            with nc.Block() as b:
                @b.tensor
                def _(t):
                    t.wait_ge(kb.s["ve"], kb.val("ve"))
                    orow = kb.sb["onesrow_f32"]
                    for c0 in range(0, T, 512):
                        i = t.matmul(ps[:, c0:c0 + 512], orow[0:1, :],
                                     mean[0:1, c0:c0 + 512], start=True, stop=True)
                    chain(i, kb, "pe")

                @b.vector
                def _(v):
                    v.wait_ge(kb.s["pe"], kb.val("pe"))
                    i = v.tensor_copy(mean_b[:, :], ps[:, 0:T])
                    chain(i, kb, "ve")
    kb.free(sc, nmb, rows)


def emit_mm(kb, tiles, stat_fn, mov_fn, nk_fn, evac_fn, tag="mm", banks=4,
            tcols=512, waits=(), wait_ln=True, out_rows=128):
    nc = kb.nc
    with contextlib.ExitStack() as pc:
        pss = [pc.enter_context(nc.psum_tensor(kb.uname(tag), [128, tcols], F32))
               for _ in range(banks)]
        pe0 = kb.val("pe")
        ev0 = kb.val("evd")
        with nc.Block() as b:
            @b.tensor
            def _(t):
                if wait_ln:
                    t.wait_ge(kb.s["ln"], kb.val("ln"))
                for wn, wv in waits:
                    t.wait_ge(kb.s[wn], wv)
                for i, key in enumerate(tiles):
                    if i >= banks:
                        t.wait_ge(kb.s["evd"], ev0 + (i - banks) + 1)
                    ps = pss[i % banks]
                    nk = nk_fn(key)
                    for kc in range(nk):
                        ins = t.matmul(ps[0:out_rows, :], stat_fn(key, kc), mov_fn(key, kc),
                                       start=(kc == 0), stop=(kc == nk - 1))
                    ins.then_inc(kb.s["pe"], 1)
                    kb.inc("pe")

            for i, key in enumerate(tiles):
                evac_fn(b, key, pss[i % banks], pe0 + i + 1)
        assert kb.val("evd") == ev0 + len(tiles), "evac must inc evd once per tile"


def emit_l0ffn(kb):
    nc = kb.nc
    x, xn = kb.sb["x_res"], kb.sb["xn"]
    wsg = kb.alloc("l0fw", [128, DC, 1024], BF16)
    sc = kb.alloc("l0fv", [128, 16], F32)
    tbuf = kb.alloc("l0ft", [128, 4, 512], F32)

    with nc.Block() as b:
        @b.gpsimd
        def _(g):
            g.wait_ge(kb.s["xr"], kb.val("xr"))
            i = g.dma_start(out=wsg[:, :, :],
                            in_=kb.dram["l0f_wT"].rearrange("(c p) m -> p c m", p=128))
            dchain(i, kb, "lw")

        @b.sync
        def _(sy):
            i = sy.dma_start(out=sc[:, 0:8], in_=kb.dram["l0f_sc"])
            dchain(i, kb, "lw")
            i = sy.dma_start(out=sc[:, 8:16], in_=kb.dram["l0f_bi"])
            dchain(i, kb, "lw")

        @b.scalar
        def _(s):
            s.wait_ge(kb.s["lw"], kb.val("lw"))
            for dc in range(DC):
                i = s.activation(wsg[:, dc, :], wsg[:, dc, :], AF.Sign)
            chain(i, kb, "ac")

    tiles = [(mc, tc) for mc in range(DC) for tc in range(4)]
    po0 = kb.val("po")

    def evac(b, key, ps, pe_t):
        mc, tc = key
        i_t = tiles.index(key)
        tb = tbuf[:, i_t % 4, :]

        @b.vector
        def _(v):
            v.wait_ge(kb.s["pe"], pe_t)
            if i_t >= 4:
                v.wait_ge(kb.s["po"], po0 + (i_t - 4) + 1)
            i = v.tensor_scalar(tb, ps[:], sc[:, mc:mc + 1], sc[:, 8 + mc:9 + mc],
                                ALU.mult, ALU.add)
            chain(i, kb, "evd")

        @b.gpsimd
        def _(g):
            g.wait_ge(kb.s["evd"], kb.val("evd"))
            i = g.tensor_scalar_min(tb, tb, 100.0)
            chain(i, kb, "po")

        @b.vector
        def _(v):
            v.wait_ge(kb.s["po"], kb.val("po"))
            xs = x[:, mc, tc * 512:(tc + 1) * 512]
            i = v.scalar_tensor_tensor(xs, tb, -100.0, xs, ALU.max, ALU.add)
            chain(i, kb, "xr")

    emit_mm(kb, tiles,
            stat_fn=lambda key, kc: wsg[:, kc, key[0] * 128:(key[0] + 1) * 128],
            mov_fn=lambda key, kc: xn[:, kc, key[1] * 512:(key[1] + 1) * 512],
            nk_fn=lambda key: DC,
            evac_fn=evac, tag="l0f", waits=[("ac", kb.val("ac"))])
    kb.free(wsg, sc, tbuf)



def emit_proj_to_dram(kb, wname, dram_out, scale=None, bias_col=None,
                      n_m=DC, tag="pj", extra_waits=(), src=None, vtm_dram=None,
                      m_off=0):
    nc = kb.nc
    xn = kb.sb["xn"]
    if src is None:
        src = xn
    wsb = kb.alloc(tag + "w", [128, DC, n_m * 128], BF16)
    obuf = kb.alloc(tag + "o", [128, n_m, T], BF16)

    with nc.Block() as b:
        @b.gpsimd
        def _(g):
            g.wait_ge(kb.s["pw"], kb.val("pw"))
            i = g.dma_start(out=wsb[:, :, :],
                            in_=kb.dram[wname].rearrange("(c p) m -> p c m", p=128)[
                                :, :, m_off * 128:(m_off + n_m) * 128])
            dchain(i, kb, "lw")

    def evac(b, key, ps, pe_t):
        mc, tc = key

        @b.vector
        def _(v):
            v.wait_ge(kb.s["pe"], pe_t)
            o = obuf[:, mc, tc * 512:(tc + 1) * 512]
            if bias_col is not None and scale is not None:
                ins = v.tensor_scalar(o, ps[:], bias_col(mc), scale, ALU.add, ALU.mult)
            elif bias_col is not None:
                ins = v.tensor_scalar(o, ps[:], bias_col(mc), None, ALU.add)
            elif scale is not None:
                ins = v.tensor_scalar(o, ps[:], scale, None, ALU.mult)
            else:
                ins = v.tensor_copy(o, ps[:])
            chain(ins, kb, "evd")

    tiles = [(mc, tc) for mc in range(n_m) for tc in range(4)]
    emit_mm(kb, tiles,
            stat_fn=lambda key, kc: wsb[:, kc, key[0] * 128:(key[0] + 1) * 128],
            mov_fn=lambda key, kc: src[:, kc, key[1] * 512:(key[1] + 1) * 512],
            nk_fn=lambda key: DC, evac_fn=evac, tag=tag,
            waits=[("lw", kb.val("lw"))] + list(extra_waits))

    with nc.Block() as b:
        @b.sync
        def _(sy):
            sy.wait_ge(kb.s["evd"], kb.val("evd"))
            i = sy.dma_start(out=dram_out.rearrange("(c p) t -> p c t", p=128)[
                                 :, m_off:m_off + n_m, :],
                             in_=obuf[:, :, :])
            dchain(i, kb, "spl")

        @b.gpsimd
        def _(g):
            g.wait_ge(kb.s["spl"], kb.val("spl"))
            i = g.engine_nop()
            i.then_inc(kb.s["pw"], 1)
            kb.inc("pw")

    if vtm_dram is not None:
        emit_vtm_spill(kb, obuf, vtm_dram, n_m, hs_off=m_off * 2)
    kb.free(wsb, obuf)


def emit_vtm_spill(kb, vbuf, vtm_dram, n_m=DC, hs_off=0):
    nc = kb.nc
    idn = kb.sb["ident_bf"]
    with contextlib.ExitStack() as pc:
        pst = [pc.enter_context(nc.psum_tensor(kb.uname("vt"), [128, 128], BF16))
               for _ in range(4)]
        tb_ = kb.alloc("vtb", [128, 4, 128], BF16)
        pairs = [(dc, tb) for dc in range(n_m) for tb in range(16)]
        pe0 = kb.val("pe")
        ev0 = kb.val("evd")
        sp0 = kb.val("spl")
        with nc.Block() as b:
            @b.tensor
            def _(t):
                t.wait_ge(kb.s["evd"], kb.val("evd"))
                for i, (dc, tb) in enumerate(pairs):
                    if i >= 4:
                        t.wait_ge(kb.s["evd"], ev0 + (i - 4) + 1)
                    ins = t.transpose(pst[i % 4][:], vbuf[:, dc, tb * 128:(tb + 1) * 128], idn[:])
                    ins.then_inc(kb.s["pe"], 1)
                    kb.inc("pe")

            @b.vector
            def _(v):
                for i, (dc, tb) in enumerate(pairs):
                    v.wait_ge(kb.s["pe"], pe0 + i + 1)
                    if i >= 4:
                        v.wait_ge(kb.s["spl"], sp0 + 16 * 2 * (i - 3))
                    ins = v.tensor_copy(tb_[:, i % 4, :], pst[i % 4][:])
                    chain(ins, kb, "evd")

            @b.sync
            def _(sy):
                for i, (dc, tb) in enumerate(pairs):
                    sy.wait_ge(kb.s["evd"], ev0 + i + 1)
                    for half in range(2):
                        ins = sy.dma_start(
                            out=vtm_dram[hs_off + dc * 2 + half, tb * 128:(tb + 1) * 128, :],
                            in_=tb_[:, i % 4, half * 64:(half + 1) * 64])
                        dchain(ins, kb, "spl")

        with nc.Block() as b:
            @b.gpsimd
            def _(g):
                g.wait_ge(kb.s["spl"], kb.val("spl"))
                i = g.engine_nop()
                i.then_inc(kb.s["pw"], 1)
                kb.inc("pw")
        kb.free(tb_)


def emit_std_attn_core(kb, q_dram, k_dram, vtm_dram, nh, causal, sink,
                       l1aa_norm=False, tag="at"):
    nc = kb.nc
    cst = kb.sb["consts"]
    idn = kb.sb["ident_bf"]
    idnf = kb.sb["ident_f32"]
    onesf = kb.sb["onesrow_f32"]
    qb = kb.alloc(tag + "q", [128, 2, T], BF16)
    kb_t = kb.alloc(tag + "k", [128, 2, T], BF16)
    vt = kb.alloc(tag + "v", [128, 2, 32, 64], BF16)
    srow = kb.alloc(tag + "s", [128, T], F32)
    prow = kb.alloc(tag + "p", [128, T], BF16)
    ptb = kb.alloc(tag + "t", [128, 16, 128], BF16)
    cols = kb.alloc(tag + "c", [128, 8], F32)
    recb = kb.alloc(tag + "r", [64, 128], F32)
    outv = kb.alloc(tag + "o", [64, 128], F32)
    rrow = kb.alloc(tag + "rr", [1, 128], F32)

    NM, SUMS, REC, NMP, EV, LV = range(6)

    with contextlib.ExitStack() as pc:
        psS = [pc.enter_context(nc.psum_tensor(kb.uname(tag + "S"), [128, 512], F32))
               for _ in range(2)]
        psT = [pc.enter_context(nc.psum_tensor(kb.uname(tag + "T"), [128, 128], BF16))
               for _ in range(2)]
        psO = pc.enter_context(nc.psum_tensor(kb.uname(tag + "O"), [64, 128], F32))
        psR = pc.enter_context(nc.psum_tensor(kb.uname(tag + "R"), [64, 128], F32))

        for hc in range(nh // 2):
            par = hc % 2
            with nc.Block() as b:
                @b.sync
                def _(sy):
                    if hc >= 2:
                        sy.wait_ge(kb.s["hcd"], kb.val("hcd") - 1)
                    i = sy.dma_start(out=qb[:, par, :], in_=q_dram.rearrange(
                        "(c p) t -> p c t", p=128)[:, hc, :])
                    dchain(i, kb, "lq")
                    i = sy.dma_start(out=kb_t[:, par, :], in_=k_dram.rearrange(
                        "(c p) t -> p c t", p=128)[:, hc, :])
                    dchain(i, kb, "lq")
                    vtp = vt[:, par]
                    for half in range(2):
                        i = sy.dma_start(
                            out=vtp[:, half * 16:(half + 1) * 16, :],
                            in_=vtm_dram[hc * 2 + half].rearrange("(a p) b -> p a b", p=128))
                        dchain(i, kb, "lq")

            lq_t = kb.val("lq")
            for hh in range(2):
                h = hc * 2 + hh
                hp = hh * 64
                for qi in range(16):
                    w = (qi + 1) * 128 if causal else T
                    nkc = (w + 511) // 512
                    nkb = w // 128
                    with nc.Block() as b:
                        evA0 = kb.val("evd")

                        @b.tensor
                        def _(t):
                            t.wait_ge(kb.s["lq"], lq_t)
                            t.wait_ge(kb.s["acs"], kb.val("acs"))
                            for kc in range(nkc):
                                wc = min(512, w - kc * 512)
                                if kc >= 2:
                                    t.wait_ge(kb.s["evd"], evA0 + kc - 1)
                                ins = t.matmul(psS[kc % 2][:, 0:wc],
                                               qb[hp:hp + 64, par, qi * 128:(qi + 1) * 128],
                                               kb_t[hp:hp + 64, par, kc * 512:kc * 512 + wc],
                                               start=True, stop=True)
                                ins.then_inc(kb.s["pe"], 1)
                                kb.inc("pe")

                        pe_base = kb.val("pe") - nkc

                        @b.vector
                        def _(v):
                            for kc in range(nkc):
                                wc = min(512, w - kc * 512)
                                v.wait_ge(kb.s["pe"], pe_base + kc + 1)
                                ins = v.tensor_copy(srow[:, kc * 512:kc * 512 + wc],
                                                    psS[kc % 2][:, 0:wc])
                                chain(ins, kb, "evd")

                        if causal:
                            @b.gpsimd
                            def _(g):
                                g.wait_ge(kb.s["evd"], kb.val("evd"))
                                ins = g.tensor_tensor(srow[:, (w - 128):w],
                                                      srow[:, (w - 128):w],
                                                      kb.sb["cmask"][:], ALU.add)
                                chain(ins, kb, "po")

                        @b.vector
                        def _(v):
                            if causal:
                                v.wait_ge(kb.s["po"], kb.val("po"))
                            else:
                                v.wait_ge(kb.s["evd"], kb.val("evd"))
                            ins = v.tensor_reduce(cols[:, NM:NM + 1], srow[:, 0:w],
                                                  axis=AX.X, op=ALU.max, negate=True)
                            if l1aa_norm:
                                ins = v.tensor_reduce(cols[:, NMP:NMP + 1], srow[:, 0:w - 64],
                                                      axis=AX.X, op=ALU.max, negate=True)
                            chain(ins, kb, "ve")

                        @b.scalar
                        def _(s):
                            s.wait_ge(kb.s["ve"], kb.val("ve"))
                            if l1aa_norm:
                                s.activation(prow[:, 0:w - 64], srow[:, 0:w - 64], AF.Exp,
                                             bias=cols[:, NM:NM + 1])
                                ins = s.activation(prow[:, w - 64:w], srow[:, w - 64:w], AF.Exp,
                                                   bias=cols[:, NM:NM + 1],
                                                   accum_out=cols[:, SUMS:SUMS + 1])
                            else:
                                ins = s.activation(prow[:, 0:w], srow[:, 0:w], AF.Exp,
                                                   bias=cols[:, NM:NM + 1],
                                                   accum_out=cols[:, SUMS:SUMS + 1])
                            chain(ins, kb, "acs")

                        if l1aa_norm:
                            @b.vector
                            def _(v):
                                v.wait_ge(kb.s["acs"], kb.val("acs"))
                                ins = v.tensor_tensor(cols[:, EV:EV + 1], cols[:, NM:NM + 1],
                                                      cols[:, NMP:NMP + 1], ALU.subtract)
                                chain(ins, kb, "ve")

                            @b.scalar
                            def _(s):
                                s.wait_ge(kb.s["ve"], kb.val("ve"))
                                ins = s.activation(cols[:, EV:EV + 1], cols[:, EV:EV + 1], AF.Exp)
                                chain(ins, kb, "ac")

                            @b.vector
                            def _(v):
                                v.wait_ge(kb.s["ac"], kb.val("ac"))
                                ins = v.scalar_tensor_tensor(
                                    cols[:, LV:LV + 1], cols[:, EV:EV + 1],
                                    cols[:, SUMS:SUMS + 1], cols[:, SUMS:SUMS + 1],
                                    ALU.mult, ALU.add)
                                ins.then_inc(kb.s["ve"], 1)
                                kb.inc("ve")
                                v.wait_ge(kb.s["ve"], kb.val("ve"))
                                ins = v.reciprocal(cols[:, REC:REC + 1], cols[:, LV:LV + 1])
                                chain(ins, kb, "ve")
                        else:
                            @b.vector
                            def _(v):
                                v.wait_ge(kb.s["acs"], kb.val("acs"))
                                ins = v.reciprocal(cols[:, REC:REC + 1], cols[:, SUMS:SUMS + 1])
                                chain(ins, kb, "ve")

                        @b.tensor
                        def _(t):
                            t.wait_ge(kb.s["ve"], kb.val("ve"))
                            ins = t.matmul(psR[0:1, :], cols[:, REC:REC + 1], idnf[:, :],
                                           start=True, stop=True)
                            chain(ins, kb, "pe")

                        @b.vector
                        def _(v):
                            v.wait_ge(kb.s["pe"], kb.val("pe"))
                            ins = v.tensor_copy(rrow[0:1, :], psR[0:1, :])
                            chain(ins, kb, "ve")

                        @b.tensor
                        def _(t):
                            t.wait_ge(kb.s["ve"], kb.val("ve"))
                            ins = t.matmul(psR[0:64, :], onesf[0:1, 0:64], rrow[0:1, :],
                                           start=True, stop=True)
                            chain(ins, kb, "pe")

                        @b.vector
                        def _(v):
                            v.wait_ge(kb.s["pe"], kb.val("pe"))
                            ins = v.tensor_copy(recb[:, :], psR[0:64, :])
                            chain(ins, kb, "ve")

                        evG0 = kb.val("evd")

                        @b.tensor
                        def _(t):
                            t.wait_ge(kb.s["acs"], kb.val("acs"))
                            for kbi in range(nkb):
                                if kbi >= 2:
                                    t.wait_ge(kb.s["evd"], evG0 + kbi - 1)
                                ins = t.transpose(psT[kbi % 2][:],
                                                  prow[:, kbi * 128:(kbi + 1) * 128], idn[:])
                                ins.then_inc(kb.s["pe"], 1)
                                kb.inc("pe")

                        peT = kb.val("pe") - nkb

                        @b.vector
                        def _(v):
                            for kbi in range(nkb):
                                v.wait_ge(kb.s["pe"], peT + kbi + 1)
                                ins = v.tensor_copy(ptb[:, kbi, :], psT[kbi % 2][:])
                                chain(ins, kb, "evd")

                        @b.tensor
                        def _(t):
                            t.wait_ge(kb.s["evd"], kb.val("evd"))
                            for kbi in range(nkb):
                                ins = t.matmul(psO[:, :], vt[:, par][:, hh * 16 + kbi, :],
                                               ptb[:, kbi, :],
                                               start=(kbi == 0), stop=(kbi == nkb - 1))
                            ins.then_inc(kb.s["pe"], 1)
                            kb.inc("pe")

                        @b.vector
                        def _(v):
                            v.wait_ge(kb.s["pe"], kb.val("pe"))
                            ins = v.tensor_tensor(outv[:, :], psO[:, :], recb[:, :], ALU.mult)
                            chain(ins, kb, "evd")

                        sink(b, h, qi, outv)

            with nc.Block() as b:
                @b.gpsimd
                def _(g):
                    g.wait_ge(kb.s["spl"], kb.val("spl"))
                    g.wait_ge(kb.s["evd"], kb.val("evd"))
                    i = g.engine_nop()
                    i.then_inc(kb.s["hcd"], 1)
                    kb.inc("hcd")

    kb.free(qb, kb_t, vt, srow, prow, ptb, cols, recb, outv, rrow)


def emit_out_proj(kb, att_dram, wname, bias_name=None, n_k=DC, tag="op"):
    nc = kb.nc
    x = kb.sb["x_res"]
    wsb = kb.alloc(tag + "w", [128, n_k, 1024], BF16)
    ab = kb.alloc(tag + "a", [128, n_k, T], BF16)
    if bias_name is not None:
        bcol = kb.alloc(tag + "b", [128, 8], F32)

    with nc.Block() as b:
        @b.gpsimd
        def _(g):
            g.wait_ge(kb.s["pw"], kb.val("pw"))
            i = g.dma_start(out=wsb[:, :, :],
                            in_=kb.dram[wname].rearrange("(c p) m -> p c m", p=128))
            dchain(i, kb, "lw")

        @b.sync
        def _(sy):
            sy.wait_ge(kb.s["spl"], kb.val("spl"))
            i = sy.dma_start(out=ab[:, :, :],
                             in_=att_dram.rearrange("(c p) t -> p c t", p=128))
            dchain(i, kb, "lq")
            if bias_name is not None:
                i = sy.dma_start(out=bcol[:, :], in_=kb.dram[bias_name])
                dchain(i, kb, "lq")

    def evac(b, key, ps, pe_t):
        mc, tc = key

        @b.vector
        def _(v):
            v.wait_ge(kb.s["pe"], pe_t)
            xs = kb.sb["x_res"][:, mc, tc * 512:(tc + 1) * 512]
            if bias_name is not None:
                ins = v.scalar_tensor_tensor(xs, ps[:], bcol[:, mc:mc + 1], xs,
                                             ALU.add, ALU.add)
            else:
                ins = v.tensor_tensor(xs, xs, ps[:], ALU.add)
            chain(ins, kb, "evd")

    tiles = [(mc, tc) for mc in range(DC) for tc in range(4)]
    emit_mm(kb, tiles,
            stat_fn=lambda key, kc: wsb[:, kc, key[0] * 128:(key[0] + 1) * 128],
            mov_fn=lambda key, kc: ab[:, kc, key[1] * 512:(key[1] + 1) * 512],
            nk_fn=lambda key: n_k, evac_fn=evac, tag=tag,
            waits=[("lw", kb.val("lw")), ("lq", kb.val("lq"))])
    with kb.nc.Block() as b:
        @b.gpsimd
        def _(g):
            g.wait_ge(kb.s["evd"], kb.val("evd"))
            i = g.engine_nop()
            i.then_inc(kb.s["xr"], 1)
            kb.inc("xr")
            gi = g.engine_nop()
            gi.then_inc(kb.s["pw"], 1)
            kb.inc("pw")
    fl = [wsb, ab] + ([bcol] if bias_name is not None else [])
    kb.free(*fl)


def mk_spill_sink(kb, att_dram):
    def sink(b, h, qi, outv):
        @b.gpsimd
        def _(g):
            g.wait_ge(kb.s["evd"], kb.val("evd"))
            ins = g.dma_start(out=att_dram[h * 64:(h + 1) * 64, qi * 128:(qi + 1) * 128],
                              in_=outv[:, :])
            dchain(ins, kb, "spl")
    return sink


def emit_l2aa(kb):
    q_d = kb.dscratch(kb.uname("l2aq"), [D, T], BF16)
    k_d = kb.dscratch(kb.uname("l2ak"), [D, T], BF16)
    v_d = kb.dscratch(kb.uname("l2av"), [16, T, 64], BF16)
    a_d = kb.dscratch(kb.uname("l2ao"), [D, T], BF16)
    emit_proj_to_dram(kb, "l2a_wv", kb.dscratch(kb.uname("l2avf"), [D, T], BF16),
                      tag="l2av", vtm_dram=v_d)
    emit_proj_to_dram(kb, "l2a_wq", q_d, scale=0.125, tag="l2aq")
    emit_proj_to_dram(kb, "l2a_wk", k_d, tag="l2ak")
    kb.free(kb.sb["xn"])
    emit_std_attn_core(kb, q_d, k_d, v_d, 16, causal=False,
                       sink=mk_spill_sink(kb, a_d), tag="l2at")
    kb.sb["xn"] = kb.alloc("xn", [128, DC, T], BF16)
    emit_out_proj(kb, a_d, "l2a_wo", tag="l2ap")


def emit_l1aa(kb):
    q_d = kb.dscratch(kb.uname("l1aq"), [D, T], BF16)
    k_d = kb.dscratch(kb.uname("l1ak"), [D, T], BF16)
    v_d = kb.dscratch(kb.uname("l1av"), [16, T, 64], BF16)
    a_d = kb.dscratch(kb.uname("l1ao"), [D, T], BF16)
    emit_proj_to_dram(kb, "l1a_wv", kb.dscratch(kb.uname("l1avf"), [D, T], BF16),
                      tag="l1av", vtm_dram=v_d)
    emit_proj_to_dram(kb, "l1a_wq", q_d, scale=0.125, tag="l1aq")
    emit_proj_to_dram(kb, "l1a_wk", k_d, tag="l1ak")
    kb.free(kb.sb["xn"])
    emit_std_attn_core(kb, q_d, k_d, v_d, 16, causal=False,
                       sink=mk_spill_sink(kb, a_d), l1aa_norm=True, tag="l1at")
    kb.sb["xn"] = kb.alloc("xn", [128, DC, T], BF16)
    emit_out_proj(kb, a_d, "l1a_wo", bias_name="l1a_ob", tag="l1ap")



def emit_vtm_local(kb, vbuf, vtm_sb, n_m=DC):
    nc = kb.nc
    idn = kb.sb["ident_bf"]
    with contextlib.ExitStack() as pc:
        pst = [pc.enter_context(nc.psum_tensor(kb.uname("vl"), [128, 128], BF16))
               for _ in range(4)]
        pairs = [(dc, tb) for dc in range(n_m) for tb in range(16)]
        pe0 = kb.val("pe")
        ev0 = kb.val("evd")
        with nc.Block() as b:
            @b.tensor
            def _(t):
                t.wait_ge(kb.s["evd"], ev0)
                for i, (dc, tb) in enumerate(pairs):
                    if i >= 4:
                        t.wait_ge(kb.s["evd"], ev0 + (i - 4) + 1)
                    ins = t.transpose(pst[i % 4][:], vbuf[:, dc, tb * 128:(tb + 1) * 128], idn[:])
                    ins.then_inc(kb.s["pe"], 1)
                    kb.inc("pe")

            @b.vector
            def _(v):
                for i, (dc, tb) in enumerate(pairs):
                    v.wait_ge(kb.s["pe"], pe0 + i + 1)
                    ins = v.tensor_copy(vtm_sb[:, tb, dc * 128:(dc + 1) * 128], pst[i % 4][:])
                    chain(ins, kb, "evd")


def emit_l0aa(kb):
    nc = kb.nc
    x, xn = kb.sb["x_res"], kb.sb["xn"]
    cst = kb.sb["consts"]
    bv = kb.alloc("l0ab_", [128, 24], F32)
    sg = kb.alloc("l0as", [128, 24], F32)
    kt = kb.alloc("l0ak", [128, DC, T], BF16)
    vtm = kb.alloc("l0av", [128, 16, 1024], BF16)

    with nc.Block() as b:
        @b.sync
        def _(sy):
            i = sy.dma_start(out=bv[:, :], in_=kb.dram["l0a_b"])
            dchain(i, kb, "lw")

        @b.scalar
        def _(s):
            s.wait_ge(kb.s["lw"], kb.val("lw"))
            i = s.activation(sg[:, 0:24], bv[:, 0:24], AF.Sign)
            chain(i, kb, "ac")

        @b.vector
        def _(v):
            v.wait_ge(kb.s["ac"], kb.val("ac"))
            i = v.tensor_tensor(sg[:, 0:8], sg[:, 0:8], sg[:, 8:16], ALU.mult)
            chain(i, kb, "ve")

    mb = kb.sb["mean_b"]
    xm = kb.alloc("l0axm", [128, 2, T], F32)
    with nc.Block() as b:
        ac_s0 = kb.val("ac")

        @b.vector
        def _(v):
            v.wait_ge(kb.s["ve"], kb.val("ve"))
            v.wait_ge(kb.s["ln"], kb.val("ln"))
            for dc in range(DC):
                if dc >= 2:
                    v.wait_ge(kb.s["ac"], ac_s0 + dc - 1)
                i = v.tensor_tensor(xm[:, dc % 2, :], x[:, dc, :], mb[:, :], ALU.subtract)
                i.then_inc(kb.s["ve"], 1)
                kb.inc("ve")

        ve_s0 = kb.val("ve") - DC
        @b.scalar
        def _(s):
            for dc in range(DC):
                s.wait_ge(kb.s["ve"], ve_s0 + dc + 1)
                i = s.activation(kt[:, dc, :], xm[:, dc % 2, :], AF.Sign)
                i.then_inc(kb.s["ac"], 1)
                kb.inc("ac")
    kb.free(xm)
    kb.free(kb.sb.pop("mean_b"))

    vf = kb.alloc("l0avf", [128, DC, T], BF16)
    with nc.Block() as b:
        @b.vector
        def _(v):
            v.wait_ge(kb.s["ac"], kb.val("ac"))
            for dc in range(DC):
                i = v.tensor_scalar(vf[:, dc, :], xn[:, dc, :], sg[:, 16 + dc:17 + dc],
                                    None, ALU.mult)
            chain(i, kb, "evd")
    emit_vtm_local(kb, vf, vtm)
    kb.free(vf)
    kb.free(kb.sb["xn"])

    qs = kb.alloc("l0aq", [128, 2, DC, 512], BF16)
    ptc = kb.alloc("l0ap", [128, 16, 512], BF16)
    scrap = kb.alloc("l0ax", [128, 8], BF16)
    with contextlib.ExitStack() as pc:
        psS = [pc.enter_context(nc.psum_tensor(kb.uname("l0S"), [128, 512], F32))
               for _ in range(4)]
        psO = [pc.enter_context(nc.psum_tensor(kb.uname("l0O"), [128, 512], F32))
               for _ in range(2)]
        for qc in range(4):
            par = qc % 2
            with nc.Block() as b:
                @b.vector
                def _(v):
                    v.wait_ge(kb.s["evd"], kb.val("evd"))
                    if qc >= 2:
                        v.wait_ge(kb.s["pe"], kb.val("pe"))
                    qsp = qs[:, par]
                    for dc in range(DC):
                        ins = v.tensor_scalar(qsp[:, dc, :],
                                              kt[:, dc, qc * 512:(qc + 1) * 512],
                                              sg[:, dc:dc + 1], None, ALU.mult)
                    chain(ins, kb, "ve")

            nkb = qc * 4 + 4
            acs_q0 = kb.val("acs")
            for kbi in range(nkb):
                with nc.Block() as b:
                    @b.tensor
                    def _(t):
                        t.wait_ge(kb.s["ve"], kb.val("ve"))
                        if kbi >= 4:
                            t.wait_ge(kb.s["acs"], acs_q0 + kbi - 3)
                        else:
                            t.wait_ge(kb.s["acs"], acs_q0)
                        qsp = qs[:, par]
                        for dc in range(DC):
                            ins = t.matmul(psS[kbi % 4][:],
                                           kt[:, dc, kbi * 128:(kbi + 1) * 128],
                                           qsp[:, dc, :],
                                           start=(dc == 0), stop=(dc == DC - 1))
                        ins.then_inc(kb.s["pe"], 1)
                        kb.inc("pe")

                    @b.scalar
                    def _(s):
                        s.wait_ge(kb.s["pe"], kb.val("pe"))
                        ins = s.activation(ptc[:, kbi, :], psS[kbi % 4][:], AF.Sigmoid,
                                           scale=cst[:, C_EIGHTH:C_EIGHTH + 1])
                        chain(ins, kb, "acs")

                    if kbi >= qc * 4:
                        @b.gpsimd
                        def _(g):
                            g.wait_ge(kb.s["acs"], kb.val("acs"))
                            ins = g.affine_select(
                                ptc[:, kbi, :], ptc[:, kbi, :],
                                pattern=[[1, 512]], compare_op=ALU.is_ge, fill=0.0,
                                base=512 * qc - 128 * kbi, channel_multiplier=-1)
                            chain(ins, kb, "po")

            import os
            if os.environ.get("L0AA_DBG") and qc == 3:
                dbg_p = kb.dram.get("dbg_ptc")
                if dbg_p is None:
                    dbg_p = kb.nc.dram_tensor("dbg_ptc", [128, 16, 512], BF16,
                                              kind="ExternalOutput").ap()
                    kb.dram["dbg_ptc"] = dbg_p
                    dbg_q = kb.nc.dram_tensor("dbg_qs", [128, DC, 512], BF16,
                                              kind="ExternalOutput").ap()
                    kb.dram["dbg_qs"] = dbg_q
                    dbg_k = kb.nc.dram_tensor("dbg_kt", [128, DC, T], BF16,
                                              kind="ExternalOutput").ap()
                    kb.dram["dbg_kt"] = dbg_k
                with nc.Block() as bdg:
                    @bdg.gpsimd
                    def _(g):
                        g.wait_ge(kb.s["acs"], kb.val("acs"))
                        g.wait_ge(kb.s["po"], kb.val("po"))
                        i = g.dma_start(out=kb.dram["dbg_ptc"], in_=ptc[:, :, :])
                        dchain(i, kb, "spl")
                        i = g.dma_start(out=kb.dram["dbg_qs"], in_=qs[:, par])
                        dchain(i, kb, "spl")
                        i = g.dma_start(out=kb.dram["dbg_kt"], in_=kt[:, :, :])
                        dchain(i, kb, "spl")
                        g.wait_ge(kb.s["spl"], kb.val("spl"))
            with nc.Block() as bset:
                @bset.scalar
                def _(s2):
                    s2.wait_ge(kb.s["acs"], kb.val("acs"))
                    ins = s2.activation(scrap[:, 0:8], ptc[:, nkb - 1, 0:8], AF.Square)
                    chain(ins, kb, "acs")

            ev_c0 = kb.val("evd")
            with nc.Block() as b:
                @b.tensor
                def _(t):
                    t.wait_ge(kb.s["acs"], kb.val("acs"))
                    t.wait_ge(kb.s["po"], kb.val("po"))
                    t.wait_ge(kb.s["evd"], ev_c0)
                    for dvc in range(DC):
                        if dvc >= 2:
                            t.wait_ge(kb.s["evd"], ev_c0 + dvc - 1)
                        for kbi in range(nkb):
                            ins = t.matmul(psO[dvc % 2][:],
                                           vtm[:, kbi, dvc * 128:(dvc + 1) * 128],
                                           ptc[:, kbi, :],
                                           start=(kbi == 0), stop=(kbi == nkb - 1))
                        ins.then_inc(kb.s["pe"], 1)
                        kb.inc("pe")

                pe0 = kb.val("pe") - DC

                @b.vector
                def _(v):
                    for dvc in range(DC):
                        v.wait_ge(kb.s["pe"], pe0 + dvc + 1)
                        xs = x[:, dvc, qc * 512:(qc + 1) * 512]
                        ins = v.tensor_tensor(xs, xs, psO[dvc % 2][:], ALU.add)
                        chain(ins, kb, "evd")
    with nc.Block() as b:
        @b.gpsimd
        def _(g):
            g.wait_ge(kb.s["evd"], kb.val("evd"))
            i = g.engine_nop()
            i.then_inc(kb.s["xr"], 1)
            kb.inc("xr")
    kb.free(bv, sg, kt, vtm, qs, ptc, scrap)
    kb.sb["xn"] = kb.alloc("xn", [128, DC, T], BF16)


def emit_l2ffn(kb):
    nc = kb.nc
    x, xn = kb.sb["x_res"], kb.sb["xn"]
    wbuf = kb.alloc("l2fw", [128, DC, 1024], BF16)
    mu = kb.alloc("l2fm", [128, 16], F32)
    kact_d = kb.dscratch(kb.uname("l2fk"), [D, T], BF16)

    def load_w(wname):
        with nc.Block() as b:
            @b.gpsimd
            def _(g):
                g.wait_ge(kb.s["pw"], kb.val("pw"))
                i = g.dma_start(out=wbuf[:, :, :],
                                in_=kb.dram[wname].rearrange("(c p) m -> p c m", p=128))
                dchain(i, kb, "lw")

    def build_mix(mix, mu_off):
        with nc.Block() as b:
            @b.vector
            def _(v):
                v.wait_ge(kb.s["ln"], kb.val("ln"))
                v.wait_ge(kb.s["pw"], kb.val("pw"))
                for dc in range(DC):
                    v.tensor_tensor(mix[:, dc, 1:T], xn[:, dc, 1:T], xn[:, dc, 0:T - 1],
                                    ALU.subtract)
                for dc in range(DC):
                    v.scalar_tensor_tensor(mix[:, dc, 1:T], mix[:, dc, 1:T],
                                           mu[:, mu_off + dc:mu_off + dc + 1],
                                           xn[:, dc, 0:T - 1], ALU.mult, ALU.add)
                for dc in range(DC):
                    ins = v.tensor_scalar(mix[:, dc, 0:1], xn[:, dc, 0:1],
                                          mu[:, mu_off + dc:mu_off + dc + 1], None, ALU.mult)
                chain(ins, kb, "ve")

    load_w("l2f_wk")
    with nc.Block() as b:
        @b.sync
        def _(sy):
            i = sy.dma_start(out=mu[:, :], in_=kb.dram["l2f_mu"])
            dchain(i, kb, "lq")
    mix = kb.alloc("l2fx", [128, DC, T], BF16)
    kbuf = kb.alloc("l2fkb", [128, DC, T], BF16)
    build_mix(mix, 8)
    tb = kb.alloc("l2ft", [128, 4, 512], BF16)

    def evac_k(b, key, ps, pe_t):
        mc, tc = key
        i_t = key[0] * 4 + key[1]

        @b.vector
        def _(v):
            v.wait_ge(kb.s["pe"], pe_t)
            if i_t >= 4:
                v.wait_ge(kb.s["po"], kb.val("po") - 3)
            ins = v.tensor_scalar_max(tb[:, i_t % 4, :], ps[:], 0.0)
            chain(ins, kb, "evd")

        @b.gpsimd
        def _(g):
            g.wait_ge(kb.s["evd"], kb.val("evd"))
            ins = g.tensor_tensor(kbuf[:, mc, tc * 512:(tc + 1) * 512],
                                  tb[:, i_t % 4, :], tb[:, i_t % 4, :], ALU.mult)
            chain(ins, kb, "po")

    tiles = [(mc, tc) for mc in range(DC) for tc in range(4)]
    emit_mm(kb, tiles,
            stat_fn=lambda key, kc: wbuf[:, kc, key[0] * 128:(key[0] + 1) * 128],
            mov_fn=lambda key, kc: mix[:, kc, key[1] * 512:(key[1] + 1) * 512],
            nk_fn=lambda key: DC, evac_fn=evac_k, tag="l2fk",
            waits=[("ve", kb.val("ve")), ("lw", kb.val("lw"))])
    with nc.Block() as b:
        @b.sync
        def _(sy):
            sy.wait_ge(kb.s["po"], kb.val("po"))
            i = sy.dma_start(out=kact_d.rearrange("(c p) t -> p c t", p=128),
                             in_=kbuf[:, :, :])
            dchain(i, kb, "spl")

        @b.gpsimd
        def _(g):
            g.wait_ge(kb.s["spl"], kb.val("spl"))
            i = g.engine_nop()
            i.then_inc(kb.s["pw"], 1)
            kb.inc("pw")
    kb.free(kbuf, tb)

    load_w("l2f_wr")
    rr = kb.alloc("l2fr", [128, DC, T], BF16)
    build_mix(mix, 0)

    def evac_r(b, key, ps, pe_t):
        mc, tc = key

        @b.scalar
        def _(s):
            s.wait_ge(kb.s["pe"], pe_t)
            ins = s.activation(rr[:, mc, tc * 512:(tc + 1) * 512], ps[:], AF.Sigmoid)
            chain(ins, kb, "evd")

    emit_mm(kb, tiles,
            stat_fn=lambda key, kc: wbuf[:, kc, key[0] * 128:(key[0] + 1) * 128],
            mov_fn=lambda key, kc: mix[:, kc, key[1] * 512:(key[1] + 1) * 512],
            nk_fn=lambda key: DC, evac_fn=evac_r, tag="l2fr",
            waits=[("ve", kb.val("ve")), ("lw", kb.val("lw"))])
    kb.free(mix)
    with nc.Block() as b:
        @b.gpsimd
        def _(g):
            g.wait_ge(kb.s["evd"], kb.val("evd"))
            i = g.engine_nop()
            i.then_inc(kb.s["pw"], 1)
            kb.inc("pw")

    load_w("l2f_wv")
    kch = kb.alloc("l2fc", [128, 2, DC, 512], BF16)
    tv = kb.alloc("l2fv", [128, 4, 512], F32)
    with contextlib.ExitStack() as pc:
        pss = [pc.enter_context(nc.psum_tensor(kb.uname("l2fp"), [128, 512], F32))
               for _ in range(4)]
        pe_v0 = kb.val("pe")
        po_v0 = kb.val("po")
        ev_v0 = kb.val("evd")
        for tc in range(4):
            par = tc % 2
            with nc.Block() as b:
                @b.sync
                def _(sy):
                    if tc >= 2:
                        sy.wait_ge(kb.s["pe"], pe_v0 + (tc - 1) * DC)
                    i = sy.dma_start(
                        out=kch[:, par],
                        in_=kact_d.rearrange("(c p) t -> p c t", p=128)[
                            :, :, tc * 512:(tc + 1) * 512])
                    dchain(i, kb, "lq")

                lq_t = kb.val("lq")

                @b.tensor
                def _(t):
                    t.wait_ge(kb.s["lq"], lq_t)
                    t.wait_ge(kb.s["lw"], kb.val("lw"))
                    kchp = kch[:, par]
                    for mc in range(DC):
                        g = tc * DC + mc
                        if g >= 4:
                            t.wait_ge(kb.s["po"], po_v0 + (g - 4) + 1)
                        for kc in range(DC):
                            ins = t.matmul(pss[g % 4][:],
                                           wbuf[:, kc, mc * 128:(mc + 1) * 128],
                                           kchp[:, kc, :],
                                           start=(kc == 0), stop=(kc == DC - 1))
                        ins.then_inc(kb.s["pe"], 1)
                        kb.inc("pe")

                @b.vector
                def _(v):
                    for mc in range(DC):
                        g = tc * DC + mc
                        v.wait_ge(kb.s["pe"], pe_v0 + g + 1)
                        ins = v.tensor_tensor(tv[:, g % 4, :], pss[g % 4][:],
                                              rr[:, mc, tc * 512:(tc + 1) * 512], ALU.mult)
                        chain(ins, kb, "evd")

                @b.gpsimd
                def _(g_):
                    for mc in range(DC):
                        g = tc * DC + mc
                        g_.wait_ge(kb.s["evd"], ev_v0 + g + 1)
                        xs = x[:, mc, tc * 512:(tc + 1) * 512]
                        ins = g_.tensor_tensor(xs, xs, tv[:, g % 4, :], ALU.add)
                        chain(ins, kb, "po")
    with nc.Block() as b:
        @b.gpsimd
        def _(g):
            g.wait_ge(kb.s["po"], kb.val("po"))
            i = g.engine_nop()
            i.then_inc(kb.s["xr"], 1)
            kb.inc("xr")
            i2 = g.engine_nop()
            i2.then_inc(kb.s["pw"], 1)
            kb.inc("pw")
    kb.free(wbuf, mu, kch, tv, rr)


def emit_l2ab(kb):
    nc = kb.nc
    x = kb.sb["x_res"]
    vcols = kb.alloc("l2bv", [128, 56], F32)
    with nc.Block() as b:
        @b.sync
        def _(sy):
            i = sy.dma_start(out=vcols[:, :], in_=kb.dram["l2b_vec"])
            dchain(i, kb, "lq")

    v_d = kb.dscratch(kb.uname("l2bv"), [16, T, 64], BF16)
    vf_d = kb.dscratch(kb.uname("l2bvf"), [D, T], BF16)
    q_d = kb.dscratch(kb.uname("l2bq"), [D, T], BF16)
    k_d = kb.dscratch(kb.uname("l2bk"), [D, T], BF16)
    av_d = kb.dscratch(kb.uname("l2bav"), [D, T], BF16)
    avtm_d = kb.dscratch(kb.uname("l2bavt"), [16, T, 64], BF16)
    aav_d = kb.dscratch(kb.uname("l2baav"), [D, T], BF16)
    comb_d = kb.dscratch(kb.uname("l2bcb"), [D, T], BF16)

    emit_proj_to_dram(kb, "l2b_wv", vf_d, bias_col=lambda mc: vcols[:, 16 + mc:17 + mc],
                      tag="l2bv", vtm_dram=v_d)
    emit_proj_to_dram(kb, "l2b_wq", q_d, scale=0.125,
                      bias_col=lambda mc: vcols[:, 0 + mc:1 + mc], tag="l2bq")
    emit_proj_to_dram(kb, "l2b_wk", k_d, bias_col=lambda mc: vcols[:, 8 + mc:9 + mc],
                      tag="l2bk")
    kb.free(kb.sb["xn"])
    emit_std_attn_core(kb, q_d, k_d, v_d, 16, causal=False,
                       sink=mk_spill_sink(kb, av_d), tag="l2b1")
    avb = kb.alloc("l2bab", [128, DC, T], BF16)
    with nc.Block() as b:
        @b.sync
        def _(sy):
            sy.wait_ge(kb.s["spl"], kb.val("spl"))
            i = sy.dma_start(out=avb[:, :, :],
                             in_=av_d.rearrange("(c p) t -> p c t", p=128))
            dchain(i, kb, "lq")

        @b.gpsimd
        def _(g):
            g.wait_ge(kb.s["lq"], kb.val("lq"))
            i = g.engine_nop()
            i.then_inc(kb.s["evd"], 1)
            kb.inc("evd")
    emit_vtm_spill(kb, avb, avtm_d)
    kb.free(avb)
    emit_std_attn_core(kb, q_d, k_d, avtm_d, 16, causal=False,
                       sink=mk_spill_sink(kb, aav_d), tag="l2b2")

    st = kb.alloc("l2bs", [128, 3, DC, 512], BF16)
    cb = kb.alloc("l2bc", [128, DC, T], BF16)
    with nc.Block() as bq:
        @bq.vector
        def _(v):
            v.wait_ge(kb.s["lq"], kb.val("lq"))
            ins = v.tensor_tensor(vcols[:, 40:48], vcols[:, 40:48], vcols[:, 48:56],
                                  ALU.subtract)
            chain(ins, kb, "ve")

        @bq.gpsimd
        def _(g):
            g.wait_ge(kb.s["ve"], kb.val("ve"))
            ins = g.tensor_scalar(vcols[:, 48:56], vcols[:, 48:56], 2.0, None, ALU.mult)
            chain(ins, kb, "po")

    for tc in range(4):
        with nc.Block() as b:
            @b.sync
            def _(sy):
                sy.wait_ge(kb.s["spl"], kb.val("spl"))
                if tc > 0:
                    sy.wait_ge(kb.s["ve"], kb.val("ve"))
                for j, dsrc in enumerate((vf_d, av_d, aav_d)):
                    i = sy.dma_start(
                        out=st[:, j],
                        in_=dsrc.rearrange("(c p) t -> p c t", p=128)[
                            :, :, tc * 512:(tc + 1) * 512])
                    dchain(i, kb, "lq")

            lq_t = kb.val("lq")

            @b.vector
            def _(v):
                v.wait_ge(kb.s["lq"], lq_t)
                v.wait_ge(kb.s["po"], kb.val("po"))
                for dc in range(DC):
                    v.tensor_scalar(cb[:, dc, tc * 512:(tc + 1) * 512], st[:, 0, dc, :],
                                    vcols[:, 32 + dc:33 + dc], None, ALU.mult)
                for dc in range(DC):
                    v.scalar_tensor_tensor(cb[:, dc, tc * 512:(tc + 1) * 512],
                                           st[:, 1, dc, :], vcols[:, 40 + dc:41 + dc],
                                           cb[:, dc, tc * 512:(tc + 1) * 512],
                                           ALU.mult, ALU.add)
                for dc in range(DC):
                    ins = v.scalar_tensor_tensor(cb[:, dc, tc * 512:(tc + 1) * 512],
                                                 st[:, 2, dc, :], vcols[:, 48 + dc:49 + dc],
                                                 cb[:, dc, tc * 512:(tc + 1) * 512],
                                                 ALU.mult, ALU.add)
                chain(ins, kb, "ve")

    with nc.Block() as b:
        @b.sync
        def _(sy):
            sy.wait_ge(kb.s["ve"], kb.val("ve"))
            i = sy.dma_start(out=comb_d.rearrange("(c p) t -> p c t", p=128),
                             in_=cb[:, :, :])
            dchain(i, kb, "spl")

        @b.gpsimd
        def _(g):
            g.wait_ge(kb.s["spl"], kb.val("spl"))
            i = g.engine_nop()
            i.then_inc(kb.s["pw"], 1)
            kb.inc("pw")
    kb.free(st, cb, vcols)
    kb.sb["xn"] = kb.alloc("xn", [128, DC, T], BF16)
    emit_out_proj(kb, comb_d, "l2b_wo", bias_name="l2b_bo", tag="l2bo")





import os as _os

def _affine_probe(kb, label):
    if not _os.environ.get("AFP"):
        return
    sr = kb.alloc("afp", [128, 256], F32)
    with kb.nc.Block() as b:
        @b.gpsimd
        def _(g):
            try:
                g.affine_select(sr[:, 0:128], sr[:, 0:128], pattern=[[-1, 128]],
                                compare_op=ALU.is_ge, fill=-1e30, base=0,
                                channel_multiplier=1)
                print("AFP", label, "ok")
            except Exception as e:
                print("AFP", label, "FAIL", str(e)[:40])
    kb.free(sr)

def emit_l3ab_lambda(kb, lamt, neglamb):
    nc = kb.nc
    idnf = kb.sb["ident_f32"]
    onesf = kb.sb["onesrow_f32"]
    scol = kb.alloc("l3bsc", [8, 4], F32)
    rro = kb.alloc("l3brr0", [1, 8], F32)
    with contextlib.ExitStack() as pc:
        psR = pc.enter_context(nc.psum_tensor(kb.uname("l3lam"), [128, 8], F32))
        with nc.Block() as b:
            @b.vector
            def _(v):
                v.wait_ge(kb.s["lq"], kb.val("lq"))
                v.tensor_tensor(lamt[:, 0:128], lamt[:, 0:128], lamt[:, 128:256], ALU.mult)
                ins = v.tensor_tensor(lamt[:, 256:384], lamt[:, 256:384],
                                      lamt[:, 384:512], ALU.mult)
                ins.then_inc(kb.s["ve"], 1)
                kb.inc("ve")
                v.wait_ge(kb.s["ve"], kb.val("ve"))
                v.reduce_sum(scol[:, 0:1], lamt[:, 0:128], axis=AX.X)
                ins = v.reduce_sum(scol[:, 1:2], lamt[:, 256:384], axis=AX.X)
                chain(ins, kb, "ve")

            @b.scalar
            def _(s):
                s.wait_ge(kb.s["ve"], kb.val("ve"))
                s.activation(scol[:, 2:3], scol[:, 0:1], AF.Exp)
                ins = s.activation(scol[:, 3:4], scol[:, 1:2], AF.Exp)
                chain(ins, kb, "ac")

            @b.vector
            def _(v):
                v.wait_ge(kb.s["ac"], kb.val("ac"))
                ins = v.tensor_tensor(scol[:, 0:1], scol[:, 2:3], scol[:, 3:4],
                                      ALU.subtract)
                ins.then_inc(kb.s["ve"], 1)
                kb.inc("ve")
                v.wait_ge(kb.s["ve"], kb.val("ve"))
                ins = v.tensor_scalar(scol[:, 1:2], scol[:, 0:1], 0.8, -1.0,
                                      ALU.add, ALU.mult)
                chain(ins, kb, "ve")

            @b.tensor
            def _(t):
                t.wait_ge(kb.s["ve"], kb.val("ve"))
                ins = t.matmul(psR[0:1, :], scol[:, 1:2], idnf[0:8, 0:8],
                               start=True, stop=True)
                chain(ins, kb, "pe")

            @b.vector
            def _(v):
                v.wait_ge(kb.s["pe"], kb.val("pe"))
                ins = v.tensor_copy(rro[0:1, :], psR[0:1, :])
                chain(ins, kb, "ve")

            @b.tensor
            def _(t):
                t.wait_ge(kb.s["ve"], kb.val("ve"))
                ins = t.matmul(psR[:, :], onesf[0:1, :], rro[0:1, :],
                               start=True, stop=True)
                chain(ins, kb, "pe")

            @b.vector
            def _(v):
                v.wait_ge(kb.s["pe"], kb.val("pe"))
                ins = v.tensor_copy(neglamb[:, :], psR[:, :])
                chain(ins, kb, "ve")
    kb.free(scol, rro)


def emit_l3ab(kb):
    nc = kb.nc
    x = kb.sb["x_res"]
    cst = kb.sb["consts"]
    idn = kb.sb["ident_bf"]
    idnf = kb.sb["ident_f32"]
    onesf = kb.sb["onesrow_f32"]
    ob128 = kb.sb["ones128_bf"]
    QSC = 0.08838834764831845

    q_d = kb.dscratch(kb.uname("l3bq"), [2 * D, T], BF16)
    k_d = kb.dscratch(kb.uname("l3bk"), [2 * D, T], BF16)
    v_d = kb.dscratch(kb.uname("l3bv"), [32, T, 64], BF16)
    vf_d = kb.dscratch(kb.uname("l3bvf"), [2 * D, T], BF16)
    a_d = kb.dscratch(kb.uname("l3bo"), [2 * D, T], BF16)

    lamt = kb.alloc("l3bl", [8, 512], F32)
    neglamb = kb.alloc("l3bnl", [128, 8], F32)
    rmsc = kb.alloc("l3brs", [128, 2], F32)
    with nc.Block() as b:
        @b.sync
        def _(sy):
            i = sy.dma_start(out=lamt[:, :], in_=kb.dram["l3b_lam"])
            dchain(i, kb, "lq")
            i = sy.dma_start(out=rmsc[:, :], in_=kb.dram["l3b_rms"])
            dchain(i, kb, "lq")
    emit_l3ab_lambda(kb, lamt, neglamb)
    kb.free(lamt)

    for mo in (0, 8):
        emit_proj_to_dram(kb, "l3b_wv", vf_d, tag=f"l3bv{mo}", vtm_dram=v_d,
                          n_m=8, m_off=mo)
    for mo in (0, 8):
        emit_proj_to_dram(kb, "l3b_wq", q_d, scale=QSC, tag=f"l3bq{mo}",
                          n_m=8, m_off=mo)
    for mo in (0, 8):
        emit_proj_to_dram(kb, "l3b_wk", k_d, tag=f"l3bk{mo}", n_m=8, m_off=mo)
    kb.free(kb.sb["xn"])

    qb = kb.alloc("l3bqq", [128, 2, 2, T], BF16)
    kbt = kb.alloc("l3bkk", [128, 2, 2, T], BF16)
    vt = kb.alloc("l3bvt", [128, 2, 16, 256], BF16)
    srow = kb.alloc("l3bsr", [128, T], F32)
    prow = kb.alloc("l3bpr", [128, 2, T], BF16)
    ptb = kb.alloc("l3bpt", [128, 2, 16, 128], BF16)
    cols = kb.alloc("l3bc", [128, 8], F32)
    recb = kb.alloc("l3brb", [128, 2, 128], F32)
    rmsb = kb.alloc("l3brm", [128, 128], F32)
    sqb = kb.alloc("l3bsq", [128, 2, 128], BF16)
    oc = kb.alloc("l3boc", [128, 2, 128], F32)
    tv = kb.alloc("l3btv", [128, 128], F32)
    obf = kb.alloc("l3bob", [128, 2, 128], BF16)
    rrow = kb.alloc("l3brr", [1, 128], F32)

    with contextlib.ExitStack() as pc:
        psS = [pc.enter_context(nc.psum_tensor(kb.uname("l3S"), [128, 512], F32))
               for _ in range(2)]
        psT = [pc.enter_context(nc.psum_tensor(kb.uname("l3T"), [128, 128], BF16))
               for _ in range(2)]
        psO = [pc.enter_context(nc.psum_tensor(kb.uname("l3O"), [128, 128], F32))
               for _ in range(2)]
        psR = pc.enter_context(nc.psum_tensor(kb.uname("l3R"), [128, 128], F32))

        for h in range(8):
            par = h % 2
            with nc.Block() as b:
                @b.sync
                def _(sy):
                    if h >= 2:
                        sy.wait_ge(kb.s["hcd"], kb.val("hcd") - 1)
                    qbp, kbp, vtp = qb[:, par], kbt[:, par], vt[:, par]
                    for j in range(2):
                        i = sy.dma_start(out=qbp[:, j, :], in_=q_d.rearrange(
                            "(c p) t -> p c t", p=128)[:, 2 * h + j, :])
                        dchain(i, kb, "lq")
                        i = sy.dma_start(out=kbp[:, j, :], in_=k_d.rearrange(
                            "(c p) t -> p c t", p=128)[:, 2 * h + j, :])
                        dchain(i, kb, "lq")
                    for hs in range(4):
                        i = sy.dma_start(
                            out=vtp[:, :, hs * 64:(hs + 1) * 64],
                            in_=v_d[4 * h + hs].rearrange("(a p) b -> p a b", p=128))
                        dchain(i, kb, "lq")

            lq_t = kb.val("lq")
            for qi in range(16):
                w = (qi + 1) * 128
                nkc = (w + 511) // 512
                nkb = w // 128
                for j in range(2):
                    evA0 = kb.val("evd")
                    with nc.Block() as bj:
                        @bj.tensor
                        def _(t, j=j):
                            t.wait_ge(kb.s["lq"], lq_t)
                            t.wait_ge(kb.s["acs"], kb.val("acs"))
                            for kc in range(nkc):
                                wc = min(512, w - kc * 512)
                                if kc >= 2:
                                    t.wait_ge(kb.s["evd"], evA0 + kc - 1)
                                ins = t.matmul(psS[kc % 2][:, 0:wc],
                                               qb[:, par][:, j, qi * 128:(qi + 1) * 128],
                                               kbt[:, par][:, j, kc * 512:kc * 512 + wc],
                                               start=True, stop=True)
                                ins.then_inc(kb.s["pe"], 1)
                                kb.inc("pe")

                        pe_base = kb.val("pe") - nkc

                        @bj.vector
                        def _(v, j=j):
                            for kc in range(nkc):
                                wc = min(512, w - kc * 512)
                                v.wait_ge(kb.s["pe"], pe_base + kc + 1)
                                ins = v.tensor_copy(srow[:, kc * 512:kc * 512 + wc],
                                                    psS[kc % 2][:, 0:wc])
                                chain(ins, kb, "evd")

                    with nc.Block() as bm:
                        @bm.gpsimd
                        def _(g):
                            g.wait_ge(kb.s["evd"], kb.val("evd"))
                            ins = g.tensor_tensor(srow[:, (w - 128):w],
                                                  srow[:, (w - 128):w],
                                                  kb.sb["cmask"][:], ALU.add)
                            chain(ins, kb, "po")

                    with nc.Block() as bx:
                        @bx.vector
                        def _(v, j=j):
                            v.wait_ge(kb.s["po"], kb.val("po"))
                            ins = v.tensor_reduce(cols[:, j:j + 1], srow[:, 0:w],
                                                  axis=AX.X, op=ALU.max, negate=True)
                            chain(ins, kb, "ve")

                        @bx.scalar
                        def _(s, j=j):
                            s.wait_ge(kb.s["ve"], kb.val("ve"))
                            ins = s.activation(prow[:, j, 0:w], srow[:, 0:w], AF.Exp,
                                               bias=cols[:, j:j + 1],
                                               accum_out=cols[:, 2 + j:3 + j])
                            chain(ins, kb, "acs")

                with nc.Block() as b:
                    @b.vector
                    def _(v):
                        v.wait_ge(kb.s["acs"], kb.val("acs"))
                        v.reciprocal(cols[:, 4:5], cols[:, 2:3])
                        ins = v.reciprocal(cols[:, 5:6], cols[:, 3:4])
                        chain(ins, kb, "ve")

                    for j in range(2):
                        @b.tensor
                        def _(t, j=j):
                            t.wait_ge(kb.s["ve"], kb.val("ve"))
                            ins = t.matmul(psR[0:1, :], cols[:, 4 + j:5 + j], idnf[:, :],
                                           start=True, stop=True)
                            chain(ins, kb, "pe")

                        @b.vector
                        def _(v, j=j):
                            v.wait_ge(kb.s["pe"], kb.val("pe"))
                            ins = v.tensor_copy(rrow[0:1, :], psR[0:1, :])
                            chain(ins, kb, "ve")

                        @b.tensor
                        def _(t, j=j):
                            t.wait_ge(kb.s["ve"], kb.val("ve"))
                            ins = t.matmul(psR[:, :], onesf[0:1, :], rrow[0:1, :],
                                           start=True, stop=True)
                            chain(ins, kb, "pe")

                        @b.vector
                        def _(v, j=j):
                            v.wait_ge(kb.s["pe"], kb.val("pe"))
                            ins = v.tensor_copy(recb[:, j, :], psR[:, :])
                            chain(ins, kb, "ve")

                    for j in range(2):
                        evG0 = kb.val("evd")

                        @b.tensor
                        def _(t, j=j):
                            t.wait_ge(kb.s["acs"], kb.val("acs"))
                            t.wait_ge(kb.s["evd"], evG0)
                            for kbi in range(nkb):
                                if kbi >= 2:
                                    t.wait_ge(kb.s["evd"], evG0 + kbi - 1)
                                ins = t.transpose(psT[kbi % 2][:],
                                                  prow[:, j, kbi * 128:(kbi + 1) * 128],
                                                  idn[:])
                                ins.then_inc(kb.s["pe"], 1)
                                kb.inc("pe")

                        peT = kb.val("pe") - nkb

                        @b.vector
                        def _(v, j=j):
                            for kbi in range(nkb):
                                v.wait_ge(kb.s["pe"], peT + kbi + 1)
                                ins = v.tensor_copy(ptb[:, j][:, kbi, :], psT[kbi % 2][:])
                                chain(ins, kb, "evd")

                    for c in range(2):
                        @b.tensor
                        def _(t, c=c):
                            t.wait_ge(kb.s["evd"], kb.val("evd"))
                            t.wait_ge(kb.s["ve"], kb.val("ve"))
                            for j in range(2):
                                for kbi in range(nkb):
                                    ins = t.matmul(psO[j][:, :],
                                                   vt[:, par][:, kbi, c * 128:(c + 1) * 128],
                                                   ptb[:, j][:, kbi, :],
                                                   start=(kbi == 0), stop=(kbi == nkb - 1))
                                ins.then_inc(kb.s["pe"], 1)
                                kb.inc("pe")

                        @b.vector
                        def _(v, c=c):
                            v.wait_ge(kb.s["pe"], kb.val("pe"))
                            v.tensor_tensor(tv[:, :], psO[0][:], recb[:, 0, :], ALU.mult)
                            ins = v.tensor_tensor(oc[:, c, :], psO[1][:], recb[:, 1, :],
                                                  ALU.mult)
                            ins.then_inc(kb.s["ve"], 1)
                            kb.inc("ve")
                            v.wait_ge(kb.s["ve"], kb.val("ve"))
                            ins = v.scalar_tensor_tensor(oc[:, c, :], oc[:, c, :],
                                                         neglamb[:, h:h + 1], tv[:, :],
                                                         ALU.mult, ALU.add)
                            chain(ins, kb, "ve")

                        @b.scalar
                        def _(s, c=c):
                            s.wait_ge(kb.s["ve"], kb.val("ve"))
                            ins = s.activation(sqb[:, c, :], oc[:, c, :], AF.Square)
                            chain(ins, kb, "acs")

                    @b.tensor
                    def _(t):
                        t.wait_ge(kb.s["acs"], kb.val("acs"))
                        for c in range(2):
                            ins = t.matmul(psR[0:1, :], ob128[:, 0:1], sqb[:, c, :],
                                           start=(c == 0), stop=(c == 1))
                        chain(ins, kb, "pe")

                    @b.scalar
                    def _(s):
                        s.wait_ge(kb.s["pe"], kb.val("pe"))
                        ins = s.activation(rrow[0:1, :], psR[0:1, :], AF.Ln,
                                           scale=cst[0:1, C_1_256:C_1_256 + 1],
                                           bias=cst[0:1, C_EPS:C_EPS + 1])
                        ins.then_inc(kb.s["ac"], 1)
                        kb.inc("ac")
                        s.wait_ge(kb.s["ac"], kb.val("ac"))
                        ins = s.activation(rrow[0:1, :], rrow[0:1, :], AF.Exp,
                                           scale=cst[0:1, C_NEGHALF:C_NEGHALF + 1])
                        chain(ins, kb, "ac")

                    @b.tensor
                    def _(t):
                        t.wait_ge(kb.s["ac"], kb.val("ac"))
                        ins = t.matmul(psR[:, :], onesf[0:1, :], rrow[0:1, :],
                                       start=True, stop=True)
                        chain(ins, kb, "pe")

                    @b.vector
                    def _(v):
                        v.wait_ge(kb.s["pe"], kb.val("pe"))
                        ins = v.tensor_copy(rmsb[:, :], psR[:, :])
                        ins.then_inc(kb.s["ve"], 1)
                        kb.inc("ve")
                        v.wait_ge(kb.s["ve"], kb.val("ve"))
                        for c in range(2):
                            v.tensor_tensor(tv[:, :], oc[:, c, :], rmsb[:, :], ALU.mult)
                            ins = v.tensor_scalar(obf[:, c, :], tv[:, :],
                                                  rmsc[:, c:c + 1], 0.2, ALU.mult,
                                                  ALU.mult)
                            ins.then_inc(kb.s["ve"], 1)
                            kb.inc("ve")
                            v.wait_ge(kb.s["ve"], kb.val("ve"))

                    @b.sync
                    def _(sy):
                        sy.wait_ge(kb.s["ve"], kb.val("ve"))
                        for c in range(2):
                            ins = sy.dma_start(
                                out=a_d[h * 256 + c * 128:h * 256 + (c + 1) * 128,
                                        qi * 128:(qi + 1) * 128],
                                in_=obf[:, c, :])
                            dchain(ins, kb, "spl")

            with nc.Block() as b:
                @b.gpsimd
                def _(g):
                    g.wait_ge(kb.s["spl"], kb.val("spl"))
                    g.wait_ge(kb.s["evd"], kb.val("evd"))
                    i = g.engine_nop()
                    i.then_inc(kb.s["hcd"], 1)
                    kb.inc("hcd")

    kb.free(qb, kbt, vt, srow, prow, ptb, cols, recb, rmsb, sqb, oc, tv, obf,
            rrow, neglamb, rmsc)
    emit_out_proj(kb, a_d, "l3b_wo", n_k=16, tag="l3bp")
    kb.sb["xn"] = kb.alloc("xn", [128, DC, T], BF16)



def emit_l1ffn(kb):
    nc = kb.nc
    x, xn = kb.sb["x_res"], kb.sb["xn"]
    cst = kb.sb["consts"]
    idnf = kb.sb["ident_f32"]
    onesf = kb.sb["onesrow_f32"]

    gate = kb.alloc("l1fg", [128, DC, 8], BF16)
    lg = kb.alloc("l1fl", [128, 16, 8], F32)
    rt = kb.alloc("l1fr", [128, 16, 8], F32)
    cmb = kb.alloc("l1fc", [128, 16, 8], F32)
    rcol = kb.alloc("l1fm", [128, 16, 6], F32)
    combT = kb.alloc("l1fct", [8, T], BF16)
    oh8 = kb.alloc("l1foh", [8, 8 * 128], BF16)
    with nc.Block() as b:
        @b.gpsimd
        def _(g):
            g.wait_ge(kb.s["pw"], kb.val("pw"))
            i = g.dma_start(out=gate[:, :, :],
                            in_=kb.dram["l1f_gate"].rearrange("(c p) m -> p c m", p=128))
            dchain(i, kb, "lw")

        @b.vector
        def _(v):
            v.memset(oh8[:, :], 1.0)
            i = v.engine_nop()
            chain(i, kb, "ve")

        @b.gpsimd
        def _(g):
            g.wait_ge(kb.s["ve"], kb.val("ve"))
            g.affine_select(oh8[:, :], oh8[:, :], pattern=[[1, 8 * 128]],
                            compare_op=ALU.is_ge, fill=0.0, base=0,
                            channel_multiplier=-128)
            i = g.affine_select(oh8[:, :], oh8[:, :], pattern=[[-1, 8 * 128]],
                               compare_op=ALU.is_ge, fill=0.0, base=127,
                               channel_multiplier=128)
            chain(i, kb, "po")

    with contextlib.ExitStack() as pc:
        psL = [pc.enter_context(nc.psum_tensor(kb.uname("l1fL"), [128, 8], F32))
               for _ in range(4)]
        with nc.Block() as b:
            pe_l0 = kb.val("pe")

            ev_lg0 = kb.val("evd")

            @b.tensor
            def _(t):
                t.wait_ge(kb.s["lw"], kb.val("lw"))
                t.wait_ge(kb.s["ln"], kb.val("ln"))
                for tt in range(16):
                    if tt >= 4:
                        t.wait_ge(kb.s["evd"], ev_lg0 + tt - 3)
                    for kc in range(DC):
                        ins = t.matmul(psL[tt % 4][:],
                                       xn[:, kc, tt * 128:(tt + 1) * 128],
                                       gate[:, kc, :],
                                       start=(kc == 0), stop=(kc == DC - 1))
                    ins.then_inc(kb.s["pe"], 1)
                    kb.inc("pe")

            ev_l0 = kb.val("evd")

            @b.vector
            def _(v):
                for tt in range(16):
                    v.wait_ge(kb.s["pe"], pe_l0 + tt + 1)
                    ins = v.tensor_copy(lg[:, tt, :], psL[tt % 4][:])
                    chain(ins, kb, "evd")

    with nc.Block() as b:
        @b.vector
        def _(v):
            v.wait_ge(kb.s["evd"], kb.val("evd"))
            for tt in range(16):
                v.tensor_reduce(rcol[:, tt, 0:1], lg[:, tt, :], axis=AX.X,
                                op=ALU.max, negate=True)
            for tt in range(16):
                v.tensor_scalar(rt[:, tt, :], lg[:, tt, :], rcol[:, tt, 0:1],
                                None, ALU.add)
            for tt in range(16):
                v.tensor_scalar(rt[:, tt, :], rt[:, tt, :], -1e30, 0.0,
                                ALU.is_ge, ALU.bypass) if False else None
            i = v.engine_nop()
            chain(i, kb, "ve")

        @b.gpsimd
        def _(g):
            g.wait_ge(kb.s["ve"], kb.val("ve"))
            for tt in range(16):
                g.tensor_scalar(cmb[:, tt, :], rt[:, tt, :], 0.0, None, ALU.is_ge)
            i = g.engine_nop()
            chain(i, kb, "po")

        @b.vector
        def _(v):
            v.wait_ge(kb.s["po"], kb.val("po"))
            for tt in range(16):
                v.scalar_tensor_tensor(rt[:, tt, :], cmb[:, tt, :], -1e30,
                                       lg[:, tt, :], ALU.mult, ALU.add)
            i = v.engine_nop()
            chain(i, kb, "ve")
            v.wait_ge(kb.s["ve"], kb.val("ve"))
            for tt in range(16):
                v.tensor_reduce(rcol[:, tt, 1:2], rt[:, tt, :], axis=AX.X,
                                op=ALU.max, negate=True)
            i = v.engine_nop()
            chain(i, kb, "ve")

        @b.scalar
        def _(s):
            s.wait_ge(kb.s["ve"], kb.val("ve"))
            for tt in range(16):
                s.activation(lg[:, tt, :], lg[:, tt, :], AF.Exp,
                             bias=rcol[:, tt, 0:1], accum_out=rcol[:, tt, 2:3])
            for tt in range(16):
                s.activation(rcol[:, tt, 3:4], rcol[:, tt, 1:2], AF.Exp,
                             scale=cst[:, C_NEGONE:C_NEGONE + 1],
                             bias=rcol[:, tt, 0:1])
            i = s.activation(rcol[:, 0, 0:1], rcol[:, 0, 0:1], AF.Identity) \
                if False else s.copy(cmb[:, 0, 0:1], cmb[:, 0, 0:1])
            chain(i, kb, "ac")

        @b.vector
        def _(v):
            v.wait_ge(kb.s["ac"], kb.val("ac"))
            for tt in range(16):
                v.tensor_scalar(rcol[:, tt, 4:5], rcol[:, tt, 2:3], 1e-20, 1.0,
                                ALU.mult, ALU.add)
            for tt in range(16):
                v.tensor_tensor(rcol[:, tt, 4:5], rcol[:, tt, 4:5],
                                rcol[:, tt, 3:4], ALU.add)
            for tt in range(16):
                v.reciprocal(rcol[:, tt, 5:6], rcol[:, tt, 4:5])
            i = v.engine_nop()
            chain(i, kb, "ve")

        @b.gpsimd
        def _(g):
            g.wait_ge(kb.s["ve"], kb.val("ve"))
            for tt in range(16):
                g.tensor_scalar(rt[:, tt, :], rt[:, tt, :], rcol[:, tt, 1:2],
                                None, ALU.add)
            i = g.engine_nop()
            chain(i, kb, "po")

        @b.vector
        def _(v):
            v.wait_ge(kb.s["po"], kb.val("po"))
            for tt in range(16):
                v.tensor_scalar(rt[:, tt, :], rt[:, tt, :], 0.0, None, ALU.is_ge)
            i = v.engine_nop()
            chain(i, kb, "ve")
            v.wait_ge(kb.s["ve"], kb.val("ve"))
            for tt in range(16):
                v.tensor_tensor(rcol[:, tt, 3:4], rcol[:, tt, 3:4],
                                rcol[:, tt, 5:6], ALU.mult)
            i = v.engine_nop()
            chain(i, kb, "ve")
            v.wait_ge(kb.s["ve"], kb.val("ve"))
            for tt in range(16):
                v.tensor_scalar(cmb[:, tt, :], cmb[:, tt, :], rcol[:, tt, 5:6],
                                None, ALU.mult)
            i = v.engine_nop()
            chain(i, kb, "ve")
            v.wait_ge(kb.s["ve"], kb.val("ve"))
            for tt in range(16):
                v.scalar_tensor_tensor(cmb[:, tt, :], rt[:, tt, :],
                                       rcol[:, tt, 3:4], cmb[:, tt, :],
                                       ALU.mult, ALU.add)
            i = v.engine_nop()
            chain(i, kb, "ve")

    cmbb = kb.alloc("l1fcbb", [128, 16, 8], BF16)
    with nc.Block() as b:
        @b.vector
        def _(v):
            v.wait_ge(kb.s["ve"], kb.val("ve"))
            for tt in range(16):
                ins = v.tensor_copy(cmbb[:, tt, :], cmb[:, tt, :])
            chain(ins, kb, "ve")
    with contextlib.ExitStack() as pc:
        psC = [pc.enter_context(nc.psum_tensor(kb.uname("l1fT"), [8, 128], BF16))
               for _ in range(2)]
        with nc.Block() as b:
            pe_c0 = kb.val("pe")
            ev_c0 = kb.val("evd")

            @b.tensor
            def _(t):
                t.wait_ge(kb.s["ve"], kb.val("ve"))
                for tt in range(16):
                    if tt >= 2:
                        t.wait_ge(kb.s["evd"], ev_c0 + tt - 1)
                    ins = t.transpose(psC[tt % 2][:], cmbb[:, tt, :], kb.sb["ident_bf"][:, :])
                    ins.then_inc(kb.s["pe"], 1)
                    kb.inc("pe")

            @b.vector
            def _(v):
                for tt in range(16):
                    v.wait_ge(kb.s["pe"], pe_c0 + tt + 1)
                    ins = v.tensor_copy(combT[:, tt * 128:(tt + 1) * 128],
                                        psC[tt % 2][:])
                    chain(ins, kb, "evd")

    if _os.environ.get("L1F_DBG"):
        dbg_l = kb.nc.dram_tensor("dbg_lg", [128, 16, 8], F32, kind="ExternalOutput").ap()
        dbg_c = kb.nc.dram_tensor("dbg_cmb", [128, 16, 8], F32, kind="ExternalOutput").ap()
        with nc.Block() as b:
            @b.gpsimd
            def _(g):
                g.wait_ge(kb.s["ve"], kb.val("ve"))
                i = g.dma_start(out=dbg_l, in_=lg[:, :, :])
                dchain(i, kb, "spl")
                i = g.dma_start(out=dbg_c, in_=cmb[:, :, :])
                dchain(i, kb, "spl")
                g.wait_ge(kb.s["spl"], kb.val("spl"))
    kb.free(gate, lg, rt, rcol)

    xn_d = kb.dscratch(kb.uname("l1fx"), [D, T], BF16)
    with nc.Block() as b:
        @b.sync
        def _(sy):
            sy.wait_ge(kb.s["ve"], kb.val("ve"))
            i = sy.dma_start(out=xn_d.rearrange("(c p) t -> p c t", p=128),
                             in_=xn[:, :, :])
            dchain(i, kb, "spl")

        @b.gpsimd
        def _(g):
            g.wait_ge(kb.s["spl"], kb.val("spl"))
            i = g.engine_nop()
            i.then_inc(kb.s["pw"], 1)
            kb.inc("pw")
    kb.free(kb.sb["xn"])

    wbuf = kb.alloc("l1fw", [128, 16, 1024], BF16)
    hbuf = kb.alloc("l1fh", [128, 16, T], BF16)
    bcol = kb.alloc("l1fb", [128, 16], F32)
    cbc = kb.alloc("l1fcb", [128, T], BF16)
    tvb = kb.alloc("l1ftv", [128, 4, 512], F32)
    xnch = kb.alloc("l1fxc", [128, 2, DC, 512], BF16)

    def expert(e):
        w1n, w2n = ("l1f_w1", "l1f_w2") if e < 8 else ("l1f_ws1", "l1f_ws2")
        with nc.Block() as b:
            @b.gpsimd
            def _(g):
                g.wait_ge(kb.s["pw"], kb.val("pw"))
                src = kb.dram[w1n]
                if e < 8:
                    src = src[e]
                wv = wbuf[:, :, :].rearrange("p a m -> p (a m)").rearrange(
                    "p (c m) -> p c m", c=DC, m=2048)
                i = g.dma_start(out=wv, in_=src.rearrange("(c p) m -> p c m", p=128))
                dchain(i, kb, "lw")

            @b.sync
            def _(sy):
                bsrc = kb.dram["l1f_b1"][e] if e < 8 else kb.dram["l1f_bs1"]
                i = sy.dma_start(out=bcol[:, :], in_=bsrc)
                dchain(i, kb, "lq")

        wv = wbuf[:, :, :].rearrange("p a m -> p (a m)").rearrange(
            "p (c m) -> p c m", c=DC, m=2048)

        def evac_h(b, key, ps, pe_t):
            fc, tc = key

            @b.scalar
            def _(s):
                s.wait_ge(kb.s["pe"], pe_t)
                ins = s.activation(hbuf[:, fc, tc * 512:(tc + 1) * 512], ps[:],
                                   AF.Relu, bias=bcol[:, fc:fc + 1])
                chain(ins, kb, "evd")

        for tc in range(4):
            with nc.Block() as b:
                @b.sync
                def _(sy):
                    if tc >= 2 or e > 0:
                        sy.wait_ge(kb.s["pe"], kb.val("pe"))
                    i = sy.dma_start(
                        out=xnch[:, tc % 2],
                        in_=xn_d.rearrange("(c p) t -> p c t", p=128)[
                            :, :, tc * 512:(tc + 1) * 512])
                    dchain(i, kb, "lq")
            tiles_h = [(fc, tc) for fc in range(16)]
            emit_mm(kb, tiles_h,
                    stat_fn=lambda key, kc: wv[:, kc, key[0] * 128:(key[0] + 1) * 128],
                    mov_fn=lambda key, kc, tc=tc: xnch[:, tc % 2][:, kc, :],
                    nk_fn=lambda key: DC, evac_fn=evac_h, tag=f"l1fh{e}_{tc}",
                    waits=[("lw", kb.val("lw")), ("lq", kb.val("lq"))],
                    wait_ln=False)

        with nc.Block() as b:
            @b.gpsimd
            def _(g):
                g.wait_ge(kb.s["evd"], kb.val("evd"))
                src = kb.dram[w2n]
                if e < 8:
                    src = src[e]
                i = g.dma_start(out=wbuf[:, :, 0:1024],
                                in_=src.rearrange("(c p) m -> p c m", p=128))
                dchain(i, kb, "lw")

            @b.sync
            def _(sy):
                sy.wait_ge(kb.s["evd"], kb.val("evd"))
                bsrc = kb.dram["l1f_b2"][e] if e < 8 else kb.dram["l1f_bs2"]
                i = sy.dma_start(out=bcol[:, 0:8], in_=bsrc)
                dchain(i, kb, "lq")

        if e < 8:
            with contextlib.ExitStack() as pc:
                psB = pc.enter_context(nc.psum_tensor(kb.uname(f"l1fB{e}"), [128, 512], F32))
                with nc.Block() as b:
                    pe_b0 = kb.val("pe")
                    ev_b0 = kb.val("evd")

                    @b.tensor
                    def _(t):
                        t.wait_ge(kb.s["ve"], kb.val("ve"))
                        for tc in range(4):
                            if tc >= 1:
                                t.wait_ge(kb.s["evd"], ev_b0 + tc)
                            ins = t.matmul(psB[:], oh8[:, e * 128:(e + 1) * 128],
                                           combT[:, tc * 512:(tc + 1) * 512],
                                           start=True, stop=True)
                            ins.then_inc(kb.s["pe"], 1)
                            kb.inc("pe")

                    @b.vector
                    def _(v):
                        for tc in range(4):
                            v.wait_ge(kb.s["pe"], pe_b0 + tc + 1)
                            ins = v.tensor_copy(cbc[:, tc * 512:(tc + 1) * 512], psB[:])
                            chain(ins, kb, "evd")

        w2v = wbuf[:, :, 0:1024]

        def evac_eo(b, key, ps, pe_t):
            mc, tc = key
            slot = (mc * 4 + tc) % 4

            @b.vector
            def _(v):
                v.wait_ge(kb.s["pe"], pe_t)
                if e < 8:
                    ins = v.scalar_tensor_tensor(tvb[:, slot, :], ps[:],
                                                 bcol[:, mc:mc + 1],
                                                 cbc[:, tc * 512:(tc + 1) * 512],
                                                 ALU.add, ALU.mult)
                else:
                    ins = v.tensor_scalar(tvb[:, slot, :], ps[:], bcol[:, mc:mc + 1],
                                          None, ALU.add)
                chain(ins, kb, "evd")

            @b.gpsimd
            def _(g2):
                g2.wait_ge(kb.s["evd"], kb.val("evd"))
                xs = x[:, mc, tc * 512:(tc + 1) * 512]
                ins = g2.tensor_tensor(xs, xs, tvb[:, slot, :], ALU.add)
                chain(ins, kb, "po")

        tiles_eo = [(mc, tc) for mc in range(DC) for tc in range(4)]
        emit_mm(kb, tiles_eo,
                stat_fn=lambda key, kc: w2v[:, kc, key[0] * 128:(key[0] + 1) * 128],
                mov_fn=lambda key, kc: hbuf[:, kc, key[1] * 512:(key[1] + 1) * 512],
                nk_fn=lambda key: 16, evac_fn=evac_eo, tag=f"l1fe{e}",
                waits=[("lw", kb.val("lw")), ("lq", kb.val("lq"))], wait_ln=False)
        with nc.Block() as b:
            @b.gpsimd
            def _(g):
                g.wait_ge(kb.s["po"], kb.val("po"))
                i = g.engine_nop()
                i.then_inc(kb.s["pw"], 1)
                kb.inc("pw")

    for e in range(9):
        expert(e)
    with nc.Block() as b:
        @b.gpsimd
        def _(g):
            g.wait_ge(kb.s["po"], kb.val("po"))
            i = g.engine_nop()
            i.then_inc(kb.s["xr"], 1)
            kb.inc("xr")
    kb.free(wbuf, hbuf, bcol, cbc, tvb, cmb, cmbb, combT, oh8, xnch)
    kb.sb["xn"] = kb.alloc("xn", [128, DC, T], BF16)



def emit_l0ab(kb):
    nc = kb.nc
    x, xn = kb.sb["x_res"], kb.sb["xn"]
    cst = kb.sb["consts"]
    idn = kb.sb["ident_bf"]
    idnf = kb.sb["ident_f32"]
    onesf = kb.sb["onesrow_f32"]
    a_d = kb.dscratch(kb.uname("l0bo"), [D, T], BF16)

    xn_d = kb.dscratch(kb.uname("l0bx"), [D, T], BF16)
    with nc.Block() as b:
        @b.sync
        def _(sy):
            sy.wait_ge(kb.s["ln"], kb.val("ln"))
            i = sy.dma_start(out=xn_d.rearrange("(c p) t -> p c t", p=128),
                             in_=xn[:, :, :])
            dchain(i, kb, "spl")

        @b.gpsimd
        def _(g):
            g.wait_ge(kb.s["spl"], kb.val("spl"))
            i = g.engine_nop()
            i.then_inc(kb.s["pw"], 1)
            kb.inc("pw")
    kb.free(kb.sb["xn"])
    xnch = kb.alloc("l0bxc", [128, 2, DC, 512], BF16)
    wq = kb.alloc("l0bw", [128, DC, 384], BF16)
    qkv = kb.alloc("l0bqk", [64, 6, T], BF16)
    vctm = kb.alloc("l0bvt", [128, 16, 64], BF16)
    laT = kb.alloc("l0bla", [128, 16, T], BF16)
    t1b = kb.alloc("l0bt1", [128, 2, 128], BF16)
    srow = kb.alloc("l0bsr", [128, T], F32)
    prow = kb.alloc("l0bpr", [128, T], BF16)
    ptb = kb.alloc("l0bpt", [128, 16, 128], BF16)
    cols = kb.alloc("l0bc", [128, 4], F32)
    recb = kb.alloc("l0brc", [64, 128], F32)
    outv = kb.alloc("l0bov", [64, 128], F32)
    rrow = kb.alloc("l0brr", [1, 128], F32)

    for h in range(16):
        with nc.Block() as b:
            @b.gpsimd
            def _(g):
                g.wait_ge(kb.s["pw"], kb.val("pw"))
                i = g.dma_start(out=wq[:, :, :],
                                in_=kb.dram["l0b_wqkv"][h].rearrange(
                                    "(c p) m -> p c m", p=128))
                dchain(i, kb, "lw")

        def evac_q(b, key, ps, pe_t):
            sidx, tc = key

            @b.vector
            def _(v):
                v.wait_ge(kb.s["pe"], pe_t)
                o = qkv[:, sidx, tc * 512:(tc + 1) * 512]
                if sidx in (0, 3):
                    ins = v.tensor_scalar(o, ps[0:64, :], 0.125, None, ALU.mult)
                else:
                    ins = v.tensor_copy(o, ps[0:64, :])
                chain(ins, kb, "evd")

        for tc in range(4):
            with nc.Block() as b:
                @b.sync
                def _(sy):
                    if tc >= 2 or h > 0:
                        sy.wait_ge(kb.s["pe"], kb.val("pe"))
                    i = sy.dma_start(
                        out=xnch[:, tc % 2],
                        in_=xn_d.rearrange("(c p) t -> p c t", p=128)[
                            :, :, tc * 512:(tc + 1) * 512])
                    dchain(i, kb, "lq")
            tiles = [(sidx, tc) for sidx in range(6)]
            emit_mm(kb, tiles,
                    stat_fn=lambda key, kc: wq[:, kc, key[0] * 64:(key[0] + 1) * 64],
                    mov_fn=lambda key, kc, tc=tc: xnch[:, tc % 2][:, kc, :],
                    nk_fn=lambda key: DC, evac_fn=evac_q, tag=f"l0bq{h}_{tc}",
                    waits=[("lw", kb.val("lw")), ("lq", kb.val("lq"))], wait_ln=False,
                    out_rows=64)

        qu = qkv[:, 0]
        ku = qkv[:, 1]
        vu = qkv[:, 2]
        qc = qkv[:, 3]
        kcn = qkv[:, 4]
        vc = qkv[:, 5]

        with contextlib.ExitStack() as pc1:
            psT = pc1.enter_context(nc.psum_tensor(kb.uname("l0bT"), [128, 128], BF16))
            psL = [pc1.enter_context(nc.psum_tensor(kb.uname("l0bL"), [128, 512], F32))
                   for _ in range(2)]

            with nc.Block() as b:
                pe_v0 = kb.val("pe")
                ev_v0 = kb.val("evd")

                @b.tensor
                def _(t):
                    t.wait_ge(kb.s["evd"], ev_v0)
                    for tb in range(16):
                        if tb >= 1:
                            t.wait_ge(kb.s["evd"], ev_v0 + tb)
                        ins = t.transpose(psT[0:128, 0:64],
                                          vc[:, tb * 128:(tb + 1) * 128],
                                          idn[0:64, 0:64])
                        ins.then_inc(kb.s["pe"], 1)
                        kb.inc("pe")

                @b.vector
                def _(v):
                    for tb in range(16):
                        v.wait_ge(kb.s["pe"], pe_v0 + tb + 1)
                        ins = v.tensor_copy(vctm[:, tb, :], psT[:, 0:64])
                        chain(ins, kb, "evd")

            jobs = []
            for J in range(16):
                wk = (J + 1) * 128
                for kc in range((wk + 511) // 512):
                    jobs.append((J, kc, min(512, wk - kc * 512)))
            with nc.Block() as b:
                acs_l0 = kb.val("acs")
                ev_l0 = kb.val("evd")

                @b.tensor
                def _(t):
                    t.wait_ge(kb.s["evd"], ev_l0)
                    for n_mm, (J, kc, wc) in enumerate(jobs):
                        if n_mm >= 2:
                            t.wait_ge(kb.s["acs"], acs_l0 + n_mm - 1)
                        ins = t.matmul(psL[n_mm % 2][:, 0:wc],
                                       ku[:, J * 128:(J + 1) * 128],
                                       qu[:, kc * 512:kc * 512 + wc],
                                       start=True, stop=True)
                        ins.then_inc(kb.s["pe"], 1)
                        kb.inc("pe")

                pe_l0 = kb.val("pe") - len(jobs)

                @b.scalar
                def _(sc_):
                    for n_mm, (J, kc, wc) in enumerate(jobs):
                        sc_.wait_ge(kb.s["pe"], pe_l0 + n_mm + 1)
                        ins = sc_.activation(laT[:, J, kc * 512:kc * 512 + wc],
                                             psL[n_mm % 2][:, 0:wc], AF.Sigmoid)
                        chain(ins, kb, "acs")

            with nc.Block() as b:
                @b.gpsimd
                def _(g):
                    g.wait_ge(kb.s["acs"], kb.val("acs"))
                    for J in range(16):
                        ins = g.tensor_tensor(laT[:, J, J * 128:(J + 1) * 128],
                                              laT[:, J, J * 128:(J + 1) * 128],
                                              kb.sb["smask"][:, :], ALU.mult)
                    chain(ins, kb, "po")

        with contextlib.ExitStack() as pc2:
            psU = [pc2.enter_context(nc.psum_tensor(kb.uname("l0bU"), [128, 512], F32))
                   for _ in range(4)]
            psT2 = [pc2.enter_context(nc.psum_tensor(kb.uname("l0bT2"), [128, 128], BF16))
                    for _ in range(2)]
            psM = [pc2.enter_context(nc.psum_tensor(kb.uname("l0bM"), [128, 128], F32))
                   for _ in range(2)]
            psO = psM[0]
            psR = psM[1]

            for qi in range(16):
                w = (qi + 1) * 128
                nkc = (w + 511) // 512
                nkb = qi + 1
                with nc.Block() as b:
                    pe_s0 = kb.val("pe")
                    ev_s0 = kb.val("evd")
                    pe_t1 = {}

                    @b.tensor
                    def _(t):
                        t.wait_ge(kb.s["po"], kb.val("po"))
                        t.wait_ge(kb.s["acs"], kb.val("acs"))
                        t.wait_ge(kb.s["evd"], ev_s0)
                        for J in range(qi + 1):
                            ins = t.matmul(psM[J % 2][:],
                                           vu[:, J * 128:(J + 1) * 128],
                                           qc[:, qi * 128:(qi + 1) * 128],
                                           start=True, stop=True)
                            ins.then_inc(kb.s["pe"], 1)
                            pe_t1[J] = kb.inc("pe")
                            t.wait_ge(kb.s["evd"], ev_s0 + J + 1)
                            for kc in range(min(J // 4, qi // 4) + 1):
                                wc = min(512, w - kc * 512)
                                wcj = min(wc, (J + 1) * 128 - kc * 512)
                                ins = t.matmul(psU[kc][:, 0:wcj], t1b[:, J % 2, :],
                                               laT[:, J, kc * 512:kc * 512 + wcj],
                                               start=(J == 4 * kc), stop=(J == qi))
                                ins.then_inc(kb.s["pe"], 1)
                                kb.inc("pe")

                    @b.vector
                    def _(v):
                        for J in range(qi + 1):
                            v.wait_ge(kb.s["pe"], pe_t1[J])
                            if J == qi:
                                ins = v.tensor_tensor(t1b[:, J % 2, :], psM[J % 2][:],
                                                      kb.sb["lemask"][:, :], ALU.mult)
                            else:
                                ins = v.tensor_copy(t1b[:, J % 2, :], psM[J % 2][:])
                            chain(ins, kb, "evd")

                    pe_su = kb.val("pe")

                    @b.scalar
                    def _(sc_):
                        sc_.wait_ge(kb.s["pe"], pe_su)
                        for kc in range(nkc):
                            wc = min(512, w - kc * 512)
                            ins = sc_.activation(srow[:, kc * 512:kc * 512 + wc],
                                                 psU[kc][:, 0:wc], AF.Silu)
                        chain(ins, kb, "acs")

                with nc.Block() as b:
                    pe_c0 = kb.val("pe")

                    @b.tensor
                    def _(t):
                        t.wait_ge(kb.s["acs"], kb.val("acs"))
                        for kc in range(nkc):
                            wc = min(512, w - kc * 512)
                            ins = t.matmul(psU[kc][:, 0:wc],
                                           qc[:, qi * 128:(qi + 1) * 128],
                                           kcn[:, kc * 512:kc * 512 + wc],
                                           start=True, stop=True)
                            ins.then_inc(kb.s["pe"], 1)
                            kb.inc("pe")

                    @b.vector
                    def _(v):
                        for kc in range(nkc):
                            wc = min(512, w - kc * 512)
                            v.wait_ge(kb.s["pe"], pe_c0 + kc + 1)
                            sr = srow[:, kc * 512:kc * 512 + wc]
                            ins = v.tensor_tensor(sr, psU[kc][:, 0:wc], sr, ALU.subtract)
                            chain(ins, kb, "evd")

                if _os.environ.get("L0B_DBG") and h == 0 and qi == 1:
                    dbg_s = kb.nc.dram_tensor("dbg_srow", [128, 512], F32,
                                              kind="ExternalOutput").ap()
                    dbg_la = kb.nc.dram_tensor("dbg_la", [128, 2, T], BF16,
                                               kind="ExternalOutput").ap()
                    dbg_t1 = kb.nc.dram_tensor("dbg_t1", [128, 2, 128], BF16,
                                               kind="ExternalOutput").ap()
                    dbg_qk = kb.nc.dram_tensor("dbg_qkv", [64, 6, T], BF16,
                                               kind="ExternalOutput").ap()
                    with nc.Block() as bd:
                        @bd.gpsimd
                        def _(g):
                            g.wait_ge(kb.s["evd"], kb.val("evd"))
                            i = g.dma_start(out=dbg_s, in_=srow[:, 0:512])
                            dchain(i, kb, "spl")
                            i = g.dma_start(out=dbg_la, in_=laT[:, 0:2, :])
                            dchain(i, kb, "spl")
                            i = g.dma_start(out=dbg_t1, in_=t1b[:, :, :])
                            dchain(i, kb, "spl")
                            i = g.dma_start(out=dbg_qk, in_=qkv[:, :, :])
                            dchain(i, kb, "spl")
                            g.wait_ge(kb.s["spl"], kb.val("spl"))

                with nc.Block() as b:
                    @b.gpsimd
                    def _(g):
                        g.wait_ge(kb.s["evd"], kb.val("evd"))
                        ins = g.tensor_tensor(srow[:, (w - 128):w], srow[:, (w - 128):w],
                                              kb.sb["cmask"][:], ALU.add)
                        chain(ins, kb, "po")

                with nc.Block() as b:
                    @b.vector
                    def _(v):
                        v.wait_ge(kb.s["po"], kb.val("po"))
                        ins = v.tensor_reduce(cols[:, 0:1], srow[:, 0:w], axis=AX.X,
                                              op=ALU.max, negate=True)
                        chain(ins, kb, "ve")

                    @b.scalar
                    def _(sc_):
                        sc_.wait_ge(kb.s["ve"], kb.val("ve"))
                        ins = sc_.activation(prow[:, 0:w], srow[:, 0:w], AF.Exp,
                                             bias=cols[:, 0:1], accum_out=cols[:, 1:2])
                        chain(ins, kb, "acs")

                    @b.vector
                    def _(v):
                        v.wait_ge(kb.s["acs"], kb.val("acs"))
                        ins = v.reciprocal(cols[:, 2:3], cols[:, 1:2])
                        chain(ins, kb, "ve")

                    @b.tensor
                    def _(t):
                        t.wait_ge(kb.s["ve"], kb.val("ve"))
                        ins = t.matmul(psR[0:1, :], cols[:, 2:3], idnf[:, :],
                                       start=True, stop=True)
                        chain(ins, kb, "pe")

                    @b.vector
                    def _(v):
                        v.wait_ge(kb.s["pe"], kb.val("pe"))
                        ins = v.tensor_copy(rrow[0:1, :], psR[0:1, :])
                        chain(ins, kb, "ve")

                    @b.tensor
                    def _(t):
                        t.wait_ge(kb.s["ve"], kb.val("ve"))
                        ins = t.matmul(psR[0:64, :], onesf[0:1, 0:64], rrow[0:1, :],
                                       start=True, stop=True)
                        chain(ins, kb, "pe")

                    @b.vector
                    def _(v):
                        v.wait_ge(kb.s["pe"], kb.val("pe"))
                        ins = v.tensor_copy(recb[:, :], psR[0:64, :])
                        chain(ins, kb, "ve")

                    evG0 = kb.val("evd")

                    @b.tensor
                    def _(t):
                        t.wait_ge(kb.s["acs"], kb.val("acs"))
                        for kbi in range(nkb):
                            if kbi >= 2:
                                t.wait_ge(kb.s["evd"], evG0 + kbi - 1)
                            ins = t.transpose(psT2[kbi % 2][:],
                                              prow[:, kbi * 128:(kbi + 1) * 128], idn[:])
                            ins.then_inc(kb.s["pe"], 1)
                            kb.inc("pe")

                    peT = kb.val("pe") - nkb

                    @b.vector
                    def _(v):
                        for kbi in range(nkb):
                            v.wait_ge(kb.s["pe"], peT + kbi + 1)
                            ins = v.tensor_copy(ptb[:, kbi, :], psT2[kbi % 2][:])
                            chain(ins, kb, "evd")

                    @b.tensor
                    def _(t):
                        t.wait_ge(kb.s["evd"], kb.val("evd"))
                        for kbi in range(nkb):
                            ins = t.matmul(psO[0:64, :], vctm[:, kbi, :], ptb[:, kbi, :],
                                           start=(kbi == 0), stop=(kbi == nkb - 1))
                        ins.then_inc(kb.s["pe"], 1)
                        kb.inc("pe")

                    @b.vector
                    def _(v):
                        v.wait_ge(kb.s["pe"], kb.val("pe"))
                        ins = v.tensor_tensor(outv[:, :], psO[0:64, :], recb[:, :], ALU.mult)
                        chain(ins, kb, "evd")

                    @b.gpsimd
                    def _(g):
                        g.wait_ge(kb.s["evd"], kb.val("evd"))
                        ins = g.dma_start(
                            out=a_d[h * 64:(h + 1) * 64, qi * 128:(qi + 1) * 128],
                            in_=outv[:, :])
                        dchain(ins, kb, "spl")

        with nc.Block() as b:
            @b.gpsimd
            def _(g):
                g.wait_ge(kb.s["spl"], kb.val("spl"))
                g.wait_ge(kb.s["evd"], kb.val("evd"))
                i = g.engine_nop()
                i.then_inc(kb.s["pw"], 1)
                kb.inc("pw")

    kb.free(wq, qkv, vctm, laT, t1b, srow, prow, ptb, cols, recb, outv, rrow, xnch)
    emit_out_proj(kb, a_d, "l0b_wout", tag="l0bp")
    kb.sb["xn"] = kb.alloc("xn", [128, DC, T], BF16)



def emit_l3ffn(kb):
    nc = kb.nc
    x, xn = kb.sb["x_res"], kb.sb["xn"]
    cst = kb.sb["consts"]
    idnf = kb.sb["ident_f32"]
    onesf = kb.sb["onesrow_f32"]
    ob128 = kb.sb["ones128_bf"]
    E = 16

    kf = kb.alloc("l3fk", [128, DC, E], F32)
    rwf = kb.alloc("l3fr", [128, DC, E], F32)
    krw = kb.alloc("l3fkr", [128, DC, E], BF16)
    ksq = kb.alloc("l3fks", [128, DC, E], BF16)
    srow = kb.alloc("l3fsr", [1, 64], F32)
    crowb = kb.alloc("l3fcw", [128, E], F32)
    lg = kb.alloc("l3flg", [128, 16, E], F32)
    rt = kb.alloc("l3frt", [128, 16, E], F32)
    cmb = kb.alloc("l3fcm", [128, 16, E], F32)
    rcol = kb.alloc("l3fmc", [128, 16, 6], F32)
    cmbb = kb.alloc("l3fcb", [128, 16, E], BF16)
    combT = kb.alloc("l3fct", [E, T], BF16)
    oh16 = kb.alloc("l3foh", [E, E * 128], BF16)

    with nc.Block() as b:
        @b.sync
        def _(sy):
            i = sy.dma_start(out=kf[:, :, :],
                             in_=kb.dram["l3f_keys"].rearrange("(c p) m -> p c m", p=128))
            dchain(i, kb, "lq")
            i = sy.dma_start(out=rwf[:, :, :],
                             in_=kb.dram["l3f_rw"].rearrange("(c p) m -> p c m", p=128))
            dchain(i, kb, "lq")
            i = sy.dma_start(out=srow[0:1, 0:16], in_=kb.dram["l3f_rb"])
            dchain(i, kb, "lq")

        @b.vector
        def _(v):
            v.memset(oh16[:, :], 1.0)
            v.wait_ge(kb.s["lq"], kb.val("lq"))
            for dc in range(DC):
                v.scalar_tensor_tensor(krw[:, dc, :], kf[:, dc, :], 2.0,
                                       rwf[:, dc, :], ALU.mult, ALU.add)
            i = v.engine_nop()
            chain(i, kb, "ve")

        @b.scalar
        def _(s):
            s.wait_ge(kb.s["lq"], kb.val("lq"))
            for dc in range(DC):
                i = s.activation(ksq[:, dc, :], kf[:, dc, :], AF.Square)
            chain(i, kb, "ac")

        @b.gpsimd
        def _(g):
            g.wait_ge(kb.s["ve"], kb.val("ve"))
            g.affine_select(oh16[:, :], oh16[:, :], pattern=[[1, E * 128]],
                            compare_op=ALU.is_ge, fill=0.0, base=0,
                            channel_multiplier=-128)
            i = g.affine_select(oh16[:, :], oh16[:, :], pattern=[[-1, E * 128]],
                               compare_op=ALU.is_ge, fill=0.0, base=127,
                               channel_multiplier=128)
            chain(i, kb, "po")

    with contextlib.ExitStack() as pc:
        psK = pc.enter_context(nc.psum_tensor(kb.uname("l3fK"), [128, E], F32))
        with nc.Block() as b:
            @b.tensor
            def _(t):
                t.wait_ge(kb.s["ac"], kb.val("ac"))
                for dc in range(DC):
                    i = t.matmul(psK[0:1, :], ob128[:, 0:1], ksq[:, dc, :],
                                 start=(dc == 0), stop=(dc == DC - 1))
                chain(i, kb, "pe")

            @b.vector
            def _(v):
                v.wait_ge(kb.s["pe"], kb.val("pe"))
                ins = v.tensor_copy(srow[0:1, 16:32], psK[0:1, :])
                ins.then_inc(kb.s["ve"], 1)
                kb.inc("ve")
                v.wait_ge(kb.s["ve"], kb.val("ve"))
                ins = v.tensor_tensor(srow[0:1, 32:48], srow[0:1, 0:16],
                                      srow[0:1, 16:32], ALU.subtract)
                chain(ins, kb, "ve")

            @b.tensor
            def _(t):
                t.wait_ge(kb.s["ve"], kb.val("ve"))
                ins = t.matmul(psK[:, :], onesf[0:1, :], srow[0:1, 32:48],
                               start=True, stop=True)
                chain(ins, kb, "pe")

            @b.vector
            def _(v):
                v.wait_ge(kb.s["pe"], kb.val("pe"))
                ins = v.tensor_copy(crowb[:, :], psK[:, :])
                chain(ins, kb, "ve")

    with contextlib.ExitStack() as pc:
        psL = [pc.enter_context(nc.psum_tensor(kb.uname("l3fL"), [128, E], F32))
               for _ in range(4)]
        with nc.Block() as b:
            pe_l0 = kb.val("pe")
            ev_l0 = kb.val("evd")

            @b.tensor
            def _(t):
                t.wait_ge(kb.s["ve"], kb.val("ve"))
                t.wait_ge(kb.s["ln"], kb.val("ln"))
                for tt in range(16):
                    if tt >= 4:
                        t.wait_ge(kb.s["evd"], ev_l0 + tt - 3)
                    for kc in range(DC):
                        ins = t.matmul(psL[tt % 4][:],
                                       xn[:, kc, tt * 128:(tt + 1) * 128],
                                       krw[:, kc, :],
                                       start=(kc == 0), stop=(kc == DC - 1))
                    ins.then_inc(kb.s["pe"], 1)
                    kb.inc("pe")

            @b.vector
            def _(v):
                for tt in range(16):
                    v.wait_ge(kb.s["pe"], pe_l0 + tt + 1)
                    ins = v.tensor_tensor(lg[:, tt, :], psL[tt % 4][:], crowb[:, :],
                                          ALU.add)
                    chain(ins, kb, "evd")

    with nc.Block() as b:
        @b.vector
        def _(v):
            v.wait_ge(kb.s["evd"], kb.val("evd"))
            for tt in range(16):
                v.tensor_copy(rt[:, tt, :], lg[:, tt, :])
            i = v.engine_nop()
            chain(i, kb, "ve")

    for rnd in range(4):
        with nc.Block() as b:
            @b.vector
            def _(v):
                v.wait_ge(kb.s["ve"], kb.val("ve"))
                if rnd > 0:
                    v.wait_ge(kb.s["po"], kb.val("po"))
                for tt in range(16):
                    v.tensor_reduce(rcol[:, tt, 1:2], rt[:, tt, :], axis=AX.X,
                                    op=ALU.max, negate=True)
                i = v.engine_nop()
                chain(i, kb, "ve")
                if rnd == 0:
                    v.wait_ge(kb.s["ve"], kb.val("ve"))
                    for tt in range(16):
                        v.tensor_copy(rcol[:, tt, 0:1], rcol[:, tt, 1:2])
                    i = v.engine_nop()
                    chain(i, kb, "ve")

            if rnd < 3:
                @b.gpsimd
                def _(g):
                    g.wait_ge(kb.s["ve"], kb.val("ve"))
                    for tt in range(16):
                        g.tensor_scalar(cmb[:, tt, :], rt[:, tt, :],
                                        rcol[:, tt, 1:2], None, ALU.add)
                    i = g.engine_nop()
                    chain(i, kb, "po")

                @b.vector
                def _(v):
                    v.wait_ge(kb.s["po"], kb.val("po"))
                    for tt in range(16):
                        v.tensor_scalar(cmb[:, tt, :], cmb[:, tt, :], 0.0, None,
                                        ALU.is_ge)
                    i = v.engine_nop()
                    chain(i, kb, "ve")
                    v.wait_ge(kb.s["ve"], kb.val("ve"))
                    for tt in range(16):
                        v.scalar_tensor_tensor(rt[:, tt, :], cmb[:, tt, :], -1e30,
                                               rt[:, tt, :], ALU.mult, ALU.add)
                    i = v.engine_nop()
                    chain(i, kb, "ve")

    with nc.Block() as b:
        @b.gpsimd
        def _(g):
            g.wait_ge(kb.s["ve"], kb.val("ve"))
            for tt in range(16):
                g.tensor_scalar(cmb[:, tt, :], lg[:, tt, :], rcol[:, tt, 1:2],
                                None, ALU.add)
            i = g.engine_nop()
            chain(i, kb, "po")

        @b.vector
        def _(v):
            v.wait_ge(kb.s["po"], kb.val("po"))
            for tt in range(16):
                v.tensor_scalar(rt[:, tt, :], cmb[:, tt, :], 0.0, None, ALU.is_ge)
            i = v.engine_nop()
            chain(i, kb, "ve")


    with nc.Block() as b:
        @b.scalar
        def _(s):
            s.wait_ge(kb.s["ve"], kb.val("ve"))
            for tt in range(16):
                s.activation(lg[:, tt, :], lg[:, tt, :], AF.Exp,
                             bias=rcol[:, tt, 0:1])
            i = s.copy(rcol[:, 0, 4:5], rcol[:, 0, 4:5])
            chain(i, kb, "ac")

        @b.vector
        def _(v):
            v.wait_ge(kb.s["ac"], kb.val("ac"))
            for tt in range(16):
                v.tensor_tensor(cmb[:, tt, :], lg[:, tt, :], rt[:, tt, :], ALU.mult)
            i = v.engine_nop()
            chain(i, kb, "ve")
            v.wait_ge(kb.s["ve"], kb.val("ve"))
            for tt in range(16):
                v.tensor_reduce(rcol[:, tt, 2:3], cmb[:, tt, :], axis=AX.X, op=ALU.add)
            i = v.engine_nop()
            chain(i, kb, "ve")
            v.wait_ge(kb.s["ve"], kb.val("ve"))
            for tt in range(16):
                v.reciprocal(rcol[:, tt, 3:4], rcol[:, tt, 2:3])
            i = v.engine_nop()
            chain(i, kb, "ve")
            v.wait_ge(kb.s["ve"], kb.val("ve"))
            for tt in range(16):
                v.tensor_scalar(cmbb[:, tt, :], cmb[:, tt, :], rcol[:, tt, 3:4],
                                None, ALU.mult)
            i = v.engine_nop()
            chain(i, kb, "ve")

    with contextlib.ExitStack() as pc:
        psC = [pc.enter_context(nc.psum_tensor(kb.uname("l3fTc"), [E, 128], BF16))
               for _ in range(2)]
        with nc.Block() as b:
            pe_c0 = kb.val("pe")
            ev_c0 = kb.val("evd")

            @b.tensor
            def _(t):
                t.wait_ge(kb.s["ve"], kb.val("ve"))
                for tt in range(16):
                    if tt >= 2:
                        t.wait_ge(kb.s["evd"], ev_c0 + tt - 1)
                    ins = t.transpose(psC[tt % 2][:], cmbb[:, tt, :],
                                      kb.sb["ident_bf"][:, :])
                    ins.then_inc(kb.s["pe"], 1)
                    kb.inc("pe")

            @b.vector
            def _(v):
                for tt in range(16):
                    v.wait_ge(kb.s["pe"], pe_c0 + tt + 1)
                    ins = v.tensor_copy(combT[:, tt * 128:(tt + 1) * 128],
                                        psC[tt % 2][:])
                    chain(ins, kb, "evd")

    kb.free(kf, rwf, krw, ksq, srow, lg, rt, cmb, rcol, cmbb)

    xn_d = kb.dscratch(kb.uname("l3fx"), [D, T], BF16)
    with nc.Block() as b:
        @b.sync
        def _(sy):
            sy.wait_ge(kb.s["ve"], kb.val("ve"))
            i = sy.dma_start(out=xn_d.rearrange("(c p) t -> p c t", p=128),
                             in_=xn[:, :, :])
            dchain(i, kb, "spl")

        @b.gpsimd
        def _(g):
            g.wait_ge(kb.s["spl"], kb.val("spl"))
            i = g.engine_nop()
            i.then_inc(kb.s["pw"], 1)
            kb.inc("pw")
    kb.free(kb.sb["xn"])

    w1b = kb.alloc("l3fw1", [128, DC, 2048], BF16)
    w2b = kb.alloc("l3fw2", [128, 16, 1024], BF16)
    hb = kb.alloc("l3fh", [128, 16, 512], BF16)
    bc1 = kb.alloc("l3fb1", [128, 16], F32)
    bc2 = kb.alloc("l3fb2", [128, 8], F32)
    cbc = kb.alloc("l3fcbc", [128, T], BF16)
    tvb = kb.alloc("l3ftv", [128, 4, 512], F32)
    xnch = kb.alloc("l3fxc", [128, 2, DC, 512], BF16)

    for e in range(E):
        with contextlib.ExitStack() as pc:
            psB = pc.enter_context(nc.psum_tensor(kb.uname(f"l3fB{e}"), [128, 512], F32))
            with nc.Block() as b:
                pe_b0 = kb.val("pe")
                ev_b0 = kb.val("evd")

                @b.tensor
                def _(t):
                    t.wait_ge(kb.s["ve"], kb.val("ve"))
                    t.wait_ge(kb.s["po"], kb.val("po"))
                    for tc in range(4):
                        if tc >= 1:
                            t.wait_ge(kb.s["evd"], ev_b0 + tc)
                        ins = t.matmul(psB[:], oh16[:, e * 128:(e + 1) * 128],
                                       combT[:, tc * 512:(tc + 1) * 512],
                                       start=True, stop=True)
                        ins.then_inc(kb.s["pe"], 1)
                        kb.inc("pe")

                @b.vector
                def _(v):
                    for tc in range(4):
                        v.wait_ge(kb.s["pe"], pe_b0 + tc + 1)
                        ins = v.tensor_copy(cbc[:, tc * 512:(tc + 1) * 512], psB[:])
                        chain(ins, kb, "evd")

        for half in range(2):
            with nc.Block() as b:
                @b.gpsimd
                def _(g):
                    g.wait_ge(kb.s["pw"], kb.val("pw"))
                    i = g.dma_start(
                        out=w1b[:, :, :],
                        in_=kb.dram["l3f_w1"][e].rearrange("(c p) m -> p c m", p=128)[
                            :, :, half * 2048:(half + 1) * 2048])
                    dchain(i, kb, "lw")
                    i = g.dma_start(
                        out=w2b[:, :, :],
                        in_=kb.dram["l3f_w2"][e].rearrange("(c p) m -> p c m", p=128)[
                            :, half * 16:(half + 1) * 16, :])
                    dchain(i, kb, "lw")

                @b.sync
                def _(sy):
                    i = sy.dma_start(out=bc1[:, :],
                                     in_=kb.dram["l3f_b1"][e][:, half * 16:(half + 1) * 16])
                    dchain(i, kb, "lq")
                    if half == 1:
                        i = sy.dma_start(out=bc2[:, :], in_=kb.dram["l3f_b2"][e])
                        dchain(i, kb, "lq")

            with contextlib.ExitStack() as pc:
                psH = [pc.enter_context(nc.psum_tensor(kb.uname(f"l3fH{e}_{half}"),
                                                       [128, 512], F32))
                       for _ in range(4)]
                psE = [pc.enter_context(nc.psum_tensor(kb.uname(f"l3fE{e}_{half}"),
                                                       [128, 512], F32))
                       for _ in range(4)]
                for tc in range(4):
                    with nc.Block() as b:
                        @b.sync
                        def _(sy):
                            sy.wait_ge(kb.s["pe"], kb.val("pe"))
                            i = sy.dma_start(
                                out=xnch[:, tc % 2],
                                in_=xn_d.rearrange("(c p) t -> p c t", p=128)[
                                    :, :, tc * 512:(tc + 1) * 512])
                            dchain(i, kb, "lq")

                    with nc.Block() as b:
                        lq_t = kb.val("lq")
                        pe_h0 = kb.val("pe")
                        acs_h0 = kb.val("acs")

                        @b.tensor
                        def _(t):
                            t.wait_ge(kb.s["lq"], lq_t)
                            t.wait_ge(kb.s["lw"], kb.val("lw"))
                            t.wait_ge(kb.s["pe2"], kb.val("pe2"))
                            for fc in range(16):
                                if fc >= 4:
                                    t.wait_ge(kb.s["acs"], acs_h0 + fc - 3)
                                for kc in range(DC):
                                    ins = t.matmul(psH[fc % 4][:],
                                                   w1b[:, kc, fc * 128:(fc + 1) * 128],
                                                   xnch[:, tc % 2][:, kc, :],
                                                   start=(kc == 0), stop=(kc == DC - 1))
                                ins.then_inc(kb.s["pe"], 1)
                                kb.inc("pe")

                        @b.scalar
                        def _(s):
                            for fc in range(16):
                                s.wait_ge(kb.s["pe"], pe_h0 + fc + 1)
                                ins = s.activation(hb[:, fc, :], psH[fc % 4][:],
                                                   AF.Gelu, bias=bc1[:, half * 0 + fc:
                                                                     half * 0 + fc + 1]
                                                   if False else bc1[:, fc:fc + 1])
                                chain(ins, kb, "acs")

                    with nc.Block() as b:
                        pe_e0 = kb.val("pe2")
                        ev_e0 = kb.val("evd")

                        @b.tensor
                        def _(t):
                            t.wait_ge(kb.s["acs"], kb.val("acs"))
                            for gc in range(DC):
                                if gc >= 4:
                                    t.wait_ge(kb.s["evd"], ev_e0 + gc - 3)
                                for fc in range(16):
                                    ins = t.matmul(psE[gc % 4][:],
                                                   w2b[:, fc, gc * 128:(gc + 1) * 128],
                                                   hb[:, fc, :],
                                                   start=(fc == 0), stop=(fc == 15))
                                ins.then_inc(kb.s["pe2"], 1)
                                kb.inc("pe2")

                        @b.vector
                        def _(v):
                            for gc in range(DC):
                                v.wait_ge(kb.s["pe2"], pe_e0 + gc + 1)
                                if half == 0:
                                    ins = v.tensor_tensor(tvb[:, gc % 4, :], psE[gc % 4][:],
                                                          cbc[:, tc * 512:(tc + 1) * 512],
                                                          ALU.mult)
                                else:
                                    ins = v.scalar_tensor_tensor(
                                        tvb[:, gc % 4, :], psE[gc % 4][:],
                                        bc2[:, gc:gc + 1],
                                        cbc[:, tc * 512:(tc + 1) * 512],
                                        ALU.add, ALU.mult)
                                chain(ins, kb, "evd")

                        ev_p0 = kb.val("evd") - DC

                        @b.gpsimd
                        def _(g):
                            for gc in range(DC):
                                g.wait_ge(kb.s["evd"], ev_p0 + gc + 1)
                                xs = x[:, gc, tc * 512:(tc + 1) * 512]
                                ins = g.tensor_tensor(xs, xs, tvb[:, gc % 4, :], ALU.add)
                                chain(ins, kb, "po")

            with nc.Block() as b:
                @b.gpsimd
                def _(g):
                    g.wait_ge(kb.s["po"], kb.val("po"))
                    i = g.engine_nop()
                    i.then_inc(kb.s["pw"], 1)
                    kb.inc("pw")

    with nc.Block() as b:
        @b.gpsimd
        def _(g):
            g.wait_ge(kb.s["po"], kb.val("po"))
            i = g.engine_nop()
            i.then_inc(kb.s["xr"], 1)
            kb.inc("xr")
    kb.free(w1b, w2b, hb, bc1, bc2, cbc, tvb, xnch, combT, oh16, crowb)
    kb.sb["xn"] = kb.alloc("xn", [128, DC, T], BF16)



def emit_l3aa(kb):
    nc = kb.nc
    x, xn = kb.sb["x_res"], kb.sb["xn"]
    cst = kb.sb["consts"]
    onesf = kb.sb["onesrow_f32"]

    vcs = kb.alloc("l3acv", [128, 64], F32)
    alc = kb.alloc("l3aal", [128, 128], F32)
    with nc.Block() as b:
        @b.sync
        def _(sy):
            i = sy.dma_start(out=vcs[:, 0:32], in_=kb.dram["l3a_cw"])
            dchain(i, kb, "lq")
            i = sy.dma_start(out=vcs[:, 32:40], in_=kb.dram["l3a_cb"])
            dchain(i, kb, "lq")
            i = sy.dma_start(out=vcs[:, 40:48], in_=kb.dram["l3a_dtw"])
            dchain(i, kb, "lq")
            i = sy.dma_start(out=vcs[:, 48:56], in_=kb.dram["l3a_dtb"])
            dchain(i, kb, "lq")
            i = sy.dma_start(out=vcs[:, 56:64], in_=kb.dram["l3a_dp"])
            dchain(i, kb, "lq")
            i = sy.dma_start(out=alc[:, :], in_=kb.dram["l3a_al"])
            dchain(i, kb, "lq")

        @b.scalar
        def _(s):
            s.wait_ge(kb.s["lq"], kb.val("lq"))
            i = s.activation(alc[:, :], alc[:, :], AF.Exp)
            chain(i, kb, "ac")

        @b.vector
        def _(v):
            v.wait_ge(kb.s["ac"], kb.val("ac"))
            i = v.tensor_scalar(alc[:, :], alc[:, :], -1.0, None, ALU.mult)
            chain(i, kb, "ve")

    acc = kb.alloc("l3aac", [128, DC, T], F32)
    with nc.Block() as b:
        @b.vector
        def _(v):
            v.wait_ge(kb.s["ln"], kb.val("ln"))
            v.wait_ge(kb.s["ve"], kb.val("ve"))
            for dc in range(DC):
                v.tensor_scalar(acc[:, dc, :], xn[:, dc, :],
                                vcs[:, dc * 4 + 3:dc * 4 + 4], None, ALU.mult)
            for k in (2, 1, 0):
                for dc in range(DC):
                    sh = 3 - k
                    v.scalar_tensor_tensor(acc[:, dc, sh:T], xn[:, dc, 0:T - sh],
                                           vcs[:, dc * 4 + k:dc * 4 + k + 1],
                                           acc[:, dc, sh:T], ALU.mult, ALU.add)
            i = v.engine_nop()
            chain(i, kb, "ve")

    xc = kb.alloc("l3axc", [128, DC, T], BF16)
    with nc.Block() as b:
        @b.scalar
        def _(s):
            s.wait_ge(kb.s["ve"], kb.val("ve"))
            for dc in range(DC):
                i = s.activation(xc[:, dc, :], acc[:, dc, :], AF.Silu,
                                 bias=vcs[:, 32 + dc:33 + dc])
            chain(i, kb, "ac")
    kb.free(acc)

    xpw = kb.alloc("l3axp", [128, DC, 33], BF16)
    bcselb = kb.alloc("l3abb", [32, T], BF16)
    oh32 = kb.alloc("l3aoh", [32, 32 * 128], BF16)
    prow = kb.alloc("l3apw", [1, T], F32)
    with nc.Block() as b:
        @b.gpsimd
        def _(g):
            i = g.dma_start(out=xpw[:, :, :],
                            in_=kb.dram["l3a_xp"].rearrange("(c p) m -> p c m", p=128))
            dchain(i, kb, "lw")

        @b.vector
        def _(v):
            i = v.memset(oh32[:, :], 1.0)
            chain(i, kb, "ve")

        @b.gpsimd
        def _(g):
            g.wait_ge(kb.s["ve"], kb.val("ve"))
            g.affine_select(oh32[:, :], oh32[:, :], pattern=[[1, 32 * 128]],
                            compare_op=ALU.is_ge, fill=0.0, base=0,
                            channel_multiplier=-128)
            i = g.affine_select(oh32[:, :], oh32[:, :], pattern=[[-1, 32 * 128]],
                               compare_op=ALU.is_ge, fill=0.0, base=127,
                               channel_multiplier=128)
            chain(i, kb, "po")

    with contextlib.ExitStack() as pc:
        psP = pc.enter_context(nc.psum_tensor(kb.uname("l3aP"), [32, 2048], F32))
        psQ = pc.enter_context(nc.psum_tensor(kb.uname("l3aQ"), [1, 2048], F32))
        with nc.Block() as b:
            @b.tensor
            def _(t):
                t.wait_ge(kb.s["ac"], kb.val("ac"))
                t.wait_ge(kb.s["lw"], kb.val("lw"))
                for c0 in range(0, T, 512):
                    for kc in range(DC):
                        t.matmul(psP[:, c0:c0 + 512], xpw[:, kc, 0:32],
                                 xc[:, kc, c0:c0 + 512],
                                 start=(kc == 0), stop=(kc == DC - 1))
                    for kc in range(DC):
                        i = t.matmul(psQ[:, c0:c0 + 512], xpw[:, kc, 32:33],
                                     xc[:, kc, c0:c0 + 512],
                                     start=(kc == 0), stop=(kc == DC - 1))
                chain(i, kb, "pe")

            @b.vector
            def _(v):
                v.wait_ge(kb.s["pe"], kb.val("pe"))
                v.tensor_copy(bcselb[:, :], psP[:, :])
                i = v.tensor_copy(prow[0:1, :], psQ[0:1, :])
                chain(i, kb, "ve")

    dtb_ = kb.alloc("l3adt", [128, T], F32)
    vxb = kb.alloc("l3avx", [128, T], F32)
    ab_ = kb.alloc("l3aab", [128, T], F32)
    ub_ = kb.alloc("l3aub", [128, T], F32)
    hb_ = kb.alloc("l3ahb", [128, T], F32)
    yb_ = kb.alloc("l3ayb", [128, T], F32)
    with contextlib.ExitStack() as pc:
        psB = pc.enter_context(nc.psum_tensor(kb.uname("l3aB"), [128, 2048], F32))
        psC = pc.enter_context(nc.psum_tensor(kb.uname("l3aC"), [128, 2048], F32))
        for dc in range(DC):
            with nc.Block() as b:
                @b.tensor
                def _(t):
                    t.wait_ge(kb.s["ve"], kb.val("ve"))
                    for c0 in range(0, T, 512):
                        i = t.matmul(psB[:, c0:c0 + 512], onesf[0:1, :],
                                     prow[0:1, c0:c0 + 512], start=True, stop=True)
                    chain(i, kb, "pe")

                @b.vector
                def _(v):
                    v.wait_ge(kb.s["pe"], kb.val("pe"))
                    ins = v.tensor_scalar(dtb_[:, :], psB[:, :],
                                          vcs[:, 40 + dc:41 + dc],
                                          vcs[:, 48 + dc:49 + dc], ALU.mult, ALU.add)
                    chain(ins, kb, "ve")

                @b.scalar
                def _(s):
                    s.wait_ge(kb.s["ve"], kb.val("ve"))
                    ins = s.activation(ab_[:, :], dtb_[:, :], AF.Abs)
                    ins.then_inc(kb.s["ac"], 1)
                    kb.inc("ac")
                    s.wait_ge(kb.s["ac"], kb.val("ac"))
                    ins = s.activation(ab_[:, :], ab_[:, :], AF.Exp,
                                       scale=cst[:, C_NEGONE:C_NEGONE + 1])
                    ins.then_inc(kb.s["ac"], 1)
                    kb.inc("ac")
                    s.wait_ge(kb.s["ac"], kb.val("ac"))
                    ins = s.activation(ab_[:, :], ab_[:, :], AF.Ln,
                                       bias=cst[:, C_ONE:C_ONE + 1])
                    chain(ins, kb, "ac")

                @b.gpsimd
                def _(g):
                    g.wait_ge(kb.s["ve"], kb.val("ve"))
                    ins = g.tensor_scalar_max(ub_[:, :], dtb_[:, :], 0.0)
                    chain(ins, kb, "po")

                @b.vector
                def _(v):
                    v.wait_ge(kb.s["ac"], kb.val("ac"))
                    v.wait_ge(kb.s["po"], kb.val("po"))
                    ins = v.tensor_tensor(dtb_[:, :], ab_[:, :], ub_[:, :], ALU.add)
                    ins.then_inc(kb.s["ve"], 1)
                    kb.inc("ve")
                    v.wait_ge(kb.s["ve"], kb.val("ve"))
                    v.tensor_tensor(vxb[:, :], dtb_[:, :], xc[:, dc, :], ALU.mult)
                    ins = v.memset(yb_[:, :], 0.0)
                    chain(ins, kb, "ve")

            for n in range(16):
                with nc.Block() as b:
                    @b.tensor
                    def _(t):
                        t.wait_ge(kb.s["ve"], kb.val("ve"))
                        t.wait_ge(kb.s["po"], kb.val("po"))
                        for c0 in range(0, T, 512):
                            t.matmul(psB[:, c0:c0 + 512],
                                     oh32[:, n * 128:(n + 1) * 128],
                                     bcselb[:, c0:c0 + 512], start=True, stop=True)
                        for c0 in range(0, T, 512):
                            i = t.matmul(psC[:, c0:c0 + 512],
                                         oh32[:, (16 + n) * 128:(17 + n) * 128],
                                         bcselb[:, c0:c0 + 512], start=True, stop=True)
                        chain(i, kb, "pe")

                    @b.scalar
                    def _(s):
                        s.wait_ge(kb.s["ve"], kb.val("ve"))
                        ins = s.activation(ab_[:, :], dtb_[:, :], AF.Exp,
                                           scale=alc[:, dc * 16 + n:dc * 16 + n + 1])
                        chain(ins, kb, "acs")

                    @b.vector
                    def _(v):
                        v.wait_ge(kb.s["pe"], kb.val("pe"))
                        ins = v.tensor_tensor(ub_[:, :], vxb[:, :], psB[:, :], ALU.mult)
                        ins.then_inc(kb.s["ve"], 1)
                        kb.inc("ve")
                        v.wait_ge(kb.s["ve"], kb.val("ve"))
                        v.wait_ge(kb.s["acs"], kb.val("acs"))
                        ins = v.tensor_tensor_scan(hb_[:, :], ab_[:, :], ub_[:, :],
                                                   0.0, ALU.mult, ALU.add)
                        ins.then_inc(kb.s["ve"], 1)
                        kb.inc("ve")
                        v.wait_ge(kb.s["ve"], kb.val("ve"))
                        ins = v.tensor_tensor(hb_[:, :], hb_[:, :], psC[:, :], ALU.mult)
                        ins.then_inc(kb.s["ve"], 1)
                        kb.inc("ve")

                    @b.gpsimd
                    def _(g):
                        g.wait_ge(kb.s["ve"], kb.val("ve"))
                        ins = g.tensor_tensor(yb_[:, :], yb_[:, :], hb_[:, :], ALU.add)
                        chain(ins, kb, "po")

            with nc.Block() as b:
                @b.vector
                def _(v):
                    v.wait_ge(kb.s["po"], kb.val("po"))
                    ins = v.tensor_tensor(x[:, dc, :], x[:, dc, :], yb_[:, :], ALU.add)
                    ins.then_inc(kb.s["ve"], 1)
                    kb.inc("ve")
                    v.wait_ge(kb.s["ve"], kb.val("ve"))
                    ins = v.scalar_tensor_tensor(x[:, dc, :], xn[:, dc, :],
                                                 vcs[:, 56 + dc:57 + dc],
                                                 x[:, dc, :], ALU.mult, ALU.add)
                    chain(ins, kb, "ve")

    with nc.Block() as b:
        @b.gpsimd
        def _(g):
            g.wait_ge(kb.s["ve"], kb.val("ve"))
            i = g.engine_nop()
            i.then_inc(kb.s["xr"], 1)
            kb.inc("xr")
    kb.free(vcs, alc, xc, xpw, bcselb, oh32, prow, dtb_, vxb, ab_, ub_, hb_, yb_)



def emit_l1ab(kb):
    nc = kb.nc
    x, xn = kb.sb["x_res"], kb.sb["xn"]
    cst = kb.sb["consts"]
    idn = kb.sb["ident_bf"]
    onesf = kb.sb["onesrow_f32"]

    brow = kb.alloc("l1bbr", [1, 2 * D], F32)
    rels = kb.alloc("l1brl", [1, 16], F32)
    bcols = kb.alloc("l1bbc", [128, 16], F32)
    erow = kb.alloc("l1ber", [1, 128], F32)
    bqb = kb.alloc("l1bqb", [128, D], BF16)
    bkb = kb.alloc("l1bkb", [128, D], BF16)
    rbb = kb.alloc("l1brb", [128, 8, 128], BF16)
    wbuf = kb.alloc("l1bw", [128, DC, 1024], BF16)

    with nc.Block() as b:
        @b.sync
        def _(sy):
            i = sy.dma_start(out=brow[0:1, 0:D], in_=kb.dram["l1b_bq"])
            dchain(i, kb, "lq")
            i = sy.dma_start(out=brow[0:1, D:2 * D], in_=kb.dram["l1b_bk"])
            dchain(i, kb, "lq")
            i = sy.dma_start(out=rels[0:1, 0:15],
                             in_=kb.dram["l1b_rel"][0:1, :])
            dchain(i, kb, "lq")
            i = sy.dma_start(out=bcols[:, :], in_=kb.dram["l1b_bvo"])
            dchain(i, kb, "lq")

        @b.vector
        def _(v):
            i = v.memset(erow[0:1, :], 0.125)
            chain(i, kb, "ve")

    with contextlib.ExitStack() as pc:
        psB = pc.enter_context(nc.psum_tensor(kb.uname("l1bB"), [128, 512], F32))
        with nc.Block() as b:
            pe0 = kb.val("pe")
            ev0 = kb.val("evd")
            jobs = [(bqb, 0, erow), (bkb, D, onesf)]

            @b.tensor
            def _(t):
                t.wait_ge(kb.s["lq"], kb.val("lq"))
                t.wait_ge(kb.s["ve"], kb.val("ve"))
                nj = 0
                for dst, off, lrow in jobs:
                    for c0 in range(0, D, 512):
                        if nj >= 1:
                            t.wait_ge(kb.s["evd"], ev0 + nj)
                        ins = t.matmul(psB[:], lrow[0:1, :],
                                       brow[0:1, off + c0:off + c0 + 512],
                                       start=True, stop=True)
                        ins.then_inc(kb.s["pe"], 1)
                        kb.inc("pe")
                        nj += 1

            @b.vector
            def _(v):
                nj = 0
                for dst, off, lrow in jobs:
                    for c0 in range(0, D, 512):
                        v.wait_ge(kb.s["pe"], pe0 + nj + 1)
                        ins = v.tensor_copy(dst[:, c0:c0 + 512], psB[:])
                        chain(ins, kb, "evd")
                        nj += 1

        with nc.Block() as b:
            @b.vector
            def _(v):
                v.wait_ge(kb.s["evd"], kb.val("evd"))
                for bp in range(8):
                    src = rels[0:1, 7 - bp:15 - bp].unsqueeze(1) \
                        .broadcast_to([1, 16, 8])
                    ins = v.tensor_copy(brow[0:1, bp * 128:(bp + 1) * 128]
                                        .rearrange("p (g k) -> p g k", g=16, k=8), src)
                chain(ins, kb, "ve")

        with nc.Block() as b:
            pe0 = kb.val("pe")
            evr0 = kb.val("evd")

            @b.tensor
            def _(t):
                t.wait_ge(kb.s["ve"], kb.val("ve"))
                for bp in range(8):
                    if bp >= 1:
                        t.wait_ge(kb.s["evd"], evr0 + bp)
                    ins = t.matmul(psB[:, 0:128], erow[0:1, :],
                                   brow[0:1, bp * 128:(bp + 1) * 128],
                                   start=True, stop=True)
                    ins.then_inc(kb.s["pe"], 1)
                    kb.inc("pe")

            @b.vector
            def _(v):
                for bp in range(8):
                    v.wait_ge(kb.s["pe"], pe0 + bp + 1)
                    ins = v.tensor_copy(rbb[:, bp, :], psB[:, 0:128])
                    chain(ins, kb, "evd")

    def tm_proj(wname, dst, bias_b, scale):
        with nc.Block() as b:
            @b.gpsimd
            def _(g):
                g.wait_ge(kb.s["pw"], kb.val("pw"))
                i = g.dma_start(out=wbuf[:, :, :],
                                in_=kb.dram[wname].rearrange("(c p) m -> p c m", p=128))
                dchain(i, kb, "lw")

        def evac(b, key, ps, pe_t):
            tt, mh = key

            @b.vector
            def _(v):
                v.wait_ge(kb.s["pe"], pe_t)
                o = dst[:, tt, mh * 512:(mh + 1) * 512]
                ins = v.scalar_tensor_tensor(o, ps[:], scale,
                                             bias_b[:, mh * 512:(mh + 1) * 512],
                                             ALU.mult, ALU.add)
                chain(ins, kb, "evd")

        tiles = [(tt, mh) for tt in range(16) for mh in range(2)]
        emit_mm(kb, tiles,
                stat_fn=lambda key, kc: xn[:, kc, key[0] * 128:(key[0] + 1) * 128],
                mov_fn=lambda key, kc: wbuf[:, kc, key[1] * 512:(key[1] + 1) * 512],
                nk_fn=lambda key: DC, evac_fn=evac, tag="l1btm",
                waits=[("lw", kb.val("lw"))])
        with nc.Block() as b:
            @b.gpsimd
            def _(g):
                g.wait_ge(kb.s["evd"], kb.val("evd"))
                i = g.engine_nop()
                i.then_inc(kb.s["pw"], 1)
                kb.inc("pw")

    v_d = kb.dscratch(kb.uname("l1bvd"), [D, T], BF16)
    vf = kb.alloc("l1bvf", [128, DC, T], BF16)
    with nc.Block() as b:
        @b.gpsimd
        def _(g):
            g.wait_ge(kb.s["pw"], kb.val("pw"))
            i = g.dma_start(out=wbuf[:, :, :],
                            in_=kb.dram["l1b_wv"].rearrange("(c p) m -> p c m", p=128))
            dchain(i, kb, "lw")

    def evac_v(b, key, ps, pe_t):
        mc, tc = key

        @b.vector
        def _(v):
            v.wait_ge(kb.s["pe"], pe_t)
            ins = v.tensor_scalar(vf[:, mc, tc * 512:(tc + 1) * 512], ps[:],
                                  bcols[:, mc:mc + 1], None, ALU.add)
            chain(ins, kb, "evd")

    tiles = [(mc, tc) for mc in range(DC) for tc in range(4)]
    emit_mm(kb, tiles,
            stat_fn=lambda key, kc: wbuf[:, kc, key[0] * 128:(key[0] + 1) * 128],
            mov_fn=lambda key, kc: xn[:, kc, key[1] * 512:(key[1] + 1) * 512],
            nk_fn=lambda key: DC, evac_fn=evac_v, tag="l1bv",
            waits=[("lw", kb.val("lw"))])
    with nc.Block() as b:
        @b.sync
        def _(sy):
            sy.wait_ge(kb.s["evd"], kb.val("evd"))
            i = sy.dma_start(out=v_d.rearrange("(c p) t -> p c t", p=128),
                             in_=vf[:, :, :])
            dchain(i, kb, "spl")

        @b.gpsimd
        def _(g):
            g.wait_ge(kb.s["spl"], kb.val("spl"))
            i = g.engine_nop()
            i.then_inc(kb.s["pw"], 1)
            kb.inc("pw")
    kb.free(vf)

    qtm = kb.alloc("l1bqt", [128, 16, 1024], BF16)
    ktm = kb.alloc("l1bkt", [128, 16, 1024], BF16)
    tm_proj("l1b_wq", qtm, bqb, 0.125)
    tm_proj("l1b_wk", ktm, bkb, 1.0)
    kb.free(kb.sb["xn"], brow, rels, erow, bqb, bkb)

    sc = kb.alloc("l1bsc", [128, 2, 8, 128], F32)
    mmb = kb.alloc("l1bmm", [128, 1024], BF16)
    red = kb.alloc("l1brd", [128, 2, 8, 16], F32)
    with nc.Block() as b:
        @b.vector
        def _(v):
            v.wait_ge(kb.s["evd"], kb.val("evd"))
            for u in range(2):
                for bp in range(8):
                    for bb in range(8):
                        v.tensor_tensor(mmb[:, :], qtm[:, 2 * bp + u, :],
                                        ktm[:, 2 * bb + u, :], ALU.mult)
                        v.tensor_reduce(
                            sc[:, u, bp, bb:bb + 1 + 15 * 8:8].rearrange(
                                "p (g o) -> p g o", g=16, o=1) if False else
                            sc[:, u][:, bp][:, bb::8].unsqueeze(2),
                            mmb[:, :].rearrange("p (g k) -> p g k", g=16, k=64),
                            axis=AX.X, op=ALU.add)
            i = v.engine_nop()
            chain(i, kb, "ve")

        @b.gpsimd
        def _(g):
            g.wait_ge(kb.s["ve"], kb.val("ve"))
            for u in range(2):
                for bp in range(8):
                    ins = g.tensor_tensor(sc[:, u][:, bp], sc[:, u][:, bp],
                                          rbb[:, bp, :], ALU.add)
            chain(ins, kb, "po")

    pb = kb.alloc("l1bpb", [128, 2, 8, 128], BF16)
    with nc.Block() as b:
        @b.vector
        def _(v):
            v.wait_ge(kb.s["po"], kb.val("po"))
            for u in range(2):
                for bp in range(8):
                    v.tensor_reduce(red[:, u, bp, :].unsqueeze(2),
                                    sc[:, u][:, bp].rearrange("p (g k) -> p g k",
                                                              g=16, k=8),
                                    axis=AX.X, op=ALU.max)
            i = v.engine_nop()
            chain(i, kb, "ve")
            v.wait_ge(kb.s["ve"], kb.val("ve"))
            for u in range(2):
                for bp in range(8):
                    mxb = red[:, u, bp, :].unsqueeze(2) \
                        .broadcast_to([128, 16, 8])
                    v.tensor_tensor(sc[:, u][:, bp].rearrange("p (g k) -> p g k",
                                                              g=16, k=8),
                                    sc[:, u][:, bp].rearrange("p (g k) -> p g k",
                                                              g=16, k=8),
                                    mxb, ALU.subtract)
            i = v.engine_nop()
            chain(i, kb, "ve")

        @b.scalar
        def _(s):
            s.wait_ge(kb.s["ve"], kb.val("ve"))
            for u in range(2):
                for bp in range(8):
                    i = s.activation(sc[:, u][:, bp], sc[:, u][:, bp], AF.Exp)
            chain(i, kb, "ac")

        @b.vector
        def _(v):
            v.wait_ge(kb.s["ac"], kb.val("ac"))
            for u in range(2):
                for bp in range(8):
                    v.tensor_reduce(red[:, u, bp, :].unsqueeze(2),
                                    sc[:, u][:, bp].rearrange("p (g k) -> p g k",
                                                              g=16, k=8),
                                    axis=AX.X, op=ALU.add)
            i = v.engine_nop()
            chain(i, kb, "ve")
            v.wait_ge(kb.s["ve"], kb.val("ve"))
            for u in range(2):
                for bp in range(8):
                    v.reciprocal(red[:, u, bp, :], red[:, u, bp, :])
            i = v.engine_nop()
            chain(i, kb, "ve")
            v.wait_ge(kb.s["ve"], kb.val("ve"))
            for u in range(2):
                for bp in range(8):
                    rzb = red[:, u, bp, :].unsqueeze(2) \
                        .broadcast_to([128, 16, 8])
                    v.tensor_tensor(pb[:, u][:, bp].rearrange("p (g k) -> p g k",
                                                              g=16, k=8),
                                    sc[:, u][:, bp].rearrange("p (g k) -> p g k",
                                                              g=16, k=8),
                                    rzb, ALU.mult)
            i = v.engine_nop()
            chain(i, kb, "ve")
    kb.free(sc, mmb, red, qtm, ktm)

    ptb = kb.alloc("l1bpt", [128, 2, 8, 128], BF16)
    sel2 = kb.alloc("l1bs2", [128, 64, 128], BF16)
    outa = kb.alloc("l1boa", [128, DC, T], F32)
    w64 = kb.alloc("l1bw64", [128, 2, 256], BF16)
    idnbf = kb.sb["ident_bf"]
    with nc.Block() as b:
        @b.vector
        def _(v):
            for dc in range(DC):
                for bb in range(8):
                    j = dc * 8 + bb
                    v.tensor_copy(sel2[:, j, 0:64],
                                  idnbf[:, 16 * dc + bb:16 * dc + bb + 1]
                                  .broadcast_to([128, 64]))
                    v.tensor_copy(sel2[:, j, 64:128],
                                  idnbf[:, 16 * dc + 8 + bb:16 * dc + 8 + bb + 1]
                                  .broadcast_to([128, 64]))
            i = v.engine_nop()
            chain(i, kb, "po")

        @b.vector
        def _(v):
            v.wait_ge(kb.s["evd"], kb.val("evd"))
            for dc in range(DC):
                i = v.memset(outa[:, dc, :], 0.0)
            chain(i, kb, "ve")

    with contextlib.ExitStack() as pc:
        psT = [pc.enter_context(nc.psum_tensor(kb.uname("l1bT"), [128, 128], BF16))
               for _ in range(2)]
        psW = [pc.enter_context(nc.psum_tensor(kb.uname("l1bW"), [128, 256], F32))
               for _ in range(2)]
        with nc.Block() as b:
            pe0 = kb.val("pe")
            ev0 = kb.val("evd")
            pairs = [(u, bp) for u in range(2) for bp in range(8)]

            @b.tensor
            def _(t):
                t.wait_ge(kb.s["ve"], kb.val("ve"))
                for i_, (u, bp) in enumerate(pairs):
                    if i_ >= 2:
                        t.wait_ge(kb.s["evd"], ev0 + i_ - 1)
                    ins = t.transpose(psT[i_ % 2][:], pb[:, u][:, bp], idn[:])
                    ins.then_inc(kb.s["pe"], 1)
                    kb.inc("pe")

            @b.vector
            def _(v):
                for i_, (u, bp) in enumerate(pairs):
                    v.wait_ge(kb.s["pe"], pe0 + i_ + 1)
                    ins = v.tensor_copy(ptb[:, u][:, bp], psT[i_ % 2][:])
                    chain(ins, kb, "evd")

        vch = kb.alloc("l1bvc", [128, DC, 256], BF16)
        for bb in range(8):
            with nc.Block() as b:
                @b.sync
                def _(sy):
                    sy.wait_ge(kb.s["po"], kb.val("po"))
                    i = sy.dma_start(
                        out=vch[:, :, :],
                        in_=v_d.rearrange("(c p) t -> p c t", p=128)[
                            :, :, bb * 256:(bb + 1) * 256])
                    dchain(i, kb, "lq")
            lq_t = kb.val("lq")
            for bp in range(8):
                for dc in range(DC):
                    with nc.Block() as b:
                        pe0 = kb.val("pe")

                        @b.tensor
                        def _(t):
                            t.wait_ge(kb.s["lq"], lq_t)
                            t.wait_ge(kb.s["evd"], kb.val("evd"))
                            t.wait_ge(kb.s["po"], kb.val("po"))
                            for u in range(2):
                                ins = t.matmul(psW[dc % 2][:, u * 128:(u + 1) * 128],
                                               sel2[:, dc * 8 + bb, :],
                                               ptb[:, u][:, bp],
                                               start=True, stop=True)
                                ins.then_inc(kb.s["pe"], 1)
                                kb.inc("pe")

                        @b.vector
                        def _(v):
                            v.wait_ge(kb.s["pe"], pe0 + 2)
                            ins = v.tensor_copy(w64[:, dc % 2, :], psW[dc % 2][:])
                            chain(ins, kb, "evd")

                        @b.gpsimd
                        def _(g):
                            g.wait_ge(kb.s["evd"], kb.val("evd"))
                            o = outa[:, dc, bp * 256:(bp + 1) * 256]
                            g.tensor_tensor(w64[:, dc % 2, :], w64[:, dc % 2, :],
                                            vch[:, dc, :], ALU.mult)
                            i = g.engine_nop()
                            i.then_inc(kb.s["po"], 1)
                            kb.inc("po")
                            g.wait_ge(kb.s["po"], kb.val("po"))
                            ins = g.tensor_tensor(o, o, w64[:, dc % 2, :], ALU.add)
                            chain(ins, kb, "po")
        kb.free(vch)

    kb.free(pb, ptb, sel2, w64, rbb)

    outb = kb.alloc("l1bob", [128, DC, T], BF16)
    with nc.Block() as b:
        @b.vector
        def _(v):
            v.wait_ge(kb.s["po"], kb.val("po"))
            for dc in range(DC):
                i = v.tensor_copy(outb[:, dc, :], outa[:, dc, :])
            chain(i, kb, "ve")

        @b.gpsimd
        def _(g):
            g.wait_ge(kb.s["pw"], kb.val("pw"))
            i = g.dma_start(out=wbuf[:, :, :],
                            in_=kb.dram["l1b_wo"].rearrange("(c p) m -> p c m", p=128))
            dchain(i, kb, "lw")

    def evac_o(b, key, ps, pe_t):
        mc, tc = key

        @b.vector
        def _(v):
            v.wait_ge(kb.s["pe"], pe_t)
            xs = x[:, mc, tc * 512:(tc + 1) * 512]
            ins = v.scalar_tensor_tensor(xs, ps[:], bcols[:, 8 + mc:9 + mc], xs,
                                         ALU.add, ALU.add)
            chain(ins, kb, "evd")

    tiles = [(mc, tc) for mc in range(DC) for tc in range(4)]
    emit_mm(kb, tiles,
            stat_fn=lambda key, kc: wbuf[:, kc, key[0] * 128:(key[0] + 1) * 128],
            mov_fn=lambda key, kc: outb[:, kc, key[1] * 512:(key[1] + 1) * 512],
            nk_fn=lambda key: DC, evac_fn=evac_o, tag="l1bo",
            waits=[("lw", kb.val("lw")), ("ve", kb.val("ve"))], wait_ln=False)
    with nc.Block() as b:
        @b.gpsimd
        def _(g):
            g.wait_ge(kb.s["evd"], kb.val("evd"))
            i = g.engine_nop()
            i.then_inc(kb.s["xr"], 1)
            kb.inc("xr")
    kb.free(outa, outb, wbuf, bcols)
    kb.sb["xn"] = kb.alloc("xn", [128, DC, T], BF16)


def build_program(subs):
    nc = bass.Bass("TRN2", target_bir_lowering=False)
    kb = KB(nc)
    x_in = kb.din("x_in", [D, T])
    x_out = nc.dram_tensor("x_out", [D, T], F32, kind="ExternalOutput").ap()
    wdecl = {}
    for s in subs:
        for name, shape in SUB_WEIGHTS[s]:
            wdecl[name] = shape
    for name, shape in wdecl.items():
        kb.din(name, shape)

    with kb.ctx:
        kb.sbuf("x_res", [128, DC, T], F32)
        kb.sbuf("consts", [128, NCONST], F32)
        nc.const_aps.register(F32, -1e30, kb.sb["consts"][:, C_NEGBIG:C_NEGBIG + 1])
        kb.sbuf("ones128_bf", [128, 1], BF16)
        kb.sbuf("onesrow_f32", [1, 128], F32)
        kb.sbuf("onescol_f32", [128, 1], F32)
        kb.sbuf("negrow_f32", [1, 128], F32)
        kb.sbuf("onesrow_bf", [1, 128], BF16)
        kb.sbuf("ident_bf", [128, 128], BF16)
        kb.sbuf("ident_f32", [128, 128], F32)
        kb.sbuf("cmask", [128, 128], F32)
        kb.sbuf("smask", [128, 128], BF16)
        kb.sbuf("lemask", [128, 128], BF16)
        kb.init_arena(140 * 1024)
        kb.sb["xn"] = kb.alloc("xn", [128, DC, T], BF16)
        for s in ("xr", "ln", "ve", "ac", "pe", "po", "lw", "evd", "st", "lx",
                  "pw", "spl", "lq", "acs", "hcd", "snk", "pe2"):
            kb.sem(s)

        with nc.Block() as b:
            @b.vector
            def _(v):
                for j, val in enumerate(CONST_VALS):
                    v.memset(kb.sb["consts"][:, j:j + 1], val)
                v.memset(kb.sb["ones128_bf"][:], 1.0)
                v.memset(kb.sb["onesrow_f32"][:], 1.0)
                v.memset(kb.sb["onescol_f32"][:], 1.0)
                v.memset(kb.sb["onesrow_bf"][:], 1.0)
                v.memset(kb.sb["ident_bf"][:], 1.0)
                v.memset(kb.sb["ident_f32"][:], 1.0)
                v.memset(kb.sb["cmask"][:], -1e30)
                v.memset(kb.sb["smask"][:], 1.0)
                v.memset(kb.sb["lemask"][:], 1.0)
                i = v.memset(kb.sb["negrow_f32"][:], -1.0)
                chain(i, kb, "ve")

            @b.gpsimd
            def _(g):
                g.wait_ge(kb.s["ve"], kb.val("ve"))
                g.affine_select(kb.sb["ident_bf"][:], kb.sb["ident_bf"][:],
                                pattern=[[-1, 128]], compare_op=ALU.is_equal,
                                fill=0.0, base=0, channel_multiplier=1)
                g.affine_select(kb.sb["ident_f32"][:], kb.sb["ident_f32"][:],
                                pattern=[[-1, 128]], compare_op=ALU.is_equal,
                                fill=0.0, base=0, channel_multiplier=1)
                g.affine_select(kb.sb["cmask"][:], kb.sb["cmask"][:],
                                pattern=[[1, 128]], compare_op=ALU.is_ge,
                                fill=0.0, base=-1, channel_multiplier=-1)
                g.affine_select(kb.sb["smask"][:], kb.sb["smask"][:],
                                pattern=[[-1, 128]], compare_op=ALU.is_gt,
                                fill=0.0, base=0, channel_multiplier=1)
                i = g.affine_select(kb.sb["lemask"][:], kb.sb["lemask"][:],
                                    pattern=[[1, 128]], compare_op=ALU.is_ge,
                                    fill=0.0, base=0, channel_multiplier=-1)
                chain(i, kb, "po")

            @b.sync
            def _(sy):
                i = sy.dma_start(out=kb.sb["x_res"][:, :, :],
                                 in_=x_in.rearrange("(c p) t -> p c t", p=128))
                dchain(i, kb, "lx")

            @b.gpsimd
            def _(g):
                g.wait_ge(kb.s["lx"], kb.val("lx"))
                g.wait_ge(kb.s["po"], kb.val("po"))
                i = g.engine_nop()
                i.then_inc(kb.s["xr"], 1)
                kb.inc("xr")

        for s in subs:
            if s == "l0aa":
                kb.sb["mean_b"] = kb.alloc("meanb", [128, T], F32)
                emit_ln(kb, mean_b=kb.sb["mean_b"], exact_mean=True)
            else:
                emit_ln(kb)
            SUB_EMITTERS[s](kb)

        with nc.Block() as b:
            @b.sync
            def _(sy):
                sy.wait_ge(kb.s["xr"], kb.val("xr"))
                i = sy.dma_start(out=x_out.rearrange("(c p) t -> p c t", p=128),
                                 in_=kb.sb["x_res"][:, :, :])
                dchain(i, kb, "st")

            @b.gpsimd
            def _(g):
                g.wait_ge(kb.s["st"], kb.val("st"))

    return nc


SUB_EMITTERS = {
    "l0ffn": emit_l0ffn,
    "l2aa": emit_l2aa,
    "l1aa": emit_l1aa,
    "l0aa": emit_l0aa,
    "l2ffn": emit_l2ffn,
    "l2ab": emit_l2ab,
    "l3ab": emit_l3ab,
    "l1ffn": emit_l1ffn,
    "l0ab": emit_l0ab,
    "l3ffn": emit_l3ffn,
    "l3aa": emit_l3aa,
    "l1ab": emit_l1ab,
}

SUB_WEIGHTS = {
    "l0ffn": [("l0f_wT", (D, D)), ("l0f_sc", (128, 8)), ("l0f_bi", (128, 8))],
    "l2aa": [("l2a_wq", (D, D)), ("l2a_wk", (D, D)), ("l2a_wv", (D, D)),
             ("l2a_wo", (D, D))],
    "l1aa": [("l1a_wq", (D, D)), ("l1a_wk", (D, D)), ("l1a_wv", (D, D)),
             ("l1a_wo", (D, D)), ("l1a_ob", (128, 8))],
    "l0aa": [("l0a_b", (128, 24))],
    "l2ffn": [("l2f_wr", (D, D)), ("l2f_wk", (D, D)), ("l2f_wv", (D, D)),
              ("l2f_mu", (128, 16))],
    "l2ab": [("l2b_wq", (D, D)), ("l2b_wk", (D, D)), ("l2b_wv", (D, D)),
             ("l2b_wo", (D, D)), ("l2b_vec", (128, 56)), ("l2b_bo", (128, 8))],
    "l3ab": [("l3b_wq", (D, 2 * D)), ("l3b_wk", (D, 2 * D)), ("l3b_wv", (D, 2 * D)),
             ("l3b_wo", (2 * D, D)), ("l3b_lam", (8, 512)), ("l3b_rms", (128, 2))],
    "l1ffn": [("l1f_gate", (D, 8)), ("l1f_w1", (8, D, 2048)), ("l1f_b1", (8, 128, 16)),
              ("l1f_w2", (8, 2048, D)), ("l1f_b2", (8, 128, 8)),
              ("l1f_ws1", (D, 2048)), ("l1f_bs1", (128, 16)),
              ("l1f_ws2", (2048, D)), ("l1f_bs2", (128, 8))],
    "l0ab": [("l0b_wqkv", (16, D, 384)), ("l0b_wout", (D, D))],
    "l3ffn": [("l3f_keys", (D, 16)), ("l3f_rw", (D, 16)), ("l3f_rb", (1, 16)),
              ("l3f_w1", (16, D, 4096)), ("l3f_b1", (16, 128, 32)),
              ("l3f_w2", (16, 4096, D)), ("l3f_b2", (16, 128, 8))],
    "l3aa": [("l3a_cw", (128, 32)), ("l3a_cb", (128, 8)), ("l3a_dtw", (128, 8)),
             ("l3a_dtb", (128, 8)), ("l3a_dp", (128, 8)), ("l3a_al", (128, 128)),
             ("l3a_xp", (D, 33))],
    "l1ab": [("l1b_wq", (D, D)), ("l1b_wk", (D, D)), ("l1b_wv", (D, D)),
             ("l1b_wo", (D, D)), ("l1b_bq", (1, D)), ("l1b_bk", (1, D)),
             ("l1b_bvo", (128, 16)), ("l1b_rel", (1, 15))],
}


def col128(v):
    return np.ascontiguousarray(np.asarray(v, np.float32).reshape(DC, 128).T)


def col128x(v, n):
    return np.ascontiguousarray(np.asarray(v, np.float32).reshape(n, 128).T)


def prep_weights(inp):
    def A(x):
        return np.ascontiguousarray(np.asarray(x, np.float32))
    w = {}
    w["l0f_wT"] = A(np.asarray(inp["L0_ffn_weights"], np.float32).T)
    w["l0f_sc"] = col128(inp["L0_ffn_scale"])
    w["l0f_bi"] = col128(inp["L0_ffn_bias"])
    w["l2a_wq"] = A(np.asarray(inp["L2_attn_a_W_q"], np.float32).transpose(1, 0, 2).reshape(D, D))
    w["l2a_wk"] = A(np.asarray(inp["L2_attn_a_W_k"], np.float32).transpose(1, 0, 2).reshape(D, D))
    w["l2a_wv"] = A(np.asarray(inp["L2_attn_a_W_v"], np.float32).transpose(1, 0, 2).reshape(D, D))
    w["l2a_wo"] = A(inp["L2_attn_a_W_o"])
    w["l1a_wq"] = A(np.asarray(inp["L1_attn_a_q_w"], np.float32).T)
    w["l1a_wk"] = A(np.asarray(inp["L1_attn_a_k_w"], np.float32).T)
    w["l1a_wv"] = A(np.asarray(inp["L1_attn_a_v_w"], np.float32).T)
    w["l1a_wo"] = A(np.asarray(inp["L1_attn_a_out_w"], np.float32).T)
    w["l1a_ob"] = col128(inp["L1_attn_a_out_b"])
    w["l0a_b"] = np.concatenate([col128(inp["L0_attn_a_bv_q"]),
                                 col128(inp["L0_attn_a_bv_k"]),
                                 col128(inp["L0_attn_a_bv_v"])], axis=1)
    w["l2f_wr"] = A(np.asarray(inp["L2_ffn_W_r"], np.float32).T)
    w["l2f_wk"] = A(np.asarray(inp["L2_ffn_W_k"], np.float32).T)
    w["l2f_wv"] = A(np.asarray(inp["L2_ffn_W_v"], np.float32).T)
    w["l2f_mu"] = np.concatenate([col128(inp["L2_ffn_mu_r"]),
                                  col128(inp["L2_ffn_mu_k"])], axis=1)
    w["l2b_wq"] = A(inp["L2_attn_b_Wq"])
    w["l2b_wk"] = A(inp["L2_attn_b_Wk"])
    w["l2b_wv"] = A(inp["L2_attn_b_Wv"])
    w["l2b_wo"] = A(inp["L2_attn_b_Wo"])

    def rep64(vv):
        return col128(np.repeat(np.asarray(vv, np.float32), 64))
    w["l2b_vec"] = np.concatenate(
        [col128(inp["L2_attn_b_bq"]), col128(inp["L2_attn_b_bk"]),
         col128(inp["L2_attn_b_bv"]), col128(inp["L2_attn_b_bo"]),
         rep64(inp["L2_attn_b_w0"]), rep64(inp["L2_attn_b_w1"]),
         rep64(inp["L2_attn_b_wK"])], axis=1)
    w["l2b_bo"] = col128(inp["L2_attn_b_bo"])
    w["l3b_wq"] = A(inp["L3_attn_b_W_q"])
    w["l3b_wk"] = A(inp["L3_attn_b_W_k"])
    w["l3b_wv"] = A(inp["L3_attn_b_W_v"])
    w["l3b_wo"] = A(inp["L3_attn_b_W_o"])
    w["l3b_lam"] = np.concatenate(
        [A(inp["L3_attn_b_lambda_q1"]), A(inp["L3_attn_b_lambda_k1"]),
         A(inp["L3_attn_b_lambda_q2"]), A(inp["L3_attn_b_lambda_k2"])], axis=1)
    w["l3b_rms"] = np.ascontiguousarray(
        np.asarray(inp["L3_attn_b_rms_scale"], np.float32).reshape(2, 128).T)
    w["l1f_gate"] = A(np.asarray(inp["L1_ffn_gate_w"], np.float32).T)
    w["l1f_w1"] = A(np.asarray(inp["L1_ffn_w1"], np.float32).transpose(0, 2, 1))
    w["l1f_b1"] = A(np.asarray(inp["L1_ffn_b1"], np.float32).reshape(8, 16, 128)
                    .transpose(0, 2, 1))
    w["l1f_w2"] = A(np.asarray(inp["L1_ffn_w2"], np.float32).transpose(0, 2, 1))
    w["l1f_b2"] = A(np.asarray(inp["L1_ffn_b2"], np.float32).reshape(8, 8, 128)
                    .transpose(0, 2, 1))
    w["l1f_ws1"] = A(np.asarray(inp["L1_ffn_ws1"], np.float32).T)
    w["l1f_bs1"] = col128x(inp["L1_ffn_bs1"], 16)
    w["l1f_ws2"] = A(np.asarray(inp["L1_ffn_ws2"], np.float32).T)
    w["l1f_bs2"] = col128(inp["L1_ffn_bs2"])
    wqkv = np.asarray(inp["L0_attn_b_W_qkv"], np.float32).reshape(D, 6, 16, 64)
    w["l0b_wqkv"] = A(wqkv.transpose(2, 0, 1, 3).reshape(16, D, 384))
    w["l0b_wout"] = A(inp["L0_attn_b_W_out"])
    w["l3f_keys"] = A(np.asarray(inp["L3_ffn_keys"], np.float32).T)
    w["l3f_rw"] = A(np.asarray(inp["L3_ffn_rw"], np.float32).T)
    w["l3f_rb"] = A(np.asarray(inp["L3_ffn_rb"], np.float32).reshape(1, 16))
    w["l3f_w1"] = A(np.asarray(inp["L3_ffn_w1"], np.float32).transpose(0, 2, 1))
    w["l3f_b1"] = A(np.asarray(inp["L3_ffn_b1"], np.float32).reshape(16, 32, 128)
                    .transpose(0, 2, 1))
    w["l3f_w2"] = A(np.asarray(inp["L3_ffn_w2"], np.float32).transpose(0, 2, 1))
    w["l3f_b2"] = A(np.asarray(inp["L3_ffn_b2"], np.float32).reshape(16, 8, 128)
                    .transpose(0, 2, 1))
    w["l3a_cw"] = A(np.asarray(inp["L3_attn_a_conv_w"], np.float32).reshape(8, 128, 4)
                    .transpose(1, 0, 2).reshape(128, 32))
    w["l3a_cb"] = col128(inp["L3_attn_a_conv_b"])
    w["l3a_dtw"] = col128(np.asarray(inp["L3_attn_a_dt_w"], np.float32).reshape(1024))
    w["l3a_dtb"] = col128(inp["L3_attn_a_dt_b"])
    w["l3a_dp"] = col128(inp["L3_attn_a_Dp"])
    w["l3a_al"] = A(np.asarray(inp["L3_attn_a_a_log"], np.float32).reshape(8, 128, 16)
                    .transpose(1, 0, 2).reshape(128, 128))
    w["l3a_xp"] = A(np.asarray(inp["L3_attn_a_xproj_w"], np.float32).T)
    w["l1b_wq"] = A(inp["L1_attn_b_Wq"])
    w["l1b_wk"] = A(inp["L1_attn_b_Wk"])
    w["l1b_wv"] = A(inp["L1_attn_b_Wv"])
    w["l1b_wo"] = A(inp["L1_attn_b_Wo"])
    w["l1b_bq"] = A(np.asarray(inp["L1_attn_b_bq"], np.float32).reshape(1, D))
    w["l1b_bk"] = A(np.asarray(inp["L1_attn_b_bk"], np.float32).reshape(1, D))
    w["l1b_bvo"] = np.concatenate([col128(inp["L1_attn_b_bv"]),
                                   col128(inp["L1_attn_b_bo"])], axis=1)
    w["l1b_rel"] = A(np.asarray(inp["L1_attn_b_rel"], np.float32)[2040:2055]
                     .reshape(1, 15))
    return w


def shard_x(x, mode):
    if mode == "batch":
        return [np.ascontiguousarray(x[c].T) for c in range(NCORE)]
    return [np.ascontiguousarray(
        x[:, 256 * c:256 * (c + 1), :].reshape(8 * 256, 1024).T)
        for c in range(NCORE)]


def unshard_x(outs, mode):
    if mode == "batch":
        return np.ascontiguousarray(np.stack([o.T for o in outs]))
    x = np.empty((8, 2048, 1024), np.float32)
    for c in range(NCORE):
        x[:, 256 * c:256 * (c + 1), :] = outs[c].T.reshape(8, 256, 1024)
    return x


_PROG_CACHE = {}


def run_chain(x, subs, weights, shard="batch"):
    key = tuple(subs)
    if key not in _PROG_CACHE:
        _PROG_CACHE[key] = build_program(subs)
    nc = _PROG_CACHE[key]
    need = set(n for s in subs for (n, _) in SUB_WEIGHTS[s])
    shards = shard_x(x, shard)
    in_maps = []
    for c in range(NCORE):
        m = {"x_in": shards[c]}
        for n in need:
            m[n] = weights[n]
        in_maps.append(m)
    res = run_bass_kernel_spmd(nc, in_maps, core_ids=list(range(NCORE)))
    return unshard_x([np.asarray(r["x_out"]) for r in res.results], shard)


GROUPS = [
    (["l0aa", "l0ffn", "l0ab", "l0ffn", "l1aa", "l1ffn"], "batch"),
    (["l1ab"], "time"),
    (["l1ffn", "l2aa", "l2ffn", "l2ab", "l2ffn", "l3aa", "l3ffn", "l3ab",
      "l3ffn"], "batch"),
]


def kernel(**inputs):
    x = np.asarray(inputs["L0_attn_a_x"], np.float32)
    w = prep_weights(inputs)
    for subs, mode in GROUPS:
        x = run_chain(x, subs, w, shard=mode)
    return x
